# version 36
# baseline (speedup 1.0000x reference)
import numpy as np
import ml_dtypes
from contextlib import ExitStack

import concourse.bass as bass
import concourse.mybir as mybir
from concourse.bass_utils import run_bass_kernel_spmd
from concourse.alu_op_type import AluOpType as Op

F32 = mybir.dt.float32
F32R = mybir.dt.float32r
BF16 = mybir.dt.bfloat16
AF = mybir.ActivationFunctionType
AXX = mybir.AxisListType.X

D = 1024
H = 4
NCH = 8
CONV_K = 4
FIR_S = 5
FIR_L = 64
NTAP = FIR_S + FIR_L
NEXP = 8
ML = 512
FLOOR_EPS = 0.08
ALPHA = 1.702
SWISH_CAP = float(ALPHA * 7.0 / (1.0 + np.exp(-ALPHA * 7.0)))
F32EPS = float(np.finfo(np.float32).eps)
VL = FIR_L - 1 + ML
CL = CONV_K - 1 + ML

ENGINES = ("T", "V", "S", "G", "D")
ND_SEM = 6
SEM_NAMES = ["T", "V", "S", "G"] + [f"D{i}" for i in range(ND_SEM)] + ["GD"]


class _Rec:
    __slots__ = ("idx", "eng", "seq", "fn", "deps", "clock", "inc", "is_dma",
                 "sem", "wv", "self_wait", "tag")


class Sched:

    def __init__(self):
        self.ops = []
        self.by_eng = {e: [] for e in ENGINES}
        self.eng_seq = {e: 0 for e in ENGINES}
        self.eng_clock = {e: {n: 0 for n in SEM_NAMES} for e in ENGINES}
        self.last_w = {}
        self.readers = {}
        self.final_deps = []
        self.d_rank = 0
        self.sem_counts = {n: 0 for n in SEM_NAMES}

    def op(self, eng, fn, reads=(), writes=(), tag="", is_dma=False):
        idx = len(self.ops)
        seq = self.eng_seq[eng] + 1
        self.eng_seq[eng] = seq
        deps = set()
        nopr = set()
        for r in reads:
            w = self.last_w.get(r)
            if w is not None:
                deps.add(w)
                if r[0] in ("pb", "pbB"):
                    nopr.add(w)
            self.readers.setdefault(r, []).append(idx)
        for r in writes:
            w = self.last_w.get(r)
            if w is not None:
                deps.add(w)
                if r[0] in ("pb", "pbB"):
                    nopr.add(w)
            for rd in self.readers.get(r, ()):
                if rd != idx:
                    deps.add(rd)
                    if r[0] in ("pb", "pbB"):
                        nopr.add(rd)
            self.readers[r] = []
            self.last_w[r] = idx
        ec = self.eng_clock[eng]
        real = []
        for d in sorted(deps):
            od = self.ops[d]
            if od.eng == "T" and eng == "T":
                for k, v in od.clock.items():
                    if v > ec[k]:
                        ec[k] = v
                continue
            kv = od.wv if od.is_dma else od.seq
            if d not in nopr and ec[od.sem] >= kv:
                continue
            real.append(d)
        for d in real:
            for k, v in self.ops[d].clock.items():
                if v > ec[k]:
                    ec[k] = v
        rec = _Rec()
        rec.idx, rec.eng, rec.seq, rec.fn = idx, eng, seq, fn
        rec.is_dma = is_dma or eng == "D"
        rec.inc = rec.is_dma
        rec.self_wait = None
        if rec.is_dma:
            if eng == "D":
                self.d_rank += 1
                rec.sem = f"D{(self.d_rank - 1) % ND_SEM}"
            else:
                rec.sem = "GD"
            n = self.sem_counts[rec.sem] + 1
            self.sem_counts[rec.sem] = n
            rec.wv = 16 * n
            if n > 1:
                rec.self_wait = (rec.sem, 16 * (n - 1))
                if ec[rec.sem] < 16 * (n - 1):
                    ec[rec.sem] = 16 * (n - 1)
        else:
            rec.sem = eng
            rec.wv = 0
        if eng == "T":
            ec["T"] = seq
        clock = dict(ec)
        kown = rec.wv if rec.is_dma else seq
        if clock[rec.sem] < kown:
            clock[rec.sem] = kown
        rec.deps, rec.clock = real, clock
        rec.tag = tag
        self.ops.append(rec)
        self.by_eng[eng].append(rec)
        return idx

    def dma(self, fn, reads=(), writes=(), tag="", queue="D"):
        return self.op(queue, fn, reads, writes, tag, is_dma=True)

    def barrier_on(self, idxs):
        self.final_deps.extend(idxs)

    def finalize(self, sems):
        for rec in self.ops:
            for d in rec.deps:
                self.ops[d].inc = True
        for d in self.final_deps:
            self.ops[d].inc = True
        cnt = {e: 0 for e in ENGINES}
        for rec in self.ops:
            if rec.inc and not rec.is_dma:
                cnt[rec.eng] += 1
                rec.wv = cnt[rec.eng]
        self.sems = sems

    def emit(self, eng, proxy, final_extra=None):
        for rec in self.by_eng[eng]:
            waits = {}
            for d in rec.deps:
                dr = self.ops[d]
                if waits.get(dr.sem, -1) < dr.wv:
                    waits[dr.sem] = dr.wv
            if rec.self_wait is not None:
                snm, sv = rec.self_wait
                if waits.get(snm, -1) < sv:
                    waits[snm] = sv
            for key in sorted(waits):
                proxy.wait_ge(self.sems[key], waits[key])
            inst = rec.fn(proxy)
            if rec.inc:
                inst.then_inc(self.sems[rec.sem], 16 if rec.is_dma else 1)
        if final_extra is not None:
            final_extra(proxy)


CONST_VALUES = [0.0, 1.0, -1.0, 1e-12, 1e-5, F32EPS, 1e-6, 1.0 / 256.0,
                1.0 / 1024.0, 1.0 - 2.0 * FLOOR_EPS, SWISH_CAP, ALPHA,
                1.0 / ALPHA, -6.0, 8.0, -1e30, 7.0, 0.5, 2.0, 1.0 + 1e-30]



def _stream_w(w):
    d_in, d_out = w.shape
    nk, nd = d_in // 128, d_out // 128
    v = w.reshape(nk, 128, nd, 128).transpose(2, 1, 0, 3)
    return np.ascontiguousarray(v.reshape(nd, 128, nk * 128))


def _diag_tiles(w_taps):
    T = w_taps.shape[1]
    out = np.zeros((NCH, 128, T * 128), np.float32)
    ii = np.arange(128)
    for c in range(NCH):
        for t in range(T):
            out[c, ii, t * 128 + ii] = w_taps[c * 128:(c + 1) * 128, t]
    return out


def prepare_host(inp):
    g = {}
    g["wq_s"] = _stream_w(inp["L0_attn_Wq"])
    g["wk_s"] = _stream_w(inp["L0_attn_Wk"])
    g["wv_s"] = _stream_w(inp["L0_attn_Wv"])
    g["wb_s"] = np.ascontiguousarray(
        inp["L0_attn_Wb"].reshape(NCH, 128, H).transpose(1, 0, 2).reshape(128, NCH * H))
    w1 = inp["L0_attn_mlp_w1"]
    g["w1a_s"] = _stream_w(w1[0:1024, :])
    g["w1b"] = np.ascontiguousarray(w1[1024:1040, :])
    g["w2g"] = np.ascontiguousarray(
        inp["L0_attn_mlp_w2"].reshape(NCH, 128, H).transpose(1, 0, 2).reshape(128, NCH * H))
    g["b1g"] = np.ascontiguousarray(inp["L0_attn_mlp_b1"].reshape(NCH, 128).T)
    g["b2g"] = np.ascontiguousarray(inp["L0_attn_mlp_b2"].reshape(H, 1))
    g["logit_temp"] = np.ascontiguousarray(inp["L0_attn_logit_temp"].reshape(1, 1))
    g["crl"] = np.ascontiguousarray(inp["L0_attn_conv_res_logit"].reshape(1, H))
    g["rms_row"] = np.ascontiguousarray(inp["L0_attn_rms_w"].reshape(1, 256))
    g["wo_s"] = _stream_w(inp["L0_attn_Wo"])
    cw = np.concatenate([inp["L0_attn_conv_w_q"], inp["L0_attn_conv_w_k"],
                         inp["L0_attn_conv_w_v"]], axis=1)
    g["conv_diag"] = _diag_tiles(cw)
    fir = np.concatenate([inp["L0_attn_fir_short_w"].reshape(1024, FIR_S),
                          inp["L0_attn_fir_long_w"].reshape(1024, FIR_L)], 1)
    g["fir_diag"] = _diag_tiles(fir)
    g["norm_w"] = np.ascontiguousarray(inp["L0_ffn_norm_w"].reshape(NCH, 128).T)
    g["gate_w_t"] = np.ascontiguousarray(
        inp["L0_ffn_gate_w"].T.reshape(NCH, 128, NEXP).transpose(1, 0, 2)
        .reshape(128, NCH * NEXP))
    g["gate_b"] = np.ascontiguousarray(inp["L0_ffn_gate_b"].reshape(NEXP, 1))
    m1 = inp["L0_ffn_mlp1_w"]
    w1s = np.empty((NEXP, 16, 128, 1024), np.float32)
    for e in range(NEXP):
        w1s[e, 0:8] = _stream_w(np.ascontiguousarray(m1[e, 0::2, :].T))
        w1s[e, 8:16] = _stream_w(np.ascontiguousarray(m1[e, 1::2, :].T))
    g["ffn_w1_s"] = w1s
    m2 = inp["L0_ffn_mlp2_w"]
    w2s = np.empty((NEXP, 8, 128, 1024), np.float32)
    for e in range(NEXP):
        w2s[e] = _stream_w(np.ascontiguousarray(m2[e].T))
    g["ffn_w2_s"] = w2s
    b1 = inp["L0_ffn_mlp1_b"]
    bb = np.concatenate([b1[:, 0::2].reshape(NEXP, 8, 128),
                         b1[:, 1::2].reshape(NEXP, 8, 128)], axis=1)
    g["ffn_b1"] = np.ascontiguousarray(bb.transpose(2, 0, 1).reshape(128, NEXP * 16))
    g["ffn_b2"] = np.ascontiguousarray(inp["L0_ffn_mlp2_b"])
    g["ident"] = np.eye(128, dtype=np.float32)
    g["ident_bf"] = np.eye(128, dtype=ml_dtypes.bfloat16)
    g["ones_row"] = np.ones((1, 128), np.float32)
    g["ones_col"] = np.ones((128, 1), np.float32)
    g["m_stril"] = np.tril(np.ones((128, 128), ml_dtypes.bfloat16), -1)
    g["m_striu"] = np.triu(np.ones((128, 128), np.float32), 0)
    g["floor_row"] = np.tile(np.array([FLOOR_EPS, FLOOR_EPS, 0, 0], np.float32),
                             (128, 1))
    g["consts"] = np.tile(np.array(CONST_VALUES, np.float32), (128, 1))
    sel = np.zeros((8, 8 * 128), ml_dtypes.bfloat16)
    for e in range(8):
        sel[e, e * 128:(e + 1) * 128] = 1.0
    g["sel8"] = sel
    return g


class MK:
    def __init__(self, nseq, seqlen, taps=()):
        assert seqlen % ML == 0
        self.nseq, self.seqlen = nseq, seqlen
        self.nmac = seqlen // ML
        self.taps = list(taps)
        self.tap_specs = []
        self.nc = bass.Bass("TRN2", target_bir_lowering=False)
        self.s = Sched()
        self.es = ExitStack()
        self._nb = 0
        self._pool = [0, 1, 2, 3, 4, 5]
        self._held = set()

    def sb(self, name, shape, dtype):
        return self.es.enter_context(self.nc.sbuf_tensor(name, list(shape), dtype))

    def nbank(self):
        for _ in range(len(self._pool)):
            self._nb = (self._nb + 1) % len(self._pool)
            b = self._pool[self._nb]
            if b not in self._held:
                self._held.add(b)
                return b
        raise RuntimeError("no free psum bank")

    def free_bank(self, b):
        self._held.discard(b)

    def set_pool(self, pool):
        self._pool = list(pool)
        self._nb = 0

    def ld(self, dst, src, writes, tag, reads=()):
        return self.s.op("D", lambda p, d=dst, s_=src: p.dma_start(out=d, in_=s_),
                         reads=list(reads), writes=list(writes), tag=tag)

    def st_out(self, dst, src, reads, tag):
        i = self.s.op("D", lambda p, d=dst, s_=src: p.dma_start(out=d, in_=s_),
                      reads=list(reads), writes=[], tag=tag)
        self.s.barrier_on([i])

    def V(self, fn, reads, writes, tag):
        return self.s.op("V", fn, reads, writes, tag)

    def S_(self, fn, reads, writes, tag):
        return self.s.op("S", fn, reads, writes, tag)

    def MMb(self, bank, cols, lhsT, rhs, first, last, reads, tag):
        o = self.pb[bank][:, cols]
        return self.s.op("T", lambda p, o=o, a=lhsT, b=rhs, f=first, l=last:
                         p.matmul(o, a, b, start=f, stop=l),
                         reads=list(reads), writes=[("pb", bank)], tag=tag)

    def MMr(self, bank, cols, lhsT, rhs, first, last, reads, tag, mrows=None):
        o = self.pb[bank][0:mrows, cols] if mrows else self.pb[bank][:, cols]
        a, b = lhsT.bitcast(F32R), rhs.bitcast(F32R)
        return self.s.op("T", lambda p, o=o, a=a, b=b, f=first, l=last:
                         p.matmul(o, a, b, start=f, stop=l),
                         reads=list(reads), writes=[("pb", bank)], tag=tag)

    def TRf(self, bank, cols, in_ap, reads, tag, rows=None, pk=128):
        o = self.pb[bank][0:rows, cols] if rows else self.pb[bank][:, cols]
        ident = self.ident[0:pk, 0:pk]
        return self.s.op("T", lambda p, o=o, a=in_ap, i=ident:
                         p.transpose(o, a, i),
                         reads=list(reads) + [("c_ident",)],
                         writes=[("pb", bank)], tag=tag)

    def TRb(self, slot, cols, in_ap, reads, tag, rows=None):
        o = self.pbB[slot][0:rows, cols] if rows else self.pbB[slot][:, cols]
        return self.s.op("T", lambda p, o=o, a=in_ap:
                         p.transpose(o, a, self.ident_bf[:, :]),
                         reads=list(reads) + [("c_identb",)],
                         writes=[("pbB", slot)], tag=tag)

    def bc_row(self, bank, cols, row_ap, reads, tag):
        o = self.pb[bank][:, cols]
        a = self.m_striu[0:1, :]
        return self.s.op("T", lambda p, o=o, a=a, b=row_ap:
                         p.matmul(o, a, b, start=True, stop=True),
                         reads=list(reads) + [("c_mtu",)],
                         writes=[("pb", bank)], tag=tag)

    def tap(self, name, ap, shape, reads, dtype=F32):
        if name not in self.taps:
            return
        d = self.nc.dram_tensor("tap_" + name, list(shape), dtype,
                                kind="ExternalOutput").ap()
        self.tap_specs.append(name)
        i = self.s.dma(lambda p, d=d, a=ap: p.dma_start(out=d, in_=a),
                       reads=list(reads), writes=[], tag="tap_" + name)
        self.s.barrier_on([i])

    def build(self):
        nc, s = self.nc, self.s
        NS, SL = self.nseq, self.seqlen
        self.x_in = nc.dram_tensor("x", [NS, NCH, 128, SL], F32, kind="ExternalInput").ap()
        self.y_out = nc.dram_tensor("y", [NS, SL, D], F32, kind="ExternalOutput").ap()
        din = {}
        for nm, shp, dt in [
            ("wq_s", [8, 128, 1024], F32), ("wk_s", [8, 128, 1024], F32),
            ("wv_s", [8, 128, 1024], F32), ("wb_s", [128, 32], F32),
            ("w1a_s", [8, 128, 1024], F32), ("w1b", [16, 1024], F32),
            ("w2g", [128, 32], F32), ("b1g", [128, 8], F32), ("b2g", [4, 1], F32),
            ("logit_temp", [1, 1], F32), ("crl", [1, 4], F32),
            ("rms_row", [1, 256], F32), ("wo_s", [8, 128, 1024], F32),
            ("conv_diag", [8, 128, 12 * 128], F32),
            ("fir_diag", [8, 128, NTAP * 128], F32),
            ("norm_w", [128, 8], F32), ("gate_w_t", [128, 64], F32),
            ("gate_b", [8, 1], F32),
            ("ffn_w1_s", [NEXP, 16, 128, 1024], F32),
            ("ffn_w2_s", [NEXP, 8, 128, 1024], F32),
            ("ffn_b1", [128, NEXP * 16], F32), ("ffn_b2", [8, 1024], F32),
            ("ident", [128, 128], F32), ("ident_bf", [128, 128], BF16),
            ("ones_row", [1, 128], F32), ("ones_col", [128, 1], F32),
            ("m_stril", [128, 128], BF16), ("m_striu", [128, 128], F32),
            ("floor_row", [128, 4], F32), ("consts", [128, len(CONST_VALUES)], F32),
            ("sel8", [8, 8 * 128], BF16),
        ]:
            din[nm] = nc.dram_tensor(nm + "_d", shp, dt, kind="ExternalInput").ap()
        self.din = din
        scr = {}
        for nm, shp in [("wq_b", [8, 128, 1024]), ("wk_b", [8, 128, 1024]),
                        ("wv_b", [8, 128, 1024]), ("w1a_b", [8, 128, 1024]),
                        ("wo_b", [8, 128, 1024]), ("conv_b", [8, 128, 12 * 128]),
                        ("fir_b", [8, 128, NTAP * 128]),
                        ("fw1_b", [NEXP, 16, 128, 1024]),
                        ("fw2_b", [NEXP, 8, 128, 1024]),
                        ("w1b_c", [16, 1024])]:
            scr[nm] = nc.dram_tensor(nm, shp, BF16, kind="Internal").ap()
        self.scr = scr

        sb = self.sb
        self.consts = sb("consts", [128, len(CONST_VALUES)], F32)
        self.wb = sb("wb", [128, 32], BF16)
        self.w1b = sb("w1b", [16, 1024], BF16)
        self.w2g = sb("w2g", [128, 32], F32)
        self.b1g = sb("b1g", [128, 8], F32)
        self.b2g = sb("b2g", [4, 1], F32)
        self.lt = sb("lt", [1, 8], F32)
        self.crl = sb("crl", [1, 4], F32)
        self.rmsrow = sb("rmsrow", [1, 256], F32)
        self.normw = sb("normw", [128, 8], F32)
        self.gatewt = sb("gatewt", [128, 64], F32)
        self.gateb = sb("gateb", [8, 1], F32)
        self.fb1 = sb("fb1", [128, NEXP * 16], F32)
        self.fb1s = sb("fb1s", [128, NEXP * 8], F32)
        self.fb1p = sb("fb1p", [128, NEXP * 8], F32)
        self.fb2_bf = sb("fb2_bf", [8, 1024], BF16)
        self.ident = sb("ident", [128, 128], F32)
        self.ident_r = sb("ident_r", [128, 128], F32)
        self.onesc_r = sb("onesc_r", [128, 1], F32)
        self.w2g_bf = sb("w2g_bf", [128, 32], BF16)
        self.gatewt_r = sb("gatewt_r", [128, 64], F32)
        self.lnsc = sb("lnsc", [128, 256], F32)
        self.ident_bf = sb("ident_bf", [128, 128], BF16)
        self.ones_col = sb("ones_col", [128, 1], F32)
        self.m_stril = sb("m_stril", [128, 128], BF16)
        self.m_striu = sb("m_striu", [128, 128], F32)
        self.floor_row = sb("floor_row", [128, 4], F32)
        self.sel8 = sb("sel8", [8, 8 * 128], BF16)
        self.rms_bc = sb("rms_bc", [128, 256], F32)
        self.sigc = sb("sigc", [128, 4], F32)
        self.invt = sb("invt", [128, 1], F32)

        self.x_tm = sb("x_tm", [128, 512], F32)
        self.xT_pair = sb("xT_pair", [128, NCH * 256], F32)
        self.xT_bf = sb("xT_bf", [128, NCH * ML], BF16)
        self.pre = sb("pre", [128, 2 * CL], BF16)
        self.tails = [sb(f"tail{i}", [128, 3 * NCH], BF16) for i in range(3)]
        self.qT = sb("qT", [128, NCH * ML], BF16)
        self.kT = sb("kT", [128, NCH * ML], BF16)
        self.vT = sb("vT", [128, NCH * VL], BF16)
        self.lsT = sb("lsT", [128, NCH * ML], BF16)
        self.llT = sb("llT", [128, NCH * ML], BF16)
        self.beTa = sb("beTa", [4, ML], F32)
        self.beta_tm = sb("beta_tm", [128, 16], F32)
        self.sq_tm = sb("sq_tm", [128, 16], F32)
        self.sqrow = sb("sqrow", [1, ML], F32)
        self.sqsc = sb("sqsc", [128, ML], F32)
        self.branches = [sb(f"branches{j}", [128, H * 1024], BF16)
                         for j in range(2)]
        self.ktm = sb("ktm", [128, 2 * 256], F32)
        self.S_sb = sb("S_sb", [128, H * 512], F32)
        self.S_bf = sb("S_bf", [128, 2 * 512], BF16)
        self.chain = [[sb(f"chain{sl}_{i}", [128, 384], BF16) for i in range(2)]
                      for sl in range(2)]
        self.attnT = sb("attnT", [128, 2 * 128], F32)
        self.vbk = sb("vbk", [128, 2 * 256], F32)
        self.stat_sc = sb("stat_sc", [128, 2 * 8], F32)
        self.colsc = sb("colsc", [128, 2 * 8], F32)
        self.stats_tm = sb("stats_tm", [128, 2 * 16], F32)
        self.qcsc = sb("qcsc", [128, 8], F32)
        self.statsT = sb("statsT", [16, 2 * H * 128], BF16)
        self.hg = sb("hg", [128, 2 * 512], BF16)
        self.lgrb = sb("lgrb", [4, 256], F32)
        self.fglb = sb("fglb", [8, 256], F32)
        self.probs = sb("probs", [128, H * 4], F32)
        self.o_n = sb("o_n", [128, 1024], F32)
        self.onT = sb("onT", [128, NCH * 256], BF16)
        self.hrot = sb("hrot", [128, 2 * 256], F32)
        self.h_bf = sb("h_bf", [128, NCH * ML], BF16)
        self.acc = sb("acc", [128, NCH * ML], F32)
        self.p8 = sb("p8", [128, 4 * NEXP], F32)
        self.p8T_bf = sb("p8T_bf", [8, ML], BF16)
        self.p_be = sb("p_be", [128, ML], F32)
        self.act_bf = sb("act_bf", [128, NCH * ML], BF16)
        self.gl_sb = sb("gl_sb", [128, 4 * 8], F32)
        self.row1 = sb("row1", [1, 5 * 256], F32)
        self.scx = sb("scx", [128, ML], F32)
        self.scy = sb("scy", [128, ML], F32)
        self.scz = sb("scz", [128, 128], F32)

        self.r_w = sb("r_w", [128, 2 * 1024], BF16)
        self.r_w1a = sb("r_w1a", [128, 2 * 1024], BF16)
        self.r_wo = sb("r_wo", [128, 2 * 1024], BF16)
        self.r_conv = sb("r_conv", [128, 12 * 128], BF16)
        self.r_fir = sb("r_fir", [128, 2 * 12 * 128], BF16)
        self.r_f1 = sb("r_f1", [128, 3 * 1024], BF16)
        self.r_fw2 = sb("r_fw2", [128, 2 * 1024], BF16)

        self.pb = [self.es.enter_context(nc.psum_tensor(f"pb{i}", [128, 512], F32))
                   for i in range(6)]
        self.pbB = [self.es.enter_context(nc.psum_tensor(f"pbB{i}", [128, 1024], BF16))
                    for i in range(2)]

        for i, v in enumerate(CONST_VALUES):
            nc.const_aps.register(F32, v, self.consts[:, i:i + 1])

        self._startup()
        for sq in range(NS):
            self._seq(sq)

        sems = {e: self.es.enter_context(nc.semaphore(f"sem_{e}"))
                for e in SEM_NAMES}
        s.finalize(sems)
        maxv = {}
        for d in set(s.final_deps):
            dr = s.ops[d]
            if maxv.get(dr.sem, -1) < dr.wv:
                maxv[dr.sem] = dr.wv

        def final_waits(proxy):
            for e in sorted(maxv):
                proxy.wait_ge(sems[e], maxv[e])

        with self.es, nc.Block() as block:
            @block.sync
            def _(p):
                s.emit("D", p, final_extra=final_waits)

            @block.tensor
            def _(p):
                s.emit("T", p)

            @block.vector
            def _(p):
                s.emit("V", p)

            @block.scalar
            def _(p):
                s.emit("S", p)

            @block.gpsimd
            def _(p):
                s.emit("G", p)
        return nc

    def _startup(self):
        s, din, scr = self.s, self.din, self.scr
        V = self.V
        for src, dst, rg in [("wq_s", "wq_b", "sc_wq"), ("wk_s", "wk_b", "sc_wk"),
                             ("wv_s", "wv_b", "sc_wv"), ("w1a_s", "w1a_b", "sc_w1a"),
                             ("wo_s", "wo_b", "sc_wo"), ("conv_diag", "conv_b", "sc_cv"),
                             ("fir_diag", "fir_b", "sc_fir"),
                             ("ffn_w1_s", "fw1_b", "sc_f1"),
                             ("ffn_w2_s", "fw2_b", "sc_f2"),
                             ("w1b", "w1b_c", "sc_w1b")]:
            s.op("G", lambda p, a=din[src], b=scr[dst]: p.dma_start(out=b, in_=a),
                 reads=[], writes=[(rg,)], tag=f"cast_{dst}", is_dma=True)

        L = self.ld
        L(self.consts[:, :], din["consts"], [("c_c",)], "c_c")
        s.op("G", lambda p: p.dma_start(out=self.wb[:, :], in_=din["wb_s"]),
             reads=[], writes=[("c_wb",)], tag="c_wb", is_dma=True)
        L(self.w1b[:, :], scr["w1b_c"], [("c_w1b",)], "c_w1b",
          reads=[("sc_w1b",)])
        L(self.w2g[:, :], din["w2g"], [("c_w2g",)], "c_w2g")
        L(self.b1g[:, :], din["b1g"], [("c_b1g",)], "c_b1g")
        L(self.b2g[:, :], din["b2g"], [("c_b2g",)], "c_b2g")
        L(self.lt[0:1, 0:1], din["logit_temp"], [("lt",)], "c_lt")
        L(self.crl[:, :], din["crl"], [("c_crl",)], "c_crl")
        L(self.rmsrow[:, :], din["rms_row"], [("c_rms",)], "c_rms")
        L(self.normw[:, :], din["norm_w"], [("c_nw",)], "c_nw")
        L(self.gatewt[:, :], din["gate_w_t"], [("c_gw",)], "c_gw")
        L(self.gateb[:, :], din["gate_b"], [("c_gb",)], "c_gb")
        L(self.fb1[:, :], din["ffn_b1"], [("c_fb1",)], "c_fb1")
        s.op("G", lambda p: p.dma_start(out=self.fb2_bf[:, :], in_=din["ffn_b2"]),
             reads=[], writes=[("c_fb2b",)], tag="c_fb2b", is_dma=True)
        L(self.ident[:, :], din["ident"], [("c_ident",)], "c_id")
        L(self.ident_bf[:, :], din["ident_bf"], [("c_identb",)], "c_idb")
        L(self.ones_col[:, :], din["ones_col"], [("c_onesc",)], "c_oc")
        L(self.m_stril[:, :], din["m_stril"], [("c_mtl",)], "c_mtl")
        L(self.m_striu[:, :], din["m_striu"], [("c_mtu",)], "c_mtu")
        L(self.floor_row[:, :], din["floor_row"], [("c_floor",)], "c_fl")
        L(self.sel8[:, :], din["sel8"], [("c_sel8",)], "c_s8")
        S_ = self.S_
        S_(lambda p: p.copy(self.ident_r[:, :].bitcast(F32R), self.ident[:, :]),
           [("c_ident",)], [("c_identr",)], "identr")
        S_(lambda p: p.copy(self.onesc_r[:, :].bitcast(F32R), self.ones_col[:, :]),
           [("c_onesc",)], [("c_onescr",)], "onescr")
        S_(lambda p: p.copy(self.w2g_bf[:, :], self.w2g[:, :]),
           [("c_w2g",)], [("c_w2gb",)], "w2gb")
        S_(lambda p: p.copy(self.gatewt_r[:, :].bitcast(F32R), self.gatewt[:, :]),
           [("c_gw",)], [("c_gwr",)], "gwr")

        for e in range(NEXP):
            V(lambda p, e=e: p.tensor_scalar(
                self.fb1s[:, e * 8:(e + 1) * 8], self.fb1[:, e * 16:e * 16 + 8],
                ALPHA, None, Op.mult),
              [("c_fb1",), ("c_c",)], [("c_fb1s",)], f"b1s{e}")
            V(lambda p, e=e: p.tensor_scalar(
                self.fb1p[:, e * 8:(e + 1) * 8], self.fb1[:, e * 16 + 8:e * 16 + 16],
                1.0, None, Op.add),
              [("c_fb1",), ("c_c",)], [("c_fb1p",)], f"b1p{e}")
        b_ = self.nbank()
        self.bc_row(b_, slice(0, 256), self.rmsrow[0:1, :], [("c_rms",)], "rmsbc")
        V(lambda p, b=b_: p.tensor_copy(self.rms_bc[:, :], self.pb[b][:, 0:256]),
          [("pb", b_)], [("c_rmsbc",)], "rmsbc_e")
        self.free_bank(b_)
        S_(lambda p: p.activation(self.crl[0:1, :].bitcast(F32R), self.crl[0:1, :], AF.Sigmoid),
           [("c_crl",)], [("c_crl",)], "crlsig")
        b_ = self.nbank()
        self.bc_row(b_, slice(0, 4), self.crl[0:1, :], [("c_crl",)], "crlbc")
        V(lambda p, b=b_: p.tensor_copy(self.sigc[:, :], self.pb[b][:, 0:4]),
          [("pb", b_)], [("c_sigc",)], "sigc_e")
        self.free_bank(b_)
        lt = self.lt
        S_(lambda p: p.activation(lt[0:1, 1:2], lt[0:1, 0:1], AF.Abs),
           [("lt",)], [("lt",)], "lt1")
        S_(lambda p: p.activation(lt[0:1, 2:3], lt[0:1, 1:2], AF.Exp, scale=-1.0),
           [("lt",), ("c_c",)], [("lt",)], "lt2")
        S_(lambda p: p.activation(lt[0:1, 3:4], lt[0:1, 2:3], AF.Ln, bias=1.0),
           [("lt",), ("c_c",)], [("lt",)], "lt3")
        V(lambda p: p.tensor_scalar(lt[0:1, 4:5], lt[0:1, 0:1], 0.0, None, Op.max),
          [("lt",), ("c_c",)], [("lt",)], "lt4")
        V(lambda p: p.tensor_tensor(lt[0:1, 5:6], lt[0:1, 3:4], lt[0:1, 4:5], Op.add),
          [("lt",)], [("lt",)], "lt5")
        V(lambda p: p.tensor_scalar(lt[0:1, 6:7], lt[0:1, 5:6], 1e-6, None, Op.add),
          [("lt",), ("c_c",)], [("lt",)], "lt6")
        V(lambda p: p.reciprocal(lt[0:1, 7:8], lt[0:1, 6:7]),
          [("lt",)], [("lt",)], "lt7")
        S_(lambda p: p.copy(lt[0:1, 7:8].bitcast(F32R), lt[0:1, 7:8]),
           [("lt",)], [("lt",)], "lt8")
        b_ = self.nbank()
        self.bc_row(b_, slice(0, 1), lt[0:1, 7:8], [("lt",)], "invtbc")
        V(lambda p, b=b_: p.tensor_copy(self.invt[:, :], self.pb[b][:, 0:1]),
          [("pb", b_)], [("c_invt",)], "invt_e")
        self.free_bank(b_)

    def _seq(self, sq):
        V = self.V
        rS_all = [("S_sb", h) for h in range(H)]
        V(lambda p: p.memset(self.S_sb[:, :], 0.0), [], rS_all, f"q{sq}S0")
        self.S_(lambda p: p.copy(self.S_sb[:, :].bitcast(F32R), self.S_sb[:, :]),
                rS_all, rS_all, f"q{sq}S0r")
        for i in range(3):
            V(lambda p, i=i: p.memset(self.tails[i][:, :], 0.0),
              [], [("tail", i)], f"q{sq}t{i}")
        V(lambda p: p.memset(self.vT[:, :], 0.0), [], [("vT",)], f"q{sq}v0")
        for m in range(self.nmac):
            self._ph1(sq, m)
            self._ph2(sq, m)
            self._ph4a(sq, m)
            for pair in range(2):
                self._ph3_pair(sq, m, pair)
            self._ph7(sq, m)
            self._ph8(sq, m)

    def _ph1(self, sq, m):
        V, S_ = self.V, self.S_
        t0 = m * ML
        for dc in range(NCH):
            xa = self.x_tm[:, :]
            self.ld(xa, self.x_in[sq, dc, :, t0:t0 + ML],
                    [("x_tm",)], f"m{m}x{dc}")
            S_(lambda p, dc=dc, xa=xa: p.copy(
                self.xT_bf[:, dc * ML:(dc + 1) * ML], xa),
               [("x_tm",)], [("xTb", dc)], f"m{m}xTb{dc}")
        for dc in range(NCH):
            cpiece = self.r_conv[:, :]
            self.ld(cpiece, self.scr["conv_b"][dc], [("rconv",)],
                    f"m{m}cd{dc}", reads=[("sc_cv",)])
            for pi in range(3):
                par = (dc * 3 + pi) % 2
                scrn = ["wq_b", "wk_b", "wv_b"][pi]
                piece = self.r_w[:, par * 1024:(par + 1) * 1024]
                self.ld(piece, self.scr[scrn][dc], [("rw", par)],
                        f"m{m}w{pi}l{dc}", reads=[("sc_" + ["wq", "wk", "wv"][pi],)])
                bank = self.nbank()
                for kc in range(NCH):
                    self.MMb(bank, slice(0, ML), piece[:, kc * 128:(kc + 1) * 128],
                             self.xT_bf[:, kc * ML:(kc + 1) * ML],
                             kc == 0, kc == NCH - 1,
                             [("rw", par), ("xTb", kc)], f"m{m}p{pi}_{dc}_{kc}")
                prebuf = self.pre[:, (dc % 2) * CL:((dc % 2) + 1) * CL]
                tail = self.tails[pi]
                V(lambda p, pb_=prebuf, b=bank: p.tensor_copy(
                    pb_[:, 3:], self.pb[b][:, 0:ML]),
                    [("pb", bank)], [("pre", dc % 2)], f"m{m}pe{pi}_{dc}")
                self.free_bank(bank)
                V(lambda p, pb_=prebuf, t=tail, dc=dc: p.tensor_copy(
                    pb_[:, 0:3], t[:, dc * 3:dc * 3 + 3]),
                    [("tail", pi)], [("pre", dc % 2)], f"m{m}ph{pi}_{dc}")
                V(lambda p, pb_=prebuf, t=tail, dc=dc: p.tensor_copy(
                    t[:, dc * 3:dc * 3 + 3], pb_[:, ML:ML + 3]),
                    [("pre", dc % 2)], [("tail", pi)], f"m{m}pt{pi}_{dc}")
                bank2 = self.nbank()
                for tap in range(CONV_K):
                    ti = pi * 4 + tap
                    self.MMb(bank2, slice(0, ML),
                             cpiece[:, ti * 128:(ti + 1) * 128],
                             prebuf[:, tap:tap + ML], tap == 0, tap == CONV_K - 1,
                             [("rconv",), ("pre", dc % 2)],
                             f"m{m}c{pi}_{dc}_{tap}")
                if pi == 2:
                    dst = self.vT[:, dc * VL + FIR_L - 1: dc * VL + FIR_L - 1 + ML]
                    S_(lambda p, d=dst, b=bank2: p.activation(
                        d, self.pb[b][:, 0:ML], AF.Silu),
                       [("pb", bank2)], [("vT", dc)], f"m{m}sv{dc}")
                else:
                    outT = self.qT if pi == 0 else self.kT
                    rg = "qT" if pi == 0 else "kT"
                    dst = outT[:, dc * ML:(dc + 1) * ML]
                    S_(lambda p, d=dst, b=bank2: p.activation(
                        d, self.pb[b][:, 0:ML], AF.Silu),
                       [("pb", bank2)], [(rg, dc)], f"m{m}s{pi}_{dc}")
                self.free_bank(bank2)
            fg = self._fir_gen(m, dc)
            if dc % 2 == 1:
                gens = [fg, self._norm_gen(m, dc // 2)]
                while gens:
                    g = gens.pop(0)
                    try:
                        next(g)
                        gens.append(g)
                    except StopIteration:
                        pass
            else:
                for _ in fg:
                    pass

        bb = self.nbank()
        for kc in range(NCH):
            self.s.op("T", lambda p, kc=kc, b=bb: p.matmul(
                self.pb[b][0:4, 0:ML],
                self.wb[:, kc * 4:(kc + 1) * 4],
                self.xT_bf[:, kc * ML:(kc + 1) * ML],
                start=(kc == 0), stop=(kc == NCH - 1)),
                reads=[("c_wb",), ("xTb", kc)], writes=[("pb", bb)],
                tag=f"m{m}be{kc}")
        S_(lambda p, b=bb: p.activation(self.beTa[0:4, :].bitcast(F32R), self.pb[b][0:4, 0:ML],
                                        AF.Sigmoid),
           [("pb", bb)], [("beTa",)], f"m{m}bsig")
        self.free_bank(bb)
        for st in range(4):
            bank = self.nbank()
            self.TRf(bank, slice(0, 4), self.beTa[0:4, st * 128:(st + 1) * 128],
                     [("beTa",)], f"m{m}beT{st}", pk=4)
            V(lambda p, st=st, b=bank: p.tensor_copy(
                self.beta_tm[:, st * 4:(st + 1) * 4], self.pb[b][:, 0:4]),
                [("pb", bank)], [("beta_tm",)], f"m{m}beTe{st}")
            self.free_bank(bank)

        if m == 0 and sq == 0:
            self.tap("qT", self.qT[:, :], [128, NCH * ML],
                     [("qT", c) for c in range(NCH)], BF16)
            self.tap("kT", self.kT[:, :], [128, NCH * ML],
                     [("kT", c) for c in range(NCH)], BF16)
            self.tap("vT", self.vT[:, :], [128, NCH * VL],
                     [("vT", c) for c in range(NCH)], BF16)
            self.tap("beTa", self.beTa[0:4, :], [4, ML], [("beTa",)], F32)

    def _norm_gen(self, m, h):
        V, S_ = self.V, self.S_
        c0, c1 = 2 * h, 2 * h + 1
        if True:
            bank = self.nbank()
            S_(lambda p, c=c0: p.activation(self.sqsc[:, :].bitcast(F32R),
                                            self.kT[:, c * ML:(c + 1) * ML], AF.Square),
               [("kT", c0)], [("sqsc", 0), ("sqsc", 1)], f"m{m}ka{h}")
            yield
            self.MMr(bank, slice(0, ML), self.onesc_r[:, 0:1], self.sqsc[:, :],
                     True, False, [("c_onescr",), ("sqsc", 0), ("sqsc", 1)], f"m{m}kma{h}", mrows=1)
            S_(lambda p, c=c1: p.activation(self.sqsc[:, :].bitcast(F32R),
                                            self.kT[:, c * ML:(c + 1) * ML], AF.Square),
               [("kT", c1)], [("sqsc", 0), ("sqsc", 1)], f"m{m}kb{h}")
            self.MMr(bank, slice(0, ML), self.onesc_r[:, 0:1], self.sqsc[:, :],
                     False, True, [("c_onescr",), ("sqsc", 0), ("sqsc", 1)], f"m{m}kmb{h}", mrows=1)
            yield
            S_(lambda p, b=bank: p.activation(self.sqrow[0:1, :], self.pb[b][0:1, 0:ML],
                                              AF.Sqrt, bias=1e-12),
               [("pb", bank), ("c_c",)], [("sqrow",)], f"m{m}ks{h}")
            self.free_bank(bank)
            yield
            V(lambda p: p.reciprocal(self.sqrow[0:1, :], self.sqrow[0:1, :]),
              [("sqrow",)], [("sqrow",)], f"m{m}kr{h}")
            S_(lambda p: p.copy(self.sqrow[0:1, :].bitcast(F32R),
                                self.sqrow[0:1, :]),
               [("sqrow",)], [("sqrow",)], f"m{m}kr2{h}")
            yield
            bank2 = self.nbank()
            self.bc_row(bank2, slice(0, ML), self.sqrow[0:1, :], [("sqrow",)],
                        f"m{m}kbc{h}")
            yield
            for c in (c0, c1):
                V(lambda p, c=c, b=bank2: p.tensor_tensor(
                    self.kT[:, c * ML:(c + 1) * ML], self.kT[:, c * ML:(c + 1) * ML],
                    self.pb[b][:, 0:ML], Op.mult),
                  [("pb", bank2), ("kT", c)], [("kT", c)], f"m{m}kn{h}_{c}")
            self.free_bank(bank2)
            yield
            bank3 = self.nbank()
            S_(lambda p, c=c0: p.activation(self.sqsc[:, :].bitcast(F32R),
                                            self.qT[:, c * ML:(c + 1) * ML], AF.Square),
               [("qT", c0)], [("sqsc", 0), ("sqsc", 1)], f"m{m}qa{h}")
            self.MMr(bank3, slice(0, ML), self.onesc_r[:, 0:1], self.sqsc[:, :],
                     True, False, [("c_onescr",), ("sqsc", 0), ("sqsc", 1)], f"m{m}qma{h}", mrows=1)
            S_(lambda p, c=c1: p.activation(self.sqsc[:, :].bitcast(F32R),
                                            self.qT[:, c * ML:(c + 1) * ML], AF.Square),
               [("qT", c1)], [("sqsc", 0), ("sqsc", 1)], f"m{m}qb{h}")
            self.MMr(bank3, slice(0, ML), self.onesc_r[:, 0:1], self.sqsc[:, :],
                     False, True, [("c_onescr",), ("sqsc", 0), ("sqsc", 1)], f"m{m}qmb{h}", mrows=1)
            yield
            V(lambda p, b=bank3: p.tensor_copy(self.sqrow[0:1, :].bitcast(F32R), self.pb[b][0:1, 0:ML]),
              [("pb", bank3)], [("sqrow",)], f"m{m}qe{h}")
            self.free_bank(bank3)
            yield
            bank4 = self.nbank()
            for st in range(4):
                self.TRf(bank4, slice(st, st + 1),
                         self.sqrow[0:1, st * 128:(st + 1) * 128],
                         [("sqrow",)], f"m{m}qt{h}_{st}", pk=1)
            yield
            V(lambda p, b=bank4: p.tensor_copy(self.qcsc[:, 0:4], self.pb[b][:, 0:4]),
              [("pb", bank4)], [("qcsc",)], f"m{m}qc{h}")
            self.free_bank(bank4)
            S_(lambda p: p.activation(self.qcsc[:, 4:8], self.qcsc[:, 0:4],
                                      AF.Sqrt, bias=1e-12),
               [("qcsc",), ("c_c",)], [("qcsc",)], f"m{m}qs{h}")
            V(lambda p, h=h: p.reciprocal(self.sq_tm[:, 4 * h:4 * h + 4],
                                          self.qcsc[:, 4:8]),
              [("qcsc",)], [("sq_tm",)], f"m{m}qr{h}")


    def _fir_gen(self, m, dc):
        V = self.V
        sets = [(0, 12), (12, 12), (24, 12), (36, 11), (47, 11), (58, 11)]
        if True:
            bls = self.nbank()
            bll = self.nbank()
            for si, (toff, tcnt) in enumerate(sets):
                par = (dc * 6 + si) % 2
                ring = self.r_fir[:, par * 12 * 128:(par * 12 + 12) * 128]
                self.ld(ring[:, 0:tcnt * 128],
                        self.scr["fir_b"][dc][:, toff * 128:(toff + tcnt) * 128],
                        [("rfir", par)], f"m{m}f{dc}_{si}", reads=[("sc_fir",)])
                for tj in range(tcnt):
                    t = toff + tj
                    if t < FIR_S:
                        tap = t
                        off = (FIR_L - 1) - (FIR_S - 1) + tap
                        self.MMb(bls, slice(0, ML), ring[:, tj * 128:(tj + 1) * 128],
                                 self.vT[:, dc * VL + off: dc * VL + off + ML],
                                 tap == 0, tap == FIR_S - 1,
                                 [("rfir", par), ("vT", dc)], f"m{m}fs{dc}_{t}")
                    else:
                        j = t - FIR_S
                        off = j
                        self.MMb(bll, slice(0, ML), ring[:, tj * 128:(tj + 1) * 128],
                                 self.vT[:, dc * VL + off: dc * VL + off + ML],
                                 j == 0, j == FIR_L - 1,
                                 [("rfir", par), ("vT", dc)], f"m{m}fl{dc}_{j}")
                yield
            V(lambda p, d=dc, b=bls: p.tensor_copy(
                self.lsT[:, d * ML:(d + 1) * ML], self.pb[b][:, 0:ML]),
              [("pb", bls)], [("lsT", dc)], f"m{m}fse{dc}")
            self.free_bank(bls)
            V(lambda p, d=dc, b=bll: p.tensor_copy(
                self.llT[:, d * ML:(d + 1) * ML], self.pb[b][:, 0:ML]),
              [("pb", bll)], [("llT", dc)], f"m{m}fle{dc}")
            self.free_bank(bll)

    def _ph2(self, sq, m):
        V = self.V
        if m == 0 and sq == 0:
            self.tap("lsT", self.lsT[:, :], [128, NCH * ML],
                     [("lsT", c) for c in range(NCH)], BF16)
            self.tap("llT", self.llT[:, :], [128, NCH * ML],
                     [("llT", c) for c in range(NCH)], BF16)
        for dc in range(NCH):
            V(lambda p, d=dc: p.tensor_copy(
                self.vT[:, d * VL: d * VL + FIR_L - 1],
                self.vT[:, d * VL + ML: d * VL + ML + FIR_L - 1]),
              [("vT", dc)], [("vT", dc)], f"m{m}vr{dc}")

    def _ph4a(self, sq, m):
        V = self.V
        for fc in range(NCH):
            par = fc % 2
            piece = self.r_w1a[:, par * 1024:(par + 1) * 1024]
            self.ld(piece, self.scr["w1a_b"][fc], [("rw1a", par)],
                    f"m{m}gw{fc}", reads=[("sc_w1a",)])
            ba = self.nbank()
            for kc in range(NCH):
                self.MMb(ba, slice(0, ML), piece[:, kc * 128:(kc + 1) * 128],
                         self.xT_bf[:, kc * ML:(kc + 1) * ML],
                         kc == 0, kc == NCH - 1,
                         [("rw1a", par), ("xTb", kc)], f"m{m}gA{fc}{kc}")
            V(lambda p, fc=fc, b=ba: p.tensor_copy(
                self.act_bf[:, fc * ML:(fc + 1) * ML], self.pb[b][:, 0:ML]),
              [("pb", ba)], [("act", fc)], f"m{m}gE{fc}")
            self.free_bank(ba)

    def _ph3_pair(self, sq, m, pair):
        pending = [(sub, self._head_gen(m, pair * 2 + sub, h))
                   for sub in range(2) for h in range(H)]
        ph6 = self._ph6_gen(sq, m, pair)
        ph6_in = False
        active = []
        holders = 0
        done_cnt = [0, 0]
        epi_added = [False, False]
        while pending or active:
            while pending and holders < 2 and len(active) < 4:
                sub, g = pending.pop(0)
                active.append([g, True, sub])
                holders += 1
            if not pending and not ph6_in:
                active.append([ph6, False, None])
                ph6_in = True
            rec = active.pop(0)
            try:
                tok = next(rec[0])
                if tok == "free" and rec[1]:
                    rec[1] = False
                    holders -= 1
                if tok == "hold":
                    continue
                active.append(rec)
            except StopIteration:
                if rec[1]:
                    holders -= 1
                sub = rec[2]
                if sub is not None:
                    done_cnt[sub] += 1
                    if done_cnt[sub] == H and not epi_added[sub]:
                        active.append(
                            [self._sub_epi(m, pair * 2 + sub), False, None])
                        epi_added[sub] = True
        for _ in ph6:
            pass
        if m == 0 and sq == 0 and pair == 0:
            self.tap("branches0", self.branches[0][:, :], [128, H * 1024],
                     [("br", h) for h in range(H)], BF16)
            self.tap("statsT", self.statsT[0:16, :], [16, H * 128],
                     [("statsT", h) for h in range(H)], BF16)
            self.tap("probs0", self.probs[:, :], [128, H * 4],
                     [("probs", h2) for h2 in range(H)], F32)
            self.tap("on0", self.o_n[:, :], [128, 1024],
                     [("o_n", h2) for h2 in range(H)], F32)

    def _head_gen(self, m, st, h):
        V, S_ = self.V, self.S_
        sub = st % 2
        branches = self.branches[sub]
        rbr = ("br", sub, h)
        sl = h % 2
        c0, c1 = 2 * h, 2 * h + 1
        base = h * 1024
        qTs = [self.qT[:, c * ML + st * 128: c * ML + (st + 1) * 128] for c in (c0, c1)]
        kTs = [self.kT[:, c * ML + st * 128: c * ML + (st + 1) * 128] for c in (c0, c1)]
        bcol = self.beta_tm[:, st * 4 + h: st * 4 + h + 1]
        sqcol = self.sq_tm[:, 4 * h + st: 4 * h + st + 1]
        Ssl = self.S_sb[:, h * 512:(h + 1) * 512]
        rS = ("S_sb", h)
        ktm = self.ktm[:, sl * 256:(sl + 1) * 256]
        vbk = self.vbk[:, sl * 256:(sl + 1) * 256]
        ui = vbk
        attnT = self.attnT[:, sl * 128:(sl + 1) * 128]
        S_bf = self.S_bf[:, sl * 512:(sl + 1) * 512]
        pbB = self.pbB[sl]
        rk, rvb = ("ktm", sl), ("vbk", sl)
        ru = rvb
        rat, rsb, rpB = ("attnT", sl), ("S_bf", sl), ("pbB", sl)
        ch = self.chain[sl]
        rch = lambda i, sl=sl: ("ch", sl, i)

        for i, c in enumerate((c0, c1)):
            self.TRb(sl, slice(i * 128, (i + 1) * 128),
                     self.vT[:, c * VL + FIR_L - 1 + st * 128:
                             c * VL + FIR_L - 1 + (st + 1) * 128],
                     [("vT", c)], f"m{m}vt{st}{h}{i}")
        yield
        dstv = branches[:, base + 768: base + 1024]
        V(lambda p, d=dstv: p.tensor_copy(d, pbB[:, 0:256]),
          [rpB], [rbr], f"m{m}vb{st}{h}")
        yield
        for i in range(2):
            self.TRb(sl, slice(i * 128, (i + 1) * 128), kTs[i],
                     [("kT", 2 * h + i)], f"m{m}kt{st}{h}{i}")
        yield
        V(lambda p, d=ktm: p.tensor_copy(d.bitcast(F32R), pbB[:, 0:256]),
          [rpB], [rk], f"m{m}ke{st}{h}")
        yield
        for bi, (srcT, rgn) in enumerate([(self.lsT, "lsT"), (self.llT, "llT")]):
            for i, c in enumerate((c0, c1)):
                self.TRb(sl, slice((bi * 2 + i) * 128, (bi * 2 + i + 1) * 128),
                         srcT[:, c * ML + st * 128: c * ML + (st + 1) * 128],
                         [(rgn, c)], f"m{m}bt{st}{h}{bi}{i}")
        yield
        dst = branches[:, base: base + 512]
        V(lambda p, d=dst: p.tensor_copy(d, pbB[:, 0:512]),
          [rpB], [rbr], f"m{m}be{st}{h}")
        bg = self.nbank()
        for i in range(2):
            self.MMb(bg, slice(0, 128), kTs[i], kTs[i], i == 0, i == 1,
                     [("kT", 2 * h + i)], f"m{m}G{st}{h}{i}")
        for i in range(2):
            self.MMb(bg, slice(128, 256), kTs[i], qTs[i], i == 0, i == 1,
                     [("kT", 2 * h + i), ("qT", 2 * h + i)], f"m{m}A{st}{h}{i}")
        yield
        V(lambda p, b=bg: p.scalar_tensor_tensor(
            ch[0][:, 0:128], self.pb[b][:, 0:128], bcol,
            self.m_stril[:, :], Op.mult, Op.mult),
          [("pb", bg), ("beta_tm",), ("c_mtl",)], [rch(0)], f"m{m}Am{st}{h}")
        V(lambda p, b=bg, d=attnT: p.tensor_tensor(
            d.bitcast(F32R), self.pb[b][:, 128:256],
            self.m_striu[:, :], Op.mult),
          [("pb", bg), ("c_mtu",)], [rat], f"m{m}At{st}{h}")
        self.free_bank(bg)
        V(lambda p, d=S_bf, S=Ssl: p.tensor_copy(d, S),
          [rS], [rsb], f"m{m}Sb{st}{h}")
        yield
        self.TRb(sl, slice(512, 640), ch[0][:, 0:128], [rch(0)], f"m{m}Y0{st}{h}")
        yield
        V(lambda p: p.tensor_copy(ch[0][:, 128:256], pbB[:, 512:640]),
          [rpB], [rch(0)], f"m{m}Ye{st}{h}")
        V(lambda p: p.scalar_tensor_tensor(
            ch[0][:, 256:384], ch[0][:, 128:256], -1.0,
            self.ident[:, :], Op.mult, Op.add),
          [rch(0), ("c_ident",), ("c_c",)], [rch(0)], f"m{m}M0{st}{h}")
        yield
        par = 0
        for lvl in range(1, 7):
            nxt = par ^ 1
            bc = self.nbank()
            self.MMb(bc, slice(0, 128), ch[par][:, 128:256], ch[par][:, 0:128],
                     True, True, [rch(par)], f"m{m}c1_{st}{h}{lvl}")
            self.MMb(bc, slice(128, 256), ch[par][:, 0:128], ch[par][:, 128:256],
                     True, True, [rch(par)], f"m{m}c2_{st}{h}{lvl}")
            yield
            V(lambda p, b=bc, n=nxt: p.tensor_copy(ch[n][:, 0:256],
                                                   self.pb[b][:, 0:256]),
              [("pb", bc)], [rch(nxt)], f"m{m}ce{st}{h}{lvl}")
            self.MMb(bc, slice(256, 384), ch[nxt][:, 0:128], ch[par][:, 256:384],
                     True, True, [rch(nxt), rch(par)], f"m{m}c3_{st}{h}{lvl}")
            yield
            V(lambda p, b=bc, n=nxt, o=par: p.tensor_tensor(
                ch[n][:, 256:384], self.pb[b][:, 256:384],
                ch[o][:, 256:384], Op.add),
              [("pb", bc), rch(par)], [rch(nxt)], f"m{m}cm{st}{h}{lvl}")
            self.free_bank(bc)
            par = nxt
            yield
        Tt = ch[par][:, 256:384]
        for i in range(2):
            V(lambda p, i=i: p.tensor_copy(
                vbk[:, i * 128:(i + 1) * 128].bitcast(F32R), kTs[i]),
              [("kT", 2 * h + i)], [rvb], f"m{m}ku{st}{h}{i}")
        yield
        bu = self.nbank()
        for i in range(2):
            self.MMr(bu, slice(0, 256), vbk[:, i * 128:(i + 1) * 128],
                     Ssl[:, i * 256:(i + 1) * 256], i == 0, i == 1,
                     [rvb, rS], f"m{m}kS{st}{h}{i}")
        bo = self.nbank()
        for i in range(2):
            self.MMb(bo, slice(0, 256), qTs[i], S_bf[:, i * 256:(i + 1) * 256],
                     i == 0, False, [("qT", 2 * h + i), rsb], f"m{m}oS{st}{h}{i}")
        yield
        dstvb = branches[:, base + 512: base + 768]
        V(lambda p, b=bu, d=vbk, s=dstv: p.tensor_tensor(
            d.bitcast(F32R), s, self.pb[b][:, 0:256], Op.subtract),
          [("pb", bu), rbr], [rvb], f"m{m}vk0{st}{h}")
        V(lambda p, d=dstvb, s=vbk: p.tensor_scalar(
            d, s, bcol, None, Op.mult),
          [rvb, ("beta_tm",)], [rbr], f"m{m}vk{st}{h}")
        yield
        self.MMb(bu, slice(256, 512), Tt, dstvb, True, True,
                 [rch(par), rbr], f"m{m}u{st}{h}")
        yield
        V(lambda p, b=bu, d=ui: p.tensor_copy(d.bitcast(F32R),
                                              self.pb[b][:, 256:512]),
          [("pb", bu)], [ru], f"m{m}ue{st}{h}")
        self.free_bank(bu)
        yield
        self.MMr(bo, slice(0, 256), attnT, ui, False, True,
                 [rat, ru], f"m{m}oA{st}{h}")
        yield
        dsto = branches[:, base + 512: base + 768]
        V(lambda p, d=dsto, b=bo: p.tensor_scalar(
            d, self.pb[b][:, 0:256], sqcol, None, Op.mult),
          [("pb", bo), ("sq_tm",)], [rbr], f"m{m}oe{st}{h}")
        self.free_bank(bo)
        yield
        bs = self.nbank()
        for i in range(2):
            self.MMr(bs, slice(i * 256, (i + 1) * 256), self.ident_r[:, :],
                     Ssl[:, i * 256:(i + 1) * 256], True, False,
                     [("c_identr",), rS], f"m{m}Sr{st}{h}{i}")
            self.MMr(bs, slice(i * 256, (i + 1) * 256),
                     ktm[:, i * 128:(i + 1) * 128], ui,
                     False, True, [rk, ru], f"m{m}Su{st}{h}{i}")
        yield
        V(lambda p, b=bs, S=Ssl: p.tensor_copy(S.bitcast(F32R),
                                               self.pb[b][:, 0:512]),
          [("pb", bs)], [rS], f"m{m}Sc{st}{h}")
        self.free_bank(bs)
        yield
        br = branches[:, base: base + 1024]
        s8, s16 = sl * 8, sl * 16
        stv = self.stats_tm
        rst, rcs, rtm = ("stat", sl), ("colsc", sl), ("stats_tm", sl)
        V(lambda p, br=br: p.tensor_reduce(
            self.stat_sc[:, s8:s8 + 4], br.rearrange("p (b n) -> p b n", b=4),
            AXX, Op.add),
          [rbr], [rst], f"m{m}ts{st}{h}")
        V(lambda p, br=br: p.tensor_reduce(
            self.stat_sc[:, s8 + 4:s8 + 8], br.rearrange("p (b n) -> p b n", b=4),
            AXX, Op.add, apply_absolute_value=True),
          [rbr], [rst], f"m{m}ta{st}{h}")
        for b4 in range(4):
            S_(lambda p, br=br, b4=b4: p.activation(
                self.sqsc[:, 0:256].bitcast(F32R), br[:, b4 * 256:(b4 + 1) * 256],
                AF.Square, accum_out=self.colsc[:, s8 + b4:s8 + b4 + 1]),
               [rbr], [("sqsc", 0), rcs], f"m{m}tq{st}{h}{b4}")
        yield
        stv16 = stv[:, s16:s16 + 16].rearrange("p (b j) -> p b j", j=4)
        jm, jv = stv16[:, :, 0:1], stv16[:, :, 1:2]
        ja, jl = stv16[:, :, 2:3], stv16[:, :, 3:4]
        sum4 = self.stat_sc[:, s8:s8 + 4].unsqueeze(2)
        abs4 = self.stat_sc[:, s8 + 4:s8 + 8].unsqueeze(2)
        sq4 = self.colsc[:, s8:s8 + 4].unsqueeze(2)
        m2t = self.colsc[:, s8 + 4:s8 + 8].unsqueeze(2)
        V(lambda p: p.tensor_scalar(jm, sum4, 1.0 / 256.0, None, Op.mult),
          [rst, ("c_c",)], [rtm], f"m{m}tm{st}{h}")
        V(lambda p: p.tensor_tensor(m2t, jm, jm, Op.mult),
          [rtm], [rcs], f"m{m}t2{st}{h}")
        V(lambda p: p.scalar_tensor_tensor(
            jv, sq4, 1.0 / 256.0, m2t, Op.mult, Op.subtract),
          [rcs, ("c_c",)], [rtm], f"m{m}tv{st}{h}")
        V(lambda p: p.tensor_scalar(ja, abs4, 1.0 / 256.0, None, Op.mult),
          [rst, ("c_c",)], [rtm], f"m{m}tb{st}{h}")
        S_(lambda p: p.activation(jl, sq4, AF.Sqrt),
           [rcs], [rtm], f"m{m}tl{st}{h}")
        yield
        bt = self.nbank()
        self.TRf(bt, slice(0, 128), stv[:, s16:s16 + 16], [rtm],
                 f"m{m}tt{st}{h}", rows=16)
        yield
        V(lambda p, b=bt, h=h, sub=sub: p.tensor_copy(
            self.statsT[0:16, sub * 512 + h * 128: sub * 512 + (h + 1) * 128],
            self.pb[b][0:16, 0:128]),
          [("pb", bt)], [("statsT", sub, h)], f"m{m}te{st}{h}")
        self.free_bank(bt)
        yield "free"

    def _sub_epi(self, m, st):
        V, S_ = self.V, self.S_
        P0 = st * 128
        sub = st % 2
        branches = self.branches[sub]
        rstT = [("statsT", sub, h) for h in range(H)]
        blg = self.nbank()
        for fc in range(NCH):
            hb = self.nbank()
            self.s.op("T", lambda p, fc=fc, b=hb: p.matmul(
                self.pb[b][:, 0:512],
                self.w1b[0:16, fc * 128:(fc + 1) * 128],
                self.statsT[0:16, sub * 512: sub * 512 + 512],
                start=True, stop=True),
                reads=[("c_w1b",)] + rstT, writes=[("pb", hb)],
                tag=f"m{m}gB{st}{fc}")
            yield
            hgs = self.hg[:, (fc % 2) * 512:(fc % 2 + 1) * 512]
            t1b = self.act_bf[:, fc * ML + P0: fc * ML + P0 + 128]
            t13 = t1b.unsqueeze(1).to_broadcast([128, 4, 128])
            V(lambda p, o=hgs, b=hb, t=t13: p.tensor_tensor(
                o.rearrange("p (h n) -> p h n", h=4),
                self.pb[b][:, 0:512].rearrange("p (h n) -> p h n", h=4),
                t, Op.add),
              [("pb", hb), ("act", fc)], [("hg", fc % 2)], f"m{m}gd{st}{fc}")
            self.free_bank(hb)
            S_(lambda p, o=hgs, fc=fc: p.activation(
                o, o, AF.Gelu, bias=self.b1g[:, fc:fc + 1]),
               [("hg", fc % 2), ("c_b1g",)], [("hg", fc % 2)], f"m{m}gg{st}{fc}")
            yield
            self.s.op("T", lambda p, fc=fc, o=hgs, b=blg: p.matmul(
                self.pb[b][0:4, 0:512],
                self.w2g_bf[:, fc * 4:(fc + 1) * 4],
                o, start=(fc == 0), stop=(fc == NCH - 1)),
                reads=[("c_w2gb",), ("hg", fc % 2)],
                writes=[("pb", blg)], tag=f"m{m}gl{st}{fc}")
            yield
        lgs = self.beTa[0:4, 0:512]
        S_(lambda p, b=blg, d=lgs: p.activation(
            d, self.pb[b][0:4, 0:512], AF.Identity, bias=self.b2g[0:4, 0:1]),
           [("pb", blg), ("c_b2g",)], [("beTa",)], f"m{m}lb{st}")
        self.free_bank(blg)
        yield
        bt2 = self.nbank()
        for h in range(H):
            self.TRf(bt2, slice(h * 4, (h + 1) * 4),
                     lgs[0:4, h * 128:(h + 1) * 128],
                     [("beTa",)], f"m{m}lt{st}{h}", pk=4)
        yield
        c = self.scz
        rcz = ("sczS",)
        V(lambda p, b=bt2: p.tensor_copy(c[:, 0:16], self.pb[b][:, 0:16]),
          [("pb", bt2)], [rcz], f"m{m}le{st}")
        self.free_bank(bt2)
        z3 = c[:, 0:16].rearrange("p (h b) -> p h b", h=4)
        V(lambda p: p.tensor_reduce(c[:, 16:20], z3, AXX, Op.max, negate=True),
          [rcz], [rcz], f"m{m}l1{st}")
        nm3 = c[:, 16:20].unsqueeze(2).to_broadcast([128, 4, 4])
        V(lambda p: p.tensor_tensor(
            c[:, 20:36].rearrange("p (h b) -> p h b", h=4), z3, nm3, Op.add),
          [rcz], [rcz], f"m{m}l2{st}")
        V(lambda p: p.tensor_scalar(c[:, 20:36], c[:, 20:36],
                                    self.invt[:, 0:1], None, Op.mult),
          [rcz, ("c_invt",)], [rcz], f"m{m}l2b{st}")
        yield
        S_(lambda p: p.activation(c[:, 20:36], c[:, 20:36], AF.Exp),
           [rcz], [rcz], f"m{m}l3{st}")
        e3 = c[:, 20:36].rearrange("p (h b) -> p h b", h=4)
        V(lambda p: p.tensor_reduce(c[:, 36:40], e3, AXX, Op.add),
          [rcz], [rcz], f"m{m}l4{st}")
        V(lambda p: p.reciprocal(c[:, 36:40], c[:, 36:40]),
          [rcz], [rcz], f"m{m}l5{st}")
        rc3 = c[:, 36:40].unsqueeze(2).to_broadcast([128, 4, 4])
        V(lambda p: p.scalar_tensor_tensor(
            c[:, 20:36].rearrange("p (h b) -> p h b", h=4), e3,
            1.0 - 2.0 * FLOOR_EPS, rc3, Op.mult, Op.mult),
          [rcz, ("c_c",)], [rcz], f"m{m}l6{st}")
        fl3 = self.floor_row[:, 0:4].unsqueeze(1).to_broadcast([128, 4, 4])
        V(lambda p: p.tensor_tensor(
            self.probs[:, 0:16].rearrange("p (h b) -> p h b", h=4), e3, fl3,
            Op.add),
          [rcz, ("c_floor",)], [("probs", h2) for h2 in range(H)],
          f"m{m}l7{st}")
        yield
        c3 = c[:, 96:128].rearrange("p (h x) -> p h x", h=4)
        V(lambda p: p.tensor_tensor(
            c3[:, :, 0:1], self.probs[:, 0:16].rearrange(
                "p (h b) -> p h b", h=4)[:, :, 0:1],
            self.sigc[:, 0:4].unsqueeze(2), Op.add),
          [("probs", h2) for h2 in range(H)] + [("c_sigc",)], [rcz],
          f"m{m}mc{st}")
        for h in range(H):
            base = h * 1024
            br = branches[:, base: base + 1024]
            pr = self.probs[:, h * 4:h * 4 + 4]
            cb = 96 + h * 8
            don = self.o_n[:, h * 256:(h + 1) * 256]
            V(lambda p, br=br, cb=cb, don=don: p.tensor_scalar(
                don, br[:, 0:256], c[:, cb:cb + 1], None, Op.mult),
              [("br", sub, h), rcz], [("o_n", h)], f"m{m}m0{st}{h}")
            for bi in range(1, 4):
                V(lambda p, br=br, pr=pr, bi=bi, don=don: p.scalar_tensor_tensor(
                    don, br[:, bi * 256:(bi + 1) * 256],
                    pr[:, bi:bi + 1], don, Op.mult, Op.add),
                  [("br", sub, h), ("probs", h), ("o_n", h)], [("o_n", h)],
                  f"m{m}m{bi}{st}{h}")
            yield
        for h in range(H):
            cb = 96 + h * 8
            don = self.o_n[:, h * 256:(h + 1) * 256]
            S_(lambda p, cb=cb, don=don: p.activation(
                self.sqsc[:, (h % 2) * 256:(h % 2) * 256 + 256].bitcast(F32R),
                don, AF.Square, accum_out=c[:, cb + 1:cb + 2]),
               [("o_n", h)], [("sqsc", h % 2), rcz], f"m{m}mq{st}{h}")
        yield
        S_(lambda p: p.activation(c3[:, :, 2:3], c3[:, :, 1:2],
                                  AF.Sqrt, scale=1.0 / 256.0, bias=1e-5),
           [rcz, ("c_c",)], [rcz], f"m{m}ms{st}")
        V(lambda p: p.reciprocal(c3[:, :, 3:4], c3[:, :, 2:3]),
          [rcz], [rcz], f"m{m}mr{st}")
        yield
        for h in range(H):
            cb = 96 + h * 8
            don = self.o_n[:, h * 256:(h + 1) * 256]
            V(lambda p, cb=cb, don=don: p.scalar_tensor_tensor(
                don, don, c[:, cb + 3:cb + 4],
                self.rms_bc[:, :], Op.mult, Op.mult),
              [rcz, ("c_rmsbc",), ("o_n", h)], [("o_n", h)], f"m{m}mo{st}{h}")
            yield
            bo2 = self.nbank()
            for i, cchunk in enumerate((2 * h, 2 * h + 1)):
                self.TRf(bo2, slice(i * 128, (i + 1) * 128),
                         self.o_n[:, cchunk * 128:(cchunk + 1) * 128],
                         [("o_n", h)], f"m{m}ot{st}{cchunk}")
            yield
            for i, cchunk in enumerate((2 * h, 2 * h + 1)):
                dstT = self.onT[:, cchunk * 256 + (st % 2) * 128:
                                cchunk * 256 + (st % 2 + 1) * 128]
                S_(lambda p, d=dstT, b=bo2, i=i: p.copy(
                    d, self.pb[b][:, i * 128:(i + 1) * 128]),
                   [("pb", bo2)], [("onT", cchunk)], f"m{m}oe{st}{cchunk}")
            self.free_bank(bo2)
            yield

    def _ph6_gen(self, sq, m, pair):
        V, S_ = self.V, self.S_
        t0 = m * ML
        P0 = pair * 256
        NT = 256
        for c in range(NCH):
            self.ld(self.xT_pair[:, c * 256:(c + 1) * 256],
                    self.x_in[sq, c, :, t0 + P0: t0 + P0 + 256],
                    [("xTp", c)], f"m{m}rx{pair}{c}")
        yield
        for dc in range(2):
            piece = self.r_wo[:, (dc % 2) * 1024:((dc % 2) + 1) * 1024]
            self.ld(piece, self.scr["wo_b"][dc], [("rwo", dc % 2)],
                    f"m{m}o{pair}w{dc}", reads=[("sc_wo",)])
        yield "hold"
        bsum = self.nbank()
        bsq = self.nbank()
        for dc in range(NCH):
            par = dc % 2
            piece = self.r_wo[:, par * 1024:(par + 1) * 1024]
            if dc >= 2:
                self.ld(piece, self.scr["wo_b"][dc], [("rwo", par)],
                        f"m{m}o{pair}w{dc}", reads=[("sc_wo",)])
            bo = self.nbank()
            for kc in range(NCH):
                self.MMb(bo, slice(0, NT), piece[:, kc * 128:(kc + 1) * 128],
                         self.onT[:, kc * 256: kc * 256 + NT],
                         kc == 0, kc == NCH - 1,
                         [("rwo", par), ("onT", kc)], f"m{m}wo{pair}{dc}{kc}")
            x1 = self.acc[:, dc * ML + P0: dc * ML + P0 + NT]
            V(lambda p, x1=x1, b=bo, dc=dc: p.tensor_tensor(
                x1, self.pb[b][:, 0:NT], self.xT_pair[:, dc * 256:(dc + 1) * 256],
                Op.add),
              [("pb", bo), ("xTp", dc)], [("acc", dc)], f"m{m}x1{pair}{dc}")
            self.free_bank(bo)
            S_(lambda p, x1=x1: p.copy(self.lnsc[:, 0:NT].bitcast(F32R), x1),
               [("acc", dc)], [("lnsc",)], f"m{m}sc{pair}{dc}")
            self.MMr(bsum, slice(0, NT), self.onesc_r[:, 0:1], self.lnsc[:, 0:NT],
                     dc == 0, dc == NCH - 1, [("c_onescr",), ("lnsc",)],
                     f"m{m}s1{pair}{dc}", mrows=1)
            S_(lambda p, x1=x1: p.activation(self.sqsc[:, 0:NT].bitcast(F32R),
                                             x1, AF.Square),
               [("acc", dc)], [("sqsc", 0), ("sqsc", 1)], f"m{m}sq{pair}{dc}")
            self.MMr(bsq, slice(0, NT), self.onesc_r[:, 0:1], self.sqsc[:, 0:NT],
                     dc == 0, dc == NCH - 1, [("c_onescr",), ("sqsc", 0), ("sqsc", 1)],
                     f"m{m}s2{pair}{dc}", mrows=1)
        self._ln_rows(m, pair, bsum, bsq, NT, ffn_r=True)
        self.free_bank(bsum)
        self.free_bank(bsq)
        bs1 = self.nbank()
        bs2 = self.nbank()
        bs3 = self.nbank()
        r = self.row1
        self.bc_row(bs1, slice(0, NT), r[0:1, 768:768 + NT], [("row1",)],
                    f"m{m}bs{pair}")
        self.bc_row(bs2, slice(0, NT), r[0:1, 1024:1024 + NT], [("row1",)],
                    f"m{m}bm{pair}")
        self.bc_row(bs3, slice(0, NT), r[0:1, 256:256 + NT], [("row1",)],
                    f"m{m}br{pair}")
        bg = self.nbank()
        for dc in range(NCH):
            hpar = dc % 2
            x1 = self.acc[:, dc * ML + P0: dc * ML + P0 + NT]
            V(lambda p, x1=x1: p.tensor_tensor(
                self.scx[:, 0:NT], x1, self.pb[bs1][:, 0:NT], Op.mult),
              [("pb", bs1), ("acc", dc)], [("scx", 0)], f"m{m}n1{pair}{dc}")
            V(lambda p, x1=x1: p.tensor_tensor(
                x1, self.scx[:, 0:NT], self.pb[bs2][:, 0:NT], Op.subtract),
              [("pb", bs2), ("scx", 0)], [("acc", dc)], f"m{m}n2{pair}{dc}")
            hsl = self.hrot[:, hpar * 256:(hpar + 1) * 256]
            V(lambda p, x1=x1, o=hsl, dc=dc: p.scalar_tensor_tensor(
                o.bitcast(F32R), x1, self.normw[:, dc:dc + 1],
                self.pb[bs3][:, 0:NT], Op.mult, Op.mult),
              [("pb", bs3), ("acc", dc), ("c_nw",)], [("hrot", hpar)],
              f"m{m}h{pair}{dc}")
            hb = self.h_bf[:, dc * ML + P0: dc * ML + P0 + NT]
            S_(lambda p, o=hb, i=hsl: p.copy(o, i),
               [("hrot", hpar)], [("h_bf", dc)], f"m{m}hb{pair}{dc}")
            self.MMr(bg, slice(0, NT), self.gatewt_r[:, dc * 8:(dc + 1) * 8], hsl,
                     dc == 0, dc == NCH - 1, [("c_gwr",), ("hrot", hpar)],
                     f"m{m}fg{pair}{dc}", mrows=8)
        self.free_bank(bs1)
        self.free_bank(bs2)
        self.free_bank(bs3)
        S_(lambda p, b=bg: p.activation(self.fglb[0:8, 0:NT], self.pb[b][0:8, 0:NT],
                                        AF.Identity, bias=self.gateb[0:8, 0:1]),
           [("pb", bg), ("c_gb",)], [("fglb",)], f"m{m}fb{pair}")
        self.free_bank(bg)
        for sub in range(2):
            st = pair * 2 + sub
            bt = self.nbank()
            self.TRf(bt, slice(0, 8), self.fglb[0:8, sub * 128:(sub + 1) * 128],
                     [("fglb",)], f"m{m}ft{pair}{sub}", pk=8)
            gl = self.gl_sb[:, st * 8:(st + 1) * 8]
            V(lambda p, d=gl, b=bt: p.tensor_copy(d, self.pb[b][:, 0:8]),
              [("pb", bt)], [("gl", st)], f"m{m}fe{pair}{sub}")
            self.free_bank(bt)
            c = self.scz
            mk1 = self.scz[:, 72:80]
            l2 = self.scz[:, 80:88]
            mk2 = self.scz[:, 88:96]
            V(lambda p, gl=gl: p.tensor_reduce(c[:, 64:65], gl, AXX, Op.max),
              [("gl", st)], [("scz6",)], f"m{m}u1{st}")
            V(lambda p, gl=gl: p.tensor_scalar(mk1, gl, c[:, 64:65], None,
                                               Op.is_equal),
              [("gl", st), ("scz6",)], [("scz6",)], f"m{m}u2{st}")
            V(lambda p, gl=gl: p.scalar_tensor_tensor(l2, mk1, -1e30, gl,
                                                      Op.mult, Op.add),
              [("scz6",), ("gl", st), ("c_c",)], [("scz6",)], f"m{m}u3{st}")
            V(lambda p: p.tensor_reduce(c[:, 65:66], l2, AXX, Op.max),
              [("scz6",)], [("scz6",)], f"m{m}u4{st}")
            V(lambda p: p.tensor_scalar(mk2, l2, c[:, 65:66], None, Op.is_equal),
              [("scz6",)], [("scz6",)], f"m{m}u5{st}")
            V(lambda p: p.tensor_tensor(c[:, 66:67], c[:, 65:66], c[:, 64:65],
                                        Op.subtract),
              [("scz6",)], [("scz6",)], f"m{m}u6{st}")
            S_(lambda p: p.activation(c[:, 67:68], c[:, 66:67], AF.Exp),
               [("scz6",)], [("scz6",)], f"m{m}u7{st}")
            V(lambda p: p.tensor_scalar(c[:, 68:69], c[:, 67:68], 1.0, None, Op.add),
              [("scz6",), ("c_c",)], [("scz6",)], f"m{m}u8{st}")
            V(lambda p: p.reciprocal(c[:, 69:70], c[:, 68:69]),
              [("scz6",)], [("scz6",)], f"m{m}u9{st}")
            V(lambda p: p.tensor_tensor(c[:, 70:71], c[:, 67:68], c[:, 69:70],
                                        Op.mult),
              [("scz6",)], [("scz6",)], f"m{m}ua{st}")
            ps = self.p8[:, st * 8:(st + 1) * 8]
            V(lambda p, d=ps: p.tensor_scalar(d, mk1, c[:, 69:70], None, Op.mult),
              [("scz6",)], [("p8", st)], f"m{m}ub{st}")
            V(lambda p, d=ps: p.scalar_tensor_tensor(d, mk2, c[:, 70:71], d,
                                                     Op.mult, Op.add),
              [("scz6",), ("p8", st)], [("p8", st)], f"m{m}uc{st}")
            bt2 = self.nbank()
            self.TRf(bt2, slice(0, 128), ps, [("p8", st)], f"m{m}ud{st}", rows=8)
            V(lambda p, b=bt2, st=st: p.tensor_copy(
                self.p8T_bf[0:8, st * 128:(st + 1) * 128], self.pb[b][0:8, 0:128]),
              [("pb", bt2)], [("p8Tb",)], f"m{m}ue2{st}")
            self.free_bank(bt2)
        if m == 0 and sq == 0 and pair == 1:
            self.tap("acc1", self.acc[:, :], [128, NCH * ML],
                     [("acc", dc2) for dc2 in range(NCH)], F32)
            self.tap("hbf", self.h_bf[:, :], [128, NCH * ML],
                     [("h_bf", dc2) for dc2 in range(NCH)], BF16)
            self.tap("p8t", self.p8T_bf[0:8, :], [8, ML], [("p8Tb",)], BF16)

    def _ln_rows(self, m, pair, bsum, bsq, NT, ffn_r):
        V, S_ = self.V, self.S_
        r = self.row1
        V(lambda p: p.tensor_scalar(r[0:1, 0:NT], self.pb[bsum][0:1, 0:NT],
                                    1.0 / 1024.0, None, Op.mult),
          [("pb", bsum), ("c_c",)], [("row1",)], f"m{m}ln1{pair}")
        V(lambda p: p.tensor_tensor(r[0:1, 256:256 + NT], r[0:1, 0:NT],
                                    r[0:1, 0:NT], Op.mult),
          [("row1",)], [("row1",)], f"m{m}ln2{pair}")
        V(lambda p: p.scalar_tensor_tensor(
            r[0:1, 512:512 + NT], self.pb[bsq][0:1, 0:NT], 1.0 / 1024.0,
            r[0:1, 256:256 + NT], Op.mult, Op.subtract),
          [("pb", bsq), ("row1",), ("c_c",)], [("row1",)], f"m{m}ln3{pair}")
        S_(lambda p: p.activation(r[0:1, 768:768 + NT], r[0:1, 512:512 + NT],
                                  AF.Sqrt, bias=1e-5),
           [("row1",), ("c_c",)], [("row1",)], f"m{m}ln4{pair}")
        V(lambda p: p.reciprocal(r[0:1, 768:768 + NT], r[0:1, 768:768 + NT]),
          [("row1",)], [("row1",)], f"m{m}ln5{pair}")
        S_(lambda p: p.copy(r[0:1, 768:768 + NT].bitcast(F32R),
                            r[0:1, 768:768 + NT]),
           [("row1",)], [("row1",)], f"m{m}ln5r{pair}")
        V(lambda p: p.tensor_tensor(r[0:1, 1024:1024 + NT].bitcast(F32R),
                                    r[0:1, 0:NT], r[0:1, 768:768 + NT], Op.mult),
          [("row1",)], [("row1",)], f"m{m}ln6{pair}")
        if ffn_r:
            V(lambda p: p.tensor_tensor(r[0:1, 256:256 + NT],
                                        r[0:1, 768:768 + NT],
                                        r[0:1, 768:768 + NT], Op.mult),
              [("row1",)], [("row1",)], f"m{m}ln7{pair}")
            V(lambda p: p.tensor_tensor(r[0:1, 256:256 + NT],
                                        r[0:1, 256:256 + NT],
                                        r[0:1, 512:512 + NT], Op.mult),
              [("row1",)], [("row1",)], f"m{m}ln8{pair}")
            S_(lambda p: p.activation(r[0:1, 256:256 + NT],
                                      r[0:1, 256:256 + NT], AF.Sqrt, bias=F32EPS),
               [("row1",), ("c_c",)], [("row1",)], f"m{m}ln9{pair}")
            V(lambda p: p.reciprocal(r[0:1, 256:256 + NT], r[0:1, 256:256 + NT]),
              [("row1",)], [("row1",)], f"m{m}lnA{pair}")
            V(lambda p: p.tensor_scalar(r[0:1, 768:768 + NT].bitcast(F32R),
                                        r[0:1, 768:768 + NT], 2.0, None, Op.mult),
              [("row1",), ("c_c",)], [("row1",)], f"m{m}lnB{pair}")
            V(lambda p: p.tensor_scalar(r[0:1, 1024:1024 + NT].bitcast(F32R),
                                        r[0:1, 1024:1024 + NT], 2.0, None,
                                        Op.mult),
              [("row1",), ("c_c",)], [("row1",)], f"m{m}lnC{pair}")
            V(lambda p: p.tensor_scalar(r[0:1, 256:256 + NT].bitcast(F32R),
                                        r[0:1, 256:256 + NT], 0.5, None, Op.mult),
              [("row1",), ("c_c",)], [("row1",)], f"m{m}lnD{pair}")

    def _ph7(self, sq, m):
        V, S_ = self.V, self.S_
        for e in range(NEXP):
            if True:
                bpe = self.nbank()
                self.MMb(bpe, slice(0, ML), self.sel8[0:8, e * 128:(e + 1) * 128],
                         self.p8T_bf[0:8, :], True, True,
                         [("c_sel8",), ("p8Tb",)], f"m{m}e{e}b")
                V(lambda p, b=bpe: p.tensor_scalar(
                    self.p_be[:, :], self.pb[b][:, 0:ML], 1.0 / ALPHA, None,
                    Op.mult),
                  [("pb", bpe), ("c_c",)], [("p_be",)], f"m{m}e{e}p")
                self.free_bank(bpe)
                for fc in range(NCH):
                    s1 = (e * NCH * 2 + 2 * fc) % 3
                    pg = self.r_f1[:, s1 * 1024:(s1 + 1) * 1024]
                    self.ld(pg, self.scr["fw1_b"][e, fc], [("rf1", s1)],
                            f"m{m}e{e}wg{fc}", reads=[("sc_f1",)])
                    bG = self.nbank()
                    for kc in range(NCH):
                        self.MMb(bG, slice(0, ML), pg[:, kc * 128:(kc + 1) * 128],
                                 self.h_bf[:, kc * ML:(kc + 1) * ML],
                                 kc == 0, kc == NCH - 1,
                                 [("rf1", s1), ("h_bf", kc)], f"m{m}e{e}G{fc}{kc}")
                    s1b = (e * NCH * 2 + 2 * fc + 1) % 3
                    pl = self.r_f1[:, s1b * 1024:(s1b + 1) * 1024]
                    self.ld(pl, self.scr["fw1_b"][e, NCH + fc], [("rf1", s1b)],
                            f"m{m}e{e}wl{fc}", reads=[("sc_f1",)])
                    bL = self.nbank()
                    for kc in range(NCH):
                        self.MMb(bL, slice(0, ML), pl[:, kc * 128:(kc + 1) * 128],
                                 self.h_bf[:, kc * ML:(kc + 1) * ML],
                                 kc == 0, kc == NCH - 1,
                                 [("rf1", s1b), ("h_bf", kc)], f"m{m}e{e}L{fc}{kc}")
                    S_(lambda p, b=bG, e=e, fc=fc: p.activation(
                        self.scx[:, :], self.pb[b][:, 0:ML], AF.Silu, scale=ALPHA,
                        bias=self.fb1s[:, e * 8 + fc:e * 8 + fc + 1]),
                       [("pb", bG), ("c_fb1s",)], [("scx", 0), ("scx", 1)],
                       f"m{m}e{e}s1{fc}")
                    self.free_bank(bG)
                    S_(lambda p, b=bL, e=e, fc=fc: p.activation(
                        self.scy[:, :], self.pb[b][:, 0:ML], AF.Identity,
                        bias=self.fb1p[:, e * 8 + fc:e * 8 + fc + 1]),
                       [("pb", bL), ("c_fb1p",)], [("scy",)], f"m{m}e{e}s2{fc}")
                    self.free_bank(bL)
                    V(lambda p: p.tensor_scalar(self.scy[:, :], self.scy[:, :],
                                                8.0, -6.0, Op.min, Op.max),
                      [("scy",), ("c_c",)], [("scy",)], f"m{m}e{e}s3{fc}")
                    V(lambda p: p.tensor_tensor(
                        self.scy[:, :], self.scy[:, :], self.p_be[:, :], Op.mult),
                      [("scy",), ("p_be",)], [("scy",)], f"m{m}e{e}s4{fc}")
                    act = self.act_bf[:, fc * ML:(fc + 1) * ML]
                    V(lambda p, a=act: p.scalar_tensor_tensor(
                        a, self.scx[:, :], SWISH_CAP, self.scy[:, :],
                        Op.min, Op.mult),
                      [("scx", 0), ("scx", 1), ("scy",), ("c_c",)], [("act", fc)],
                      f"m{m}e{e}s5{fc}")
                for g in range(2):
                    m2banks = [self.nbank() for _ in range(4)]
                    for dj in range(4):
                        dd = g * 4 + dj
                        s2 = (e * 4 + dj) % 2
                        pw = self.r_fw2[:, s2 * 1024:(s2 + 1) * 1024]
                        self.ld(pw, self.scr["fw2_b"][e, dd], [("rfw2", s2)],
                                f"m{m}e{e}w2{dd}", reads=[("sc_f2",)])
                        bank = m2banks[dj]
                        cols = slice(0, 512)
                        if e == 0:
                            self.MMb(bank, cols,
                                     self.fb2_bf[0:8, dd * 128:(dd + 1) * 128],
                                     self.p8T_bf[0:8, 0:512],
                                     True, False,
                                     [("c_fb2b",), ("p8Tb",)],
                                     f"m{m}b2_{dd}")
                        for ic in range(NCH):
                            first = (e != 0 and ic == 0)
                            last = (ic == NCH - 1)
                            self.MMb(bank, cols,
                                     pw[:, ic * 128:(ic + 1) * 128],
                                     self.act_bf[:, ic * ML: ic * ML + 512],
                                     first, last,
                                     [("rfw2", s2), ("act", ic)],
                                     f"m{m}e{e}m2_{dd}_{ic}")
                    for dj in range(4):
                        dd = g * 4 + dj
                        bank = m2banks[dj]
                        x2 = self.acc[:, dd * ML: dd * ML + 512]
                        V(lambda p, x2=x2, b=bank: p.tensor_tensor(
                            x2, x2, self.pb[b][:, 0:512], Op.add),
                          [("pb", bank), ("acc", dd)], [("acc", dd)],
                          f"m{m}fa{e}_{dd}")
                        self.free_bank(bank)

    def _ph8(self, sq, m):
        V, S_ = self.V, self.S_
        t0 = m * ML
        if m == 0 and sq == 0:
            self.tap("acc2", self.acc[:, :], [128, NCH * ML],
                     [("acc", dc2) for dc2 in range(NCH)], F32)
        for pair in range(2):
            P0 = pair * 256
            NT = 256
            bsum = self.nbank()
            bsq = self.nbank()
            for dc in range(NCH):
                x2 = self.acc[:, dc * ML + P0: dc * ML + P0 + NT]
                S_(lambda p, x2=x2: p.copy(self.lnsc[:, 0:NT].bitcast(F32R), x2),
                   [("acc", dc)], [("lnsc",)], f"m{m}zc{pair}{dc}")
                self.MMr(bsum, slice(0, NT), self.onesc_r[:, 0:1],
                         self.lnsc[:, 0:NT], dc == 0, dc == NCH - 1,
                         [("c_onescr",), ("lnsc",)], f"m{m}z1{pair}{dc}", mrows=1)
                S_(lambda p, x2=x2: p.activation(self.sqsc[:, 0:NT].bitcast(F32R),
                                                 x2, AF.Square),
                   [("acc", dc)], [("sqsc", 0), ("sqsc", 1)], f"m{m}zq{pair}{dc}")
                self.MMr(bsq, slice(0, NT), self.onesc_r[:, 0:1], self.sqsc[:, 0:NT],
                         dc == 0, dc == NCH - 1, [("c_onescr",), ("sqsc", 0), ("sqsc", 1)],
                         f"m{m}z2{pair}{dc}", mrows=1)
            self._ln_rows(m, 10 + pair, bsum, bsq, NT, ffn_r=False)
            self.free_bank(bsum)
            self.free_bank(bsq)
            bs1 = self.nbank()
            bs2 = self.nbank()
            r = self.row1
            self.bc_row(bs1, slice(0, NT), r[0:1, 768:768 + NT], [("row1",)],
                        f"m{m}zb{pair}")
            self.bc_row(bs2, slice(0, NT), r[0:1, 1024:1024 + NT], [("row1",)],
                        f"m{m}zm{pair}")
            for sub in range(2):
                st = pair * 2 + sub
                for half in range(2):
                    bank = self.nbank()
                    accs = self.acc[:, half * 4 * ML:(half + 1) * 4 * ML]
                    x23 = accs.rearrange("p (d n) -> p d n", n=ML)[
                        :, :, st * 128:(st + 1) * 128]
                    racc = [("acc", half * 4 + c4) for c4 in range(4)]
                    bb1 = self.pb[bs1][:, sub * 128:(sub + 1) * 128]
                    bb2 = self.pb[bs2][:, sub * 128:(sub + 1) * 128]
                    V(lambda p, x23=x23, bb1=bb1: p.tensor_tensor(
                        self.scx[:, 0:512].rearrange("p (d n) -> p d n", d=4),
                        x23, bb1.unsqueeze(1).to_broadcast([128, 4, 128]),
                        Op.mult),
                      [("pb", bs1)] + racc, [("scx", 0), ("scx", 1)],
                      f"m{m}za{st}{half}")
                    V(lambda p, bb2=bb2: p.tensor_tensor(
                        self.scy[:, 0:512].rearrange("p (d n) -> p d n", d=4),
                        self.scx[:, 0:512].rearrange("p (d n) -> p d n", d=4),
                        bb2.unsqueeze(1).to_broadcast([128, 4, 128]),
                        Op.subtract),
                      [("pb", bs2), ("scx", 0), ("scx", 1)], [("scy",)],
                      f"m{m}zs{st}{half}")
                    for c4 in range(4):
                        self.TRf(bank, slice(c4 * 128, (c4 + 1) * 128),
                                 self.scy[:, c4 * 128:(c4 + 1) * 128],
                                 [("scy",)], f"m{m}zt{st}{half}{c4}")
                    if half == 0:
                        stg, rstg = self.scy, [("scy",)]
                    else:
                        stg, rstg = self.scx, [("scx", 0), ("scx", 1)]
                    V(lambda p, b=bank, stg=stg: p.tensor_copy(
                        stg[:, 0:512], self.pb[b][:, 0:512]),
                      [("pb", bank)], rstg, f"m{m}zo{st}{half}")
                    self.free_bank(bank)
                    self.st_out(self.y_out[sq, t0 + st * 128: t0 + (st + 1) * 128,
                                           half * 512:(half + 1) * 512],
                                stg[:, 0:512], rstg, f"m{m}so{st}{half}")
            self.free_bank(bs1)
            self.free_bank(bs2)

def build_programs(nseq, seqlen, taps=()):
    mk = MK(nseq, seqlen, taps)
    nc = mk.build()
    return mk, nc


def kernel(**inputs):
    x = np.ascontiguousarray(inputs["L0_attn_hidden_states"], dtype=np.float32)
    B, S, _ = x.shape
    n_cores = 8
    per = B // n_cores
    host = prepare_host({k: np.asarray(v) for k, v in inputs.items()})
    mk, nc = build_programs(per, S)
    in_maps = []
    for c in range(n_cores):
        xc = x[c * per:(c + 1) * per].reshape(per, S, NCH, 128)
        im = {"x": np.ascontiguousarray(xc.transpose(0, 2, 3, 1))}
        im.update({k + "_d": v for k, v in host.items()})
        in_maps.append(im)
    res = run_bass_kernel_spmd(nc, in_maps, core_ids=list(range(n_cores)))
    out = np.concatenate([res.results[c]["y"] for c in range(n_cores)], axis=0)
    return out.astype(np.float32)


# version 37
# speedup vs baseline: 1.1559x; 1.1559x over previous
import numpy as np
import ml_dtypes
from contextlib import ExitStack

import concourse.bass as bass
import concourse.mybir as mybir
from concourse.bass_utils import run_bass_kernel_spmd
from concourse.alu_op_type import AluOpType as Op

F32 = mybir.dt.float32
F32R = mybir.dt.float32r
BF16 = mybir.dt.bfloat16
AF = mybir.ActivationFunctionType
AXX = mybir.AxisListType.X

D = 1024
H = 4
NCH = 8
CONV_K = 4
FIR_S = 5
FIR_L = 64
NTAP = FIR_S + FIR_L
NEXP = 8
ML = 512
FLOOR_EPS = 0.08
ALPHA = 1.702
SWISH_CAP = float(ALPHA * 7.0 / (1.0 + np.exp(-ALPHA * 7.0)))
F32EPS = float(np.finfo(np.float32).eps)
VL = FIR_L - 1 + ML
CL = CONV_K - 1 + ML

ENGINES = ("T", "V", "S", "G", "D")
ND_SEM = 6
SEM_NAMES = ["T", "V", "S", "G"] + [f"D{i}" for i in range(ND_SEM)] + ["GD"]


class _Rec:
    __slots__ = ("idx", "eng", "seq", "fn", "deps", "clock", "inc", "is_dma",
                 "sem", "wv", "self_wait", "tag")


class Sched:

    def __init__(self):
        self.ops = []
        self.by_eng = {e: [] for e in ENGINES}
        self.eng_seq = {e: 0 for e in ENGINES}
        self.eng_clock = {e: {n: 0 for n in SEM_NAMES} for e in ENGINES}
        self.last_w = {}
        self.readers = {}
        self.final_deps = []
        self.d_rank = 0
        self.sem_counts = {n: 0 for n in SEM_NAMES}

    def op(self, eng, fn, reads=(), writes=(), tag="", is_dma=False):
        idx = len(self.ops)
        seq = self.eng_seq[eng] + 1
        self.eng_seq[eng] = seq
        deps = set()
        nopr = set()
        for r in reads:
            w = self.last_w.get(r)
            if w is not None:
                deps.add(w)
                if r[0] in ("pb", "pbB"):
                    nopr.add(w)
            self.readers.setdefault(r, []).append(idx)
        for r in writes:
            w = self.last_w.get(r)
            if w is not None:
                deps.add(w)
                if r[0] in ("pb", "pbB"):
                    nopr.add(w)
            for rd in self.readers.get(r, ()):
                if rd != idx:
                    deps.add(rd)
                    if r[0] in ("pb", "pbB"):
                        nopr.add(rd)
            self.readers[r] = []
            self.last_w[r] = idx
        ec = self.eng_clock[eng]
        real = []
        for d in sorted(deps):
            od = self.ops[d]
            if od.eng == "T" and eng == "T":
                for k, v in od.clock.items():
                    if v > ec[k]:
                        ec[k] = v
                continue
            kv = od.wv if od.is_dma else od.seq
            if d not in nopr and ec[od.sem] >= kv:
                continue
            real.append(d)
        for d in real:
            for k, v in self.ops[d].clock.items():
                if v > ec[k]:
                    ec[k] = v
        rec = _Rec()
        rec.idx, rec.eng, rec.seq, rec.fn = idx, eng, seq, fn
        rec.is_dma = is_dma or eng == "D"
        rec.inc = rec.is_dma
        rec.self_wait = None
        if rec.is_dma:
            if eng == "D":
                self.d_rank += 1
                rec.sem = f"D{(self.d_rank - 1) % ND_SEM}"
            else:
                rec.sem = "GD"
            n = self.sem_counts[rec.sem] + 1
            self.sem_counts[rec.sem] = n
            rec.wv = 16 * n
            if n > 1:
                rec.self_wait = (rec.sem, 16 * (n - 1))
                if ec[rec.sem] < 16 * (n - 1):
                    ec[rec.sem] = 16 * (n - 1)
        else:
            rec.sem = eng
            rec.wv = 0
        if eng == "T":
            ec["T"] = seq
        clock = dict(ec)
        kown = rec.wv if rec.is_dma else seq
        if clock[rec.sem] < kown:
            clock[rec.sem] = kown
        rec.deps, rec.clock = real, clock
        rec.tag = tag
        self.ops.append(rec)
        self.by_eng[eng].append(rec)
        return idx

    def dma(self, fn, reads=(), writes=(), tag="", queue="D"):
        return self.op(queue, fn, reads, writes, tag, is_dma=True)

    def barrier_on(self, idxs):
        self.final_deps.extend(idxs)

    def finalize(self, sems):
        for rec in self.ops:
            for d in rec.deps:
                self.ops[d].inc = True
        for d in self.final_deps:
            self.ops[d].inc = True
        cnt = {e: 0 for e in ENGINES}
        for rec in self.ops:
            if rec.inc and not rec.is_dma:
                cnt[rec.eng] += 1
                rec.wv = cnt[rec.eng]
        self.sems = sems

    def emit(self, eng, proxy, final_extra=None):
        for rec in self.by_eng[eng]:
            waits = {}
            for d in rec.deps:
                dr = self.ops[d]
                if waits.get(dr.sem, -1) < dr.wv:
                    waits[dr.sem] = dr.wv
            if rec.self_wait is not None:
                snm, sv = rec.self_wait
                if waits.get(snm, -1) < sv:
                    waits[snm] = sv
            for key in sorted(waits):
                proxy.wait_ge(self.sems[key], waits[key])
            inst = rec.fn(proxy)
            if rec.inc:
                inst.then_inc(self.sems[rec.sem], 16 if rec.is_dma else 1)
        if final_extra is not None:
            final_extra(proxy)


CONST_VALUES = [0.0, 1.0, -1.0, 1e-12, 1e-5, F32EPS, 1e-6, 1.0 / 256.0,
                1.0 / 1024.0, 1.0 - 2.0 * FLOOR_EPS, SWISH_CAP, ALPHA,
                1.0 / ALPHA, -6.0, 8.0, -1e30, 7.0, 0.5, 2.0, 1.0 + 1e-30]



def _stream_w(w):
    d_in, d_out = w.shape
    nk, nd = d_in // 128, d_out // 128
    v = w.reshape(nk, 128, nd, 128).transpose(2, 1, 0, 3)
    return np.ascontiguousarray(v.reshape(nd, 128, nk * 128))


def _diag_tiles(w_taps):
    T = w_taps.shape[1]
    out = np.zeros((NCH, 128, T * 128), np.float32)
    ii = np.arange(128)
    for c in range(NCH):
        for t in range(T):
            out[c, ii, t * 128 + ii] = w_taps[c * 128:(c + 1) * 128, t]
    return out


def prepare_host(inp):
    g = {}
    g["wq_s"] = _stream_w(inp["L0_attn_Wq"])
    g["wk_s"] = _stream_w(inp["L0_attn_Wk"])
    g["wv_s"] = _stream_w(inp["L0_attn_Wv"])
    g["wb_s"] = np.ascontiguousarray(
        inp["L0_attn_Wb"].reshape(NCH, 128, H).transpose(1, 0, 2).reshape(128, NCH * H))
    w1 = inp["L0_attn_mlp_w1"]
    g["w1a_s"] = _stream_w(w1[0:1024, :])
    g["w1b"] = np.ascontiguousarray(w1[1024:1040, :])
    g["w2g"] = np.ascontiguousarray(
        inp["L0_attn_mlp_w2"].reshape(NCH, 128, H).transpose(1, 0, 2).reshape(128, NCH * H))
    g["b1g"] = np.ascontiguousarray(inp["L0_attn_mlp_b1"].reshape(NCH, 128).T)
    g["b2g"] = np.ascontiguousarray(inp["L0_attn_mlp_b2"].reshape(H, 1))
    g["logit_temp"] = np.ascontiguousarray(inp["L0_attn_logit_temp"].reshape(1, 1))
    g["crl"] = np.ascontiguousarray(inp["L0_attn_conv_res_logit"].reshape(1, H))
    g["rms_row"] = np.ascontiguousarray(inp["L0_attn_rms_w"].reshape(1, 256))
    g["wo_s"] = _stream_w(inp["L0_attn_Wo"])
    cw = np.concatenate([inp["L0_attn_conv_w_q"], inp["L0_attn_conv_w_k"],
                         inp["L0_attn_conv_w_v"]], axis=1)
    g["conv_diag"] = _diag_tiles(cw)
    fir = np.concatenate([inp["L0_attn_fir_short_w"].reshape(1024, FIR_S),
                          inp["L0_attn_fir_long_w"].reshape(1024, FIR_L)], 1)
    g["fir_diag"] = _diag_tiles(fir)
    g["norm_w"] = np.ascontiguousarray(inp["L0_ffn_norm_w"].reshape(NCH, 128).T)
    g["gate_w_t"] = np.ascontiguousarray(
        inp["L0_ffn_gate_w"].T.reshape(NCH, 128, NEXP).transpose(1, 0, 2)
        .reshape(128, NCH * NEXP))
    g["gate_b"] = np.ascontiguousarray(inp["L0_ffn_gate_b"].reshape(NEXP, 1))
    m1 = inp["L0_ffn_mlp1_w"]
    w1s = np.empty((NEXP, 16, 128, 1024), np.float32)
    for e in range(NEXP):
        w1s[e, 0:8] = _stream_w(np.ascontiguousarray(m1[e, 0::2, :].T))
        w1s[e, 8:16] = _stream_w(np.ascontiguousarray(m1[e, 1::2, :].T))
    g["ffn_w1_s"] = w1s
    m2 = inp["L0_ffn_mlp2_w"]
    w2s = np.empty((NEXP, 8, 128, 1024), np.float32)
    for e in range(NEXP):
        w2s[e] = _stream_w(np.ascontiguousarray(m2[e].T))
    g["ffn_w2_s"] = w2s
    b1 = inp["L0_ffn_mlp1_b"]
    bb = np.concatenate([b1[:, 0::2].reshape(NEXP, 8, 128),
                         b1[:, 1::2].reshape(NEXP, 8, 128)], axis=1)
    g["ffn_b1"] = np.ascontiguousarray(bb.transpose(2, 0, 1).reshape(128, NEXP * 16))
    g["ffn_b2"] = np.ascontiguousarray(inp["L0_ffn_mlp2_b"])
    g["ident"] = np.eye(128, dtype=np.float32)
    g["ident_bf"] = np.eye(128, dtype=ml_dtypes.bfloat16)
    g["ones_row"] = np.ones((1, 128), np.float32)
    g["ones_col"] = np.ones((128, 1), np.float32)
    g["m_stril"] = np.tril(np.ones((128, 128), ml_dtypes.bfloat16), -1)
    g["m_striu"] = np.triu(np.ones((128, 128), np.float32), 0)
    g["floor_row"] = np.tile(np.array([FLOOR_EPS, FLOOR_EPS, 0, 0], np.float32),
                             (128, 1))
    g["consts"] = np.tile(np.array(CONST_VALUES, np.float32), (128, 1))
    sel = np.zeros((8, 8 * 128), ml_dtypes.bfloat16)
    for e in range(8):
        sel[e, e * 128:(e + 1) * 128] = 1.0
    g["sel8"] = sel
    return g


class MK:
    def __init__(self, nseq, seqlen, taps=()):
        assert seqlen % ML == 0
        self.nseq, self.seqlen = nseq, seqlen
        self.nmac = seqlen // ML
        self.taps = list(taps)
        self.tap_specs = []
        self.nc = bass.Bass("TRN2", target_bir_lowering=False)
        self.s = Sched()
        self.es = ExitStack()
        self._nb = 0
        self._pool = [0, 1, 2, 3, 4, 5]
        self._held = set()

    def sb(self, name, shape, dtype):
        return self.es.enter_context(self.nc.sbuf_tensor(name, list(shape), dtype))

    def nbank(self):
        for _ in range(len(self._pool)):
            self._nb = (self._nb + 1) % len(self._pool)
            b = self._pool[self._nb]
            if b not in self._held:
                self._held.add(b)
                return b
        raise RuntimeError("no free psum bank")

    def free_bank(self, b):
        self._held.discard(b)

    def set_pool(self, pool):
        self._pool = list(pool)
        self._nb = 0

    def ld(self, dst, src, writes, tag, reads=()):
        return self.s.op("D", lambda p, d=dst, s_=src: p.dma_start(out=d, in_=s_),
                         reads=list(reads), writes=list(writes), tag=tag)

    def st_out(self, dst, src, reads, tag):
        i = self.s.op("D", lambda p, d=dst, s_=src: p.dma_start(out=d, in_=s_),
                      reads=list(reads), writes=[], tag=tag)
        self.s.barrier_on([i])

    def V(self, fn, reads, writes, tag):
        return self.s.op("V", fn, reads, writes, tag)

    def S_(self, fn, reads, writes, tag):
        return self.s.op("S", fn, reads, writes, tag)

    def MMb(self, bank, cols, lhsT, rhs, first, last, reads, tag):
        o = self.pb[bank][:, cols]
        return self.s.op("T", lambda p, o=o, a=lhsT, b=rhs, f=first, l=last:
                         p.matmul(o, a, b, start=f, stop=l),
                         reads=list(reads), writes=[("pb", bank)], tag=tag)

    def MMr(self, bank, cols, lhsT, rhs, first, last, reads, tag, mrows=None):
        o = self.pb[bank][0:mrows, cols] if mrows else self.pb[bank][:, cols]
        a, b = lhsT.bitcast(F32R), rhs.bitcast(F32R)
        return self.s.op("T", lambda p, o=o, a=a, b=b, f=first, l=last:
                         p.matmul(o, a, b, start=f, stop=l),
                         reads=list(reads), writes=[("pb", bank)], tag=tag)

    def TRf(self, bank, cols, in_ap, reads, tag, rows=None, pk=128):
        o = self.pb[bank][0:rows, cols] if rows else self.pb[bank][:, cols]
        ident = self.ident[0:pk, 0:pk]
        return self.s.op("T", lambda p, o=o, a=in_ap, i=ident:
                         p.transpose(o, a, i),
                         reads=list(reads) + [("c_ident",)],
                         writes=[("pb", bank)], tag=tag)

    def TRb(self, slot, cols, in_ap, reads, tag, rows=None):
        o = self.pbB[slot][0:rows, cols] if rows else self.pbB[slot][:, cols]
        return self.s.op("T", lambda p, o=o, a=in_ap:
                         p.transpose(o, a, self.ident_bf[:, :]),
                         reads=list(reads) + [("c_identb",)],
                         writes=[("pbB", slot)], tag=tag)

    def bc_row(self, bank, cols, row_ap, reads, tag):
        o = self.pb[bank][:, cols]
        a = self.m_striu[0:1, :]
        return self.s.op("T", lambda p, o=o, a=a, b=row_ap:
                         p.matmul(o, a, b, start=True, stop=True),
                         reads=list(reads) + [("c_mtu",)],
                         writes=[("pb", bank)], tag=tag)

    def tap(self, name, ap, shape, reads, dtype=F32):
        if name not in self.taps:
            return
        d = self.nc.dram_tensor("tap_" + name, list(shape), dtype,
                                kind="ExternalOutput").ap()
        self.tap_specs.append(name)
        i = self.s.dma(lambda p, d=d, a=ap: p.dma_start(out=d, in_=a),
                       reads=list(reads), writes=[], tag="tap_" + name)
        self.s.barrier_on([i])

    def build(self):
        nc, s = self.nc, self.s
        NS, SL = self.nseq, self.seqlen
        self.x_in = nc.dram_tensor("x", [NS, NCH, 128, SL], F32, kind="ExternalInput").ap()
        self.y_out = nc.dram_tensor("y", [NS, SL, D], F32, kind="ExternalOutput").ap()
        din = {}
        for nm, shp, dt in [
            ("wq_s", [8, 128, 1024], F32), ("wk_s", [8, 128, 1024], F32),
            ("wv_s", [8, 128, 1024], F32), ("wb_s", [128, 32], F32),
            ("w1a_s", [8, 128, 1024], F32), ("w1b", [16, 1024], F32),
            ("w2g", [128, 32], F32), ("b1g", [128, 8], F32), ("b2g", [4, 1], F32),
            ("logit_temp", [1, 1], F32), ("crl", [1, 4], F32),
            ("rms_row", [1, 256], F32), ("wo_s", [8, 128, 1024], F32),
            ("conv_diag", [8, 128, 12 * 128], F32),
            ("fir_diag", [8, 128, NTAP * 128], F32),
            ("norm_w", [128, 8], F32), ("gate_w_t", [128, 64], F32),
            ("gate_b", [8, 1], F32),
            ("ffn_w1_s", [NEXP, 16, 128, 1024], F32),
            ("ffn_w2_s", [NEXP, 8, 128, 1024], F32),
            ("ffn_b1", [128, NEXP * 16], F32), ("ffn_b2", [8, 1024], F32),
            ("ident", [128, 128], F32), ("ident_bf", [128, 128], BF16),
            ("ones_row", [1, 128], F32), ("ones_col", [128, 1], F32),
            ("m_stril", [128, 128], BF16), ("m_striu", [128, 128], F32),
            ("floor_row", [128, 4], F32), ("consts", [128, len(CONST_VALUES)], F32),
            ("sel8", [8, 8 * 128], BF16),
        ]:
            din[nm] = nc.dram_tensor(nm + "_d", shp, dt, kind="ExternalInput").ap()
        self.din = din
        scr = {}
        for nm, shp in [("wq_b", [8, 128, 1024]), ("wk_b", [8, 128, 1024]),
                        ("wv_b", [8, 128, 1024]), ("w1a_b", [8, 128, 1024]),
                        ("wo_b", [8, 128, 1024]), ("conv_b", [8, 128, 12 * 128]),
                        ("fir_b", [8, 128, NTAP * 128]),
                        ("fw1_b", [NEXP, 16, 128, 1024]),
                        ("fw2_b", [NEXP, 8, 128, 1024]),
                        ("w1b_c", [16, 1024])]:
            scr[nm] = nc.dram_tensor(nm, shp, BF16, kind="Internal").ap()
        self.scr = scr

        sb = self.sb
        self.consts = sb("consts", [128, len(CONST_VALUES)], F32)
        self.wb = sb("wb", [128, 32], BF16)
        self.w1b = sb("w1b", [16, 1024], BF16)
        self.w2g = sb("w2g", [128, 32], F32)
        self.b1g = sb("b1g", [128, 8], F32)
        self.b2g = sb("b2g", [4, 1], F32)
        self.lt = sb("lt", [1, 8], F32)
        self.crl = sb("crl", [1, 4], F32)
        self.rmsrow = sb("rmsrow", [1, 256], F32)
        self.normw = sb("normw", [128, 8], F32)
        self.gatewt = sb("gatewt", [128, 64], F32)
        self.gateb = sb("gateb", [8, 1], F32)
        self.fb1 = sb("fb1", [128, NEXP * 16], F32)
        self.fb1s = sb("fb1s", [128, NEXP * 8], F32)
        self.fb1p = sb("fb1p", [128, NEXP * 8], F32)
        self.fb2_bf = sb("fb2_bf", [8, 1024], BF16)
        self.ident = sb("ident", [128, 128], F32)
        self.ident_r = sb("ident_r", [128, 128], F32)
        self.onesc_r = sb("onesc_r", [128, 1], F32)
        self.w2g_bf = sb("w2g_bf", [128, 32], BF16)
        self.gatewt_r = sb("gatewt_r", [128, 64], F32)
        self.lnsc = sb("lnsc", [128, 256], F32)
        self.ident_bf = sb("ident_bf", [128, 128], BF16)
        self.ones_col = sb("ones_col", [128, 1], F32)
        self.m_stril = sb("m_stril", [128, 128], BF16)
        self.m_striu = sb("m_striu", [128, 128], F32)
        self.floor_row = sb("floor_row", [128, 4], F32)
        self.sel8 = sb("sel8", [8, 8 * 128], BF16)
        self.rms_bc = sb("rms_bc", [128, 256], F32)
        self.sigc = sb("sigc", [128, 4], F32)
        self.invt = sb("invt", [128, 1], F32)

        self.x_tm = sb("x_tm", [128, 512], F32)
        self.xT_pair = sb("xT_pair", [128, NCH * 256], F32)
        self.xT_bf = sb("xT_bf", [128, NCH * ML], BF16)
        self.pre = sb("pre", [128, 2 * CL], BF16)
        self.tails = [sb(f"tail{i}", [128, 3 * NCH], BF16) for i in range(3)]
        self.qT = sb("qT", [128, NCH * ML], BF16)
        self.kT = sb("kT", [128, NCH * ML], BF16)
        self.vT = sb("vT", [128, NCH * VL], BF16)
        self.lsT = sb("lsT", [128, NCH * ML], BF16)
        self.llT = sb("llT", [128, NCH * ML], BF16)
        self.beTa = sb("beTa", [4, ML], F32)
        self.beta_tm = sb("beta_tm", [128, 16], F32)
        self.sq_tm = sb("sq_tm", [128, 16], F32)
        self.sqrow = sb("sqrow", [1, ML], F32)
        self.sqsc = sb("sqsc", [128, ML], F32)
        self.branches = [sb(f"branches{j}", [128, H * 1024], BF16)
                         for j in range(2)]
        self.ktm = sb("ktm", [128, 2 * 256], F32)
        self.S_sb = sb("S_sb", [128, H * 512], F32)
        self.S_bf = sb("S_bf", [128, 2 * 512], BF16)
        self.chain = [[sb(f"chain{sl}_{i}", [128, 384], BF16) for i in range(2)]
                      for sl in range(2)]
        self.attnT = sb("attnT", [128, 2 * 128], F32)
        self.vbk = sb("vbk", [128, 2 * 256], F32)
        self.stat_sc = sb("stat_sc", [128, 2 * 8], F32)
        self.colsc = sb("colsc", [128, 2 * 8], F32)
        self.stats_tm = sb("stats_tm", [128, 2 * 16], F32)
        self.qcsc = sb("qcsc", [128, 8], F32)
        self.statsT = sb("statsT", [16, 2 * H * 128], BF16)
        self.hg = sb("hg", [128, 2 * 512], BF16)
        self.lgrb = sb("lgrb", [4, 256], F32)
        self.fglb = sb("fglb", [8, 256], F32)
        self.probs = sb("probs", [128, H * 4], F32)
        self.o_n = sb("o_n", [128, 1024], F32)
        self.onT = sb("onT", [128, NCH * 256], BF16)
        self.hrot = sb("hrot", [128, 2 * 256], F32)
        self.h_bf = sb("h_bf", [128, NCH * ML], BF16)
        self.acc = sb("acc", [128, NCH * ML], F32)
        self.p8 = sb("p8", [128, 4 * NEXP], F32)
        self.p8T_bf = sb("p8T_bf", [8, ML], BF16)
        self.p_be = sb("p_be", [128, ML], F32)
        self.act_bf = sb("act_bf", [128, NCH * ML], BF16)
        self.gl_sb = sb("gl_sb", [128, 4 * 8], F32)
        self.row1 = sb("row1", [1, 5 * 256], F32)
        self.scx = sb("scx", [128, ML], F32)
        self.scy = sb("scy", [128, ML], F32)
        self.scz = sb("scz", [128, 128], F32)

        self.r_w = sb("r_w", [128, 2 * 1024], BF16)
        self.r_w1a = sb("r_w1a", [128, 2 * 1024], BF16)
        self.r_wo = sb("r_wo", [128, 2 * 1024], BF16)
        self.r_conv = sb("r_conv", [128, 12 * 128], BF16)
        self.r_fir = sb("r_fir", [128, 2 * 12 * 128], BF16)
        self.r_f1 = sb("r_f1", [128, 3 * 1024], BF16)
        self.r_fw2 = sb("r_fw2", [128, 2 * 1024], BF16)

        self.pb = [self.es.enter_context(nc.psum_tensor(f"pb{i}", [128, 512], F32))
                   for i in range(6)]
        self.pbB = [self.es.enter_context(nc.psum_tensor(f"pbB{i}", [128, 1024], BF16))
                    for i in range(2)]

        for i, v in enumerate(CONST_VALUES):
            nc.const_aps.register(F32, v, self.consts[:, i:i + 1])

        self._startup()
        for sq in range(NS):
            self._seq(sq)

        sems = {e: self.es.enter_context(nc.semaphore(f"sem_{e}"))
                for e in SEM_NAMES}
        s.finalize(sems)
        maxv = {}
        for d in set(s.final_deps):
            dr = s.ops[d]
            if maxv.get(dr.sem, -1) < dr.wv:
                maxv[dr.sem] = dr.wv

        def final_waits(proxy):
            for e in sorted(maxv):
                proxy.wait_ge(sems[e], maxv[e])

        with self.es, nc.Block() as block:
            @block.sync
            def _(p):
                s.emit("D", p, final_extra=final_waits)

            @block.tensor
            def _(p):
                s.emit("T", p)

            @block.vector
            def _(p):
                s.emit("V", p)

            @block.scalar
            def _(p):
                s.emit("S", p)

            @block.gpsimd
            def _(p):
                s.emit("G", p)
        return nc

    def _startup(self):
        s, din, scr = self.s, self.din, self.scr
        V = self.V
        for src, dst, rg in [("wq_s", "wq_b", "sc_wq"), ("wk_s", "wk_b", "sc_wk"),
                             ("wv_s", "wv_b", "sc_wv"), ("w1a_s", "w1a_b", "sc_w1a"),
                             ("wo_s", "wo_b", "sc_wo"), ("conv_diag", "conv_b", "sc_cv"),
                             ("fir_diag", "fir_b", "sc_fir"),
                             ("ffn_w1_s", "fw1_b", "sc_f1"),
                             ("ffn_w2_s", "fw2_b", "sc_f2"),
                             ("w1b", "w1b_c", "sc_w1b")]:
            s.op("G", lambda p, a=din[src], b=scr[dst]: p.dma_start(out=b, in_=a),
                 reads=[], writes=[(rg,)], tag=f"cast_{dst}", is_dma=True)

        L = self.ld
        L(self.consts[:, :], din["consts"], [("c_c",)], "c_c")
        s.op("G", lambda p: p.dma_start(out=self.wb[:, :], in_=din["wb_s"]),
             reads=[], writes=[("c_wb",)], tag="c_wb", is_dma=True)
        L(self.w1b[:, :], scr["w1b_c"], [("c_w1b",)], "c_w1b",
          reads=[("sc_w1b",)])
        L(self.w2g[:, :], din["w2g"], [("c_w2g",)], "c_w2g")
        L(self.b1g[:, :], din["b1g"], [("c_b1g",)], "c_b1g")
        L(self.b2g[:, :], din["b2g"], [("c_b2g",)], "c_b2g")
        L(self.lt[0:1, 0:1], din["logit_temp"], [("lt",)], "c_lt")
        L(self.crl[:, :], din["crl"], [("c_crl",)], "c_crl")
        L(self.rmsrow[:, :], din["rms_row"], [("c_rms",)], "c_rms")
        L(self.normw[:, :], din["norm_w"], [("c_nw",)], "c_nw")
        L(self.gatewt[:, :], din["gate_w_t"], [("c_gw",)], "c_gw")
        L(self.gateb[:, :], din["gate_b"], [("c_gb",)], "c_gb")
        L(self.fb1[:, :], din["ffn_b1"], [("c_fb1",)], "c_fb1")
        s.op("G", lambda p: p.dma_start(out=self.fb2_bf[:, :], in_=din["ffn_b2"]),
             reads=[], writes=[("c_fb2b",)], tag="c_fb2b", is_dma=True)
        L(self.ident[:, :], din["ident"], [("c_ident",)], "c_id")
        L(self.ident_bf[:, :], din["ident_bf"], [("c_identb",)], "c_idb")
        L(self.ones_col[:, :], din["ones_col"], [("c_onesc",)], "c_oc")
        L(self.m_stril[:, :], din["m_stril"], [("c_mtl",)], "c_mtl")
        L(self.m_striu[:, :], din["m_striu"], [("c_mtu",)], "c_mtu")
        L(self.floor_row[:, :], din["floor_row"], [("c_floor",)], "c_fl")
        L(self.sel8[:, :], din["sel8"], [("c_sel8",)], "c_s8")
        S_ = self.S_
        S_(lambda p: p.copy(self.ident_r[:, :].bitcast(F32R), self.ident[:, :]),
           [("c_ident",)], [("c_identr",)], "identr")
        S_(lambda p: p.copy(self.onesc_r[:, :].bitcast(F32R), self.ones_col[:, :]),
           [("c_onesc",)], [("c_onescr",)], "onescr")
        S_(lambda p: p.copy(self.w2g_bf[:, :], self.w2g[:, :]),
           [("c_w2g",)], [("c_w2gb",)], "w2gb")
        S_(lambda p: p.copy(self.gatewt_r[:, :].bitcast(F32R), self.gatewt[:, :]),
           [("c_gw",)], [("c_gwr",)], "gwr")

        for e in range(NEXP):
            V(lambda p, e=e: p.tensor_scalar(
                self.fb1s[:, e * 8:(e + 1) * 8], self.fb1[:, e * 16:e * 16 + 8],
                ALPHA, None, Op.mult),
              [("c_fb1",), ("c_c",)], [("c_fb1s",)], f"b1s{e}")
            V(lambda p, e=e: p.tensor_scalar(
                self.fb1p[:, e * 8:(e + 1) * 8], self.fb1[:, e * 16 + 8:e * 16 + 16],
                1.0, None, Op.add),
              [("c_fb1",), ("c_c",)], [("c_fb1p",)], f"b1p{e}")
        b_ = self.nbank()
        self.bc_row(b_, slice(0, 256), self.rmsrow[0:1, :], [("c_rms",)], "rmsbc")
        V(lambda p, b=b_: p.tensor_copy(self.rms_bc[:, :], self.pb[b][:, 0:256]),
          [("pb", b_)], [("c_rmsbc",)], "rmsbc_e")
        self.free_bank(b_)
        S_(lambda p: p.activation(self.crl[0:1, :].bitcast(F32R), self.crl[0:1, :], AF.Sigmoid),
           [("c_crl",)], [("c_crl",)], "crlsig")
        b_ = self.nbank()
        self.bc_row(b_, slice(0, 4), self.crl[0:1, :], [("c_crl",)], "crlbc")
        V(lambda p, b=b_: p.tensor_copy(self.sigc[:, :], self.pb[b][:, 0:4]),
          [("pb", b_)], [("c_sigc",)], "sigc_e")
        self.free_bank(b_)
        lt = self.lt
        S_(lambda p: p.activation(lt[0:1, 1:2], lt[0:1, 0:1], AF.Abs),
           [("lt",)], [("lt",)], "lt1")
        S_(lambda p: p.activation(lt[0:1, 2:3], lt[0:1, 1:2], AF.Exp, scale=-1.0),
           [("lt",), ("c_c",)], [("lt",)], "lt2")
        S_(lambda p: p.activation(lt[0:1, 3:4], lt[0:1, 2:3], AF.Ln, bias=1.0),
           [("lt",), ("c_c",)], [("lt",)], "lt3")
        V(lambda p: p.tensor_scalar(lt[0:1, 4:5], lt[0:1, 0:1], 0.0, None, Op.max),
          [("lt",), ("c_c",)], [("lt",)], "lt4")
        V(lambda p: p.tensor_tensor(lt[0:1, 5:6], lt[0:1, 3:4], lt[0:1, 4:5], Op.add),
          [("lt",)], [("lt",)], "lt5")
        V(lambda p: p.tensor_scalar(lt[0:1, 6:7], lt[0:1, 5:6], 1e-6, None, Op.add),
          [("lt",), ("c_c",)], [("lt",)], "lt6")
        V(lambda p: p.reciprocal(lt[0:1, 7:8], lt[0:1, 6:7]),
          [("lt",)], [("lt",)], "lt7")
        S_(lambda p: p.copy(lt[0:1, 7:8].bitcast(F32R), lt[0:1, 7:8]),
           [("lt",)], [("lt",)], "lt8")
        b_ = self.nbank()
        self.bc_row(b_, slice(0, 1), lt[0:1, 7:8], [("lt",)], "invtbc")
        V(lambda p, b=b_: p.tensor_copy(self.invt[:, :], self.pb[b][:, 0:1]),
          [("pb", b_)], [("c_invt",)], "invt_e")
        self.free_bank(b_)

    def _seq(self, sq):
        V = self.V
        rS_all = [("S_sb", h) for h in range(H)]
        V(lambda p: p.memset(self.S_sb[:, :], 0.0), [], rS_all, f"q{sq}S0")
        self.S_(lambda p: p.copy(self.S_sb[:, :].bitcast(F32R), self.S_sb[:, :]),
                rS_all, rS_all, f"q{sq}S0r")
        for i in range(3):
            V(lambda p, i=i: p.memset(self.tails[i][:, :], 0.0),
              [], [("tail", i)], f"q{sq}t{i}")
        V(lambda p: p.memset(self.vT[:, :], 0.0), [], [("vT",)], f"q{sq}v0")
        for m in range(self.nmac):
            self._ph1(sq, m)
            self._ph2(sq, m)
            self._ph4a(sq, m)
            for pair in range(2):
                self._ph3_pair(sq, m, pair)
            self._ph7(sq, m)
            self._ph8(sq, m)

    def _ph1(self, sq, m):
        V, S_ = self.V, self.S_
        t0 = m * ML
        for dc in range(NCH):
            xa = self.x_tm[:, :]
            self.ld(xa, self.x_in[sq, dc, :, t0:t0 + ML],
                    [("x_tm",)], f"m{m}x{dc}")
            S_(lambda p, dc=dc, xa=xa: p.copy(
                self.xT_bf[:, dc * ML:(dc + 1) * ML], xa),
               [("x_tm",)], [("xTb", dc)], f"m{m}xTb{dc}")
        for dc in range(NCH):
            cpiece = self.r_conv[:, :]
            self.ld(cpiece, self.scr["conv_b"][dc], [("rconv",)],
                    f"m{m}cd{dc}", reads=[("sc_cv",)])
            for pi in range(3):
                par = (dc * 3 + pi) % 2
                scrn = ["wq_b", "wk_b", "wv_b"][pi]
                piece = self.r_w[:, par * 1024:(par + 1) * 1024]
                self.ld(piece, self.scr[scrn][dc], [("rw", par)],
                        f"m{m}w{pi}l{dc}", reads=[("sc_" + ["wq", "wk", "wv"][pi],)])
                bank = self.nbank()
                for kc in range(NCH):
                    self.MMb(bank, slice(0, ML), piece[:, kc * 128:(kc + 1) * 128],
                             self.xT_bf[:, kc * ML:(kc + 1) * ML],
                             kc == 0, kc == NCH - 1,
                             [("rw", par), ("xTb", kc)], f"m{m}p{pi}_{dc}_{kc}")
                prebuf = self.pre[:, (dc % 2) * CL:((dc % 2) + 1) * CL]
                tail = self.tails[pi]
                V(lambda p, pb_=prebuf, b=bank: p.tensor_copy(
                    pb_[:, 3:], self.pb[b][:, 0:ML]),
                    [("pb", bank)], [("pre", dc % 2)], f"m{m}pe{pi}_{dc}")
                self.free_bank(bank)
                V(lambda p, pb_=prebuf, t=tail, dc=dc: p.tensor_copy(
                    pb_[:, 0:3], t[:, dc * 3:dc * 3 + 3]),
                    [("tail", pi)], [("pre", dc % 2)], f"m{m}ph{pi}_{dc}")
                V(lambda p, pb_=prebuf, t=tail, dc=dc: p.tensor_copy(
                    t[:, dc * 3:dc * 3 + 3], pb_[:, ML:ML + 3]),
                    [("pre", dc % 2)], [("tail", pi)], f"m{m}pt{pi}_{dc}")
                bank2 = self.nbank()
                for tap in range(CONV_K):
                    ti = pi * 4 + tap
                    self.MMb(bank2, slice(0, ML),
                             cpiece[:, ti * 128:(ti + 1) * 128],
                             prebuf[:, tap:tap + ML], tap == 0, tap == CONV_K - 1,
                             [("rconv",), ("pre", dc % 2)],
                             f"m{m}c{pi}_{dc}_{tap}")
                if pi == 2:
                    dst = self.vT[:, dc * VL + FIR_L - 1: dc * VL + FIR_L - 1 + ML]
                    S_(lambda p, d=dst, b=bank2: p.activation(
                        d, self.pb[b][:, 0:ML], AF.Silu),
                       [("pb", bank2)], [("vT", dc)], f"m{m}sv{dc}")
                else:
                    outT = self.qT if pi == 0 else self.kT
                    rg = "qT" if pi == 0 else "kT"
                    dst = outT[:, dc * ML:(dc + 1) * ML]
                    S_(lambda p, d=dst, b=bank2: p.activation(
                        d, self.pb[b][:, 0:ML], AF.Silu),
                       [("pb", bank2)], [(rg, dc)], f"m{m}s{pi}_{dc}")
                self.free_bank(bank2)
            fg = self._fir_gen(m, dc)
            if dc % 2 == 1:
                gens = [fg, self._norm_gen(m, dc // 2)]
                while gens:
                    g = gens.pop(0)
                    try:
                        next(g)
                        gens.append(g)
                    except StopIteration:
                        pass
            else:
                for _ in fg:
                    pass

        bb = self.nbank()
        for kc in range(NCH):
            self.s.op("T", lambda p, kc=kc, b=bb: p.matmul(
                self.pb[b][0:4, 0:ML],
                self.wb[:, kc * 4:(kc + 1) * 4],
                self.xT_bf[:, kc * ML:(kc + 1) * ML],
                start=(kc == 0), stop=(kc == NCH - 1)),
                reads=[("c_wb",), ("xTb", kc)], writes=[("pb", bb)],
                tag=f"m{m}be{kc}")
        S_(lambda p, b=bb: p.activation(self.beTa[0:4, :].bitcast(F32R), self.pb[b][0:4, 0:ML],
                                        AF.Sigmoid),
           [("pb", bb)], [("beTa",)], f"m{m}bsig")
        self.free_bank(bb)
        for st in range(4):
            bank = self.nbank()
            self.TRf(bank, slice(0, 4), self.beTa[0:4, st * 128:(st + 1) * 128],
                     [("beTa",)], f"m{m}beT{st}", pk=4)
            V(lambda p, st=st, b=bank: p.tensor_copy(
                self.beta_tm[:, st * 4:(st + 1) * 4], self.pb[b][:, 0:4]),
                [("pb", bank)], [("beta_tm",)], f"m{m}beTe{st}")
            self.free_bank(bank)

        if m == 0 and sq == 0:
            self.tap("qT", self.qT[:, :], [128, NCH * ML],
                     [("qT", c) for c in range(NCH)], BF16)
            self.tap("kT", self.kT[:, :], [128, NCH * ML],
                     [("kT", c) for c in range(NCH)], BF16)
            self.tap("vT", self.vT[:, :], [128, NCH * VL],
                     [("vT", c) for c in range(NCH)], BF16)
            self.tap("beTa", self.beTa[0:4, :], [4, ML], [("beTa",)], F32)

    def _norm_gen(self, m, h):
        V, S_ = self.V, self.S_
        c0, c1 = 2 * h, 2 * h + 1
        if True:
            bank = self.nbank()
            S_(lambda p, c=c0: p.activation(self.sqsc[:, :].bitcast(F32R),
                                            self.kT[:, c * ML:(c + 1) * ML], AF.Square),
               [("kT", c0)], [("sqsc", 0), ("sqsc", 1)], f"m{m}ka{h}")
            yield
            self.MMr(bank, slice(0, ML), self.onesc_r[:, 0:1], self.sqsc[:, :],
                     True, False, [("c_onescr",), ("sqsc", 0), ("sqsc", 1)], f"m{m}kma{h}", mrows=1)
            S_(lambda p, c=c1: p.activation(self.sqsc[:, :].bitcast(F32R),
                                            self.kT[:, c * ML:(c + 1) * ML], AF.Square),
               [("kT", c1)], [("sqsc", 0), ("sqsc", 1)], f"m{m}kb{h}")
            self.MMr(bank, slice(0, ML), self.onesc_r[:, 0:1], self.sqsc[:, :],
                     False, True, [("c_onescr",), ("sqsc", 0), ("sqsc", 1)], f"m{m}kmb{h}", mrows=1)
            yield
            S_(lambda p, b=bank: p.activation(self.sqrow[0:1, :], self.pb[b][0:1, 0:ML],
                                              AF.Sqrt, bias=1e-12),
               [("pb", bank), ("c_c",)], [("sqrow",)], f"m{m}ks{h}")
            self.free_bank(bank)
            yield
            V(lambda p: p.reciprocal(self.sqrow[0:1, :], self.sqrow[0:1, :]),
              [("sqrow",)], [("sqrow",)], f"m{m}kr{h}")
            S_(lambda p: p.copy(self.sqrow[0:1, :].bitcast(F32R),
                                self.sqrow[0:1, :]),
               [("sqrow",)], [("sqrow",)], f"m{m}kr2{h}")
            yield
            bank2 = self.nbank()
            self.bc_row(bank2, slice(0, ML), self.sqrow[0:1, :], [("sqrow",)],
                        f"m{m}kbc{h}")
            yield
            for c in (c0, c1):
                V(lambda p, c=c, b=bank2: p.tensor_tensor(
                    self.kT[:, c * ML:(c + 1) * ML], self.kT[:, c * ML:(c + 1) * ML],
                    self.pb[b][:, 0:ML], Op.mult),
                  [("pb", bank2), ("kT", c)], [("kT", c)], f"m{m}kn{h}_{c}")
            self.free_bank(bank2)
            yield
            bank3 = self.nbank()
            S_(lambda p, c=c0: p.activation(self.sqsc[:, :].bitcast(F32R),
                                            self.qT[:, c * ML:(c + 1) * ML], AF.Square),
               [("qT", c0)], [("sqsc", 0), ("sqsc", 1)], f"m{m}qa{h}")
            self.MMr(bank3, slice(0, ML), self.onesc_r[:, 0:1], self.sqsc[:, :],
                     True, False, [("c_onescr",), ("sqsc", 0), ("sqsc", 1)], f"m{m}qma{h}", mrows=1)
            S_(lambda p, c=c1: p.activation(self.sqsc[:, :].bitcast(F32R),
                                            self.qT[:, c * ML:(c + 1) * ML], AF.Square),
               [("qT", c1)], [("sqsc", 0), ("sqsc", 1)], f"m{m}qb{h}")
            self.MMr(bank3, slice(0, ML), self.onesc_r[:, 0:1], self.sqsc[:, :],
                     False, True, [("c_onescr",), ("sqsc", 0), ("sqsc", 1)], f"m{m}qmb{h}", mrows=1)
            yield
            V(lambda p, b=bank3: p.tensor_copy(self.sqrow[0:1, :].bitcast(F32R), self.pb[b][0:1, 0:ML]),
              [("pb", bank3)], [("sqrow",)], f"m{m}qe{h}")
            self.free_bank(bank3)
            yield
            bank4 = self.nbank()
            for st in range(4):
                self.TRf(bank4, slice(st, st + 1),
                         self.sqrow[0:1, st * 128:(st + 1) * 128],
                         [("sqrow",)], f"m{m}qt{h}_{st}", pk=1)
            yield
            V(lambda p, b=bank4: p.tensor_copy(self.qcsc[:, 0:4], self.pb[b][:, 0:4]),
              [("pb", bank4)], [("qcsc",)], f"m{m}qc{h}")
            self.free_bank(bank4)
            S_(lambda p: p.activation(self.qcsc[:, 4:8], self.qcsc[:, 0:4],
                                      AF.Sqrt, bias=1e-12),
               [("qcsc",), ("c_c",)], [("qcsc",)], f"m{m}qs{h}")
            V(lambda p, h=h: p.reciprocal(self.sq_tm[:, 4 * h:4 * h + 4],
                                          self.qcsc[:, 4:8]),
              [("qcsc",)], [("sq_tm",)], f"m{m}qr{h}")


    def _fir_gen(self, m, dc):
        V = self.V
        sets = [(0, 12), (12, 12), (24, 12), (36, 11), (47, 11), (58, 11)]
        if True:
            bls = self.nbank()
            bll = self.nbank()
            for si, (toff, tcnt) in enumerate(sets):
                par = (dc * 6 + si) % 2
                ring = self.r_fir[:, par * 12 * 128:(par * 12 + 12) * 128]
                self.ld(ring[:, 0:tcnt * 128],
                        self.scr["fir_b"][dc][:, toff * 128:(toff + tcnt) * 128],
                        [("rfir", par)], f"m{m}f{dc}_{si}", reads=[("sc_fir",)])
                for tj in range(tcnt):
                    t = toff + tj
                    if t < FIR_S:
                        tap = t
                        off = (FIR_L - 1) - (FIR_S - 1) + tap
                        self.MMb(bls, slice(0, ML), ring[:, tj * 128:(tj + 1) * 128],
                                 self.vT[:, dc * VL + off: dc * VL + off + ML],
                                 tap == 0, tap == FIR_S - 1,
                                 [("rfir", par), ("vT", dc)], f"m{m}fs{dc}_{t}")
                    else:
                        j = t - FIR_S
                        off = j
                        self.MMb(bll, slice(0, ML), ring[:, tj * 128:(tj + 1) * 128],
                                 self.vT[:, dc * VL + off: dc * VL + off + ML],
                                 j == 0, j == FIR_L - 1,
                                 [("rfir", par), ("vT", dc)], f"m{m}fl{dc}_{j}")
                yield
            V(lambda p, d=dc, b=bls: p.tensor_copy(
                self.lsT[:, d * ML:(d + 1) * ML], self.pb[b][:, 0:ML]),
              [("pb", bls)], [("lsT", dc)], f"m{m}fse{dc}")
            self.free_bank(bls)
            V(lambda p, d=dc, b=bll: p.tensor_copy(
                self.llT[:, d * ML:(d + 1) * ML], self.pb[b][:, 0:ML]),
              [("pb", bll)], [("llT", dc)], f"m{m}fle{dc}")
            self.free_bank(bll)

    def _ph2(self, sq, m):
        V = self.V
        if m == 0 and sq == 0:
            self.tap("lsT", self.lsT[:, :], [128, NCH * ML],
                     [("lsT", c) for c in range(NCH)], BF16)
            self.tap("llT", self.llT[:, :], [128, NCH * ML],
                     [("llT", c) for c in range(NCH)], BF16)
        for dc in range(NCH):
            V(lambda p, d=dc: p.tensor_copy(
                self.vT[:, d * VL: d * VL + FIR_L - 1],
                self.vT[:, d * VL + ML: d * VL + ML + FIR_L - 1]),
              [("vT", dc)], [("vT", dc)], f"m{m}vr{dc}")

    def _ph4a(self, sq, m):
        V = self.V
        for fc in range(NCH):
            par = fc % 2
            piece = self.r_w1a[:, par * 1024:(par + 1) * 1024]
            self.ld(piece, self.scr["w1a_b"][fc], [("rw1a", par)],
                    f"m{m}gw{fc}", reads=[("sc_w1a",)])
            ba = self.nbank()
            for kc in range(NCH):
                self.MMb(ba, slice(0, ML), piece[:, kc * 128:(kc + 1) * 128],
                         self.xT_bf[:, kc * ML:(kc + 1) * ML],
                         kc == 0, kc == NCH - 1,
                         [("rw1a", par), ("xTb", kc)], f"m{m}gA{fc}{kc}")
            V(lambda p, fc=fc, b=ba: p.tensor_copy(
                self.act_bf[:, fc * ML:(fc + 1) * ML], self.pb[b][:, 0:ML]),
              [("pb", ba)], [("act", fc)], f"m{m}gE{fc}")
            self.free_bank(ba)

    def _ph3_pair(self, sq, m, pair):
        pending = [(sub, self._head_gen(m, pair * 2 + sub, h))
                   for sub in range(2) for h in range(H)]
        ph6 = self._ph6_gen(sq, m, pair)
        ph6_in = False
        active = []
        holders = 0
        done_cnt = [0, 0]
        epi_added = [False, False]
        while pending or active:
            while pending and holders < 2 and len(active) < 5:
                sub, g = pending.pop(0)
                active.append([g, True, sub])
                holders += 1
            if not pending and not ph6_in:
                active.append([ph6, False, None])
                ph6_in = True
            rec = active.pop(0)
            try:
                tok = next(rec[0])
                if tok == "free" and rec[1]:
                    rec[1] = False
                    holders -= 1
                if tok == "hold":
                    continue
                active.append(rec)
            except StopIteration:
                if rec[1]:
                    holders -= 1
                sub = rec[2]
                if sub is not None:
                    done_cnt[sub] += 1
                    if done_cnt[sub] == H and not epi_added[sub]:
                        active.append(
                            [self._sub_epi(m, pair * 2 + sub), False, None])
                        epi_added[sub] = True
        for _ in ph6:
            pass
        if m == 0 and sq == 0 and pair == 0:
            self.tap("branches0", self.branches[0][:, :], [128, H * 1024],
                     [("br", h) for h in range(H)], BF16)
            self.tap("statsT", self.statsT[0:16, :], [16, H * 128],
                     [("statsT", h) for h in range(H)], BF16)
            self.tap("probs0", self.probs[:, :], [128, H * 4],
                     [("probs", h2) for h2 in range(H)], F32)
            self.tap("on0", self.o_n[:, :], [128, 1024],
                     [("o_n", h2) for h2 in range(H)], F32)

    def _head_gen(self, m, st, h):
        V, S_ = self.V, self.S_
        sub = st % 2
        branches = self.branches[sub]
        rbr = ("br", sub, h)
        sl = h % 2
        c0, c1 = 2 * h, 2 * h + 1
        base = h * 1024
        qTs = [self.qT[:, c * ML + st * 128: c * ML + (st + 1) * 128] for c in (c0, c1)]
        kTs = [self.kT[:, c * ML + st * 128: c * ML + (st + 1) * 128] for c in (c0, c1)]
        bcol = self.beta_tm[:, st * 4 + h: st * 4 + h + 1]
        sqcol = self.sq_tm[:, 4 * h + st: 4 * h + st + 1]
        Ssl = self.S_sb[:, h * 512:(h + 1) * 512]
        rS = ("S_sb", h)
        ktm = self.ktm[:, sl * 256:(sl + 1) * 256]
        vbk = self.vbk[:, sl * 256:(sl + 1) * 256]
        ui = vbk
        attnT = self.attnT[:, sl * 128:(sl + 1) * 128]
        S_bf = self.S_bf[:, sl * 512:(sl + 1) * 512]
        pbB = self.pbB[sl]
        rk, rvb = ("ktm", sl), ("vbk", sl)
        ru = rvb
        rat, rsb, rpB = ("attnT", sl), ("S_bf", sl), ("pbB", sl)
        ch = self.chain[sl]
        rch = lambda i, sl=sl: ("ch", sl, i)

        for i, c in enumerate((c0, c1)):
            self.TRb(sl, slice(i * 128, (i + 1) * 128),
                     self.vT[:, c * VL + FIR_L - 1 + st * 128:
                             c * VL + FIR_L - 1 + (st + 1) * 128],
                     [("vT", c)], f"m{m}vt{st}{h}{i}")
        yield
        dstv = branches[:, base + 768: base + 1024]
        V(lambda p, d=dstv: p.tensor_copy(d, pbB[:, 0:256]),
          [rpB], [rbr], f"m{m}vb{st}{h}")
        yield
        for i in range(2):
            self.TRb(sl, slice(i * 128, (i + 1) * 128), kTs[i],
                     [("kT", 2 * h + i)], f"m{m}kt{st}{h}{i}")
        yield
        V(lambda p, d=ktm: p.tensor_copy(d.bitcast(F32R), pbB[:, 0:256]),
          [rpB], [rk], f"m{m}ke{st}{h}")
        yield
        for bi, (srcT, rgn) in enumerate([(self.lsT, "lsT"), (self.llT, "llT")]):
            for i, c in enumerate((c0, c1)):
                self.TRb(sl, slice((bi * 2 + i) * 128, (bi * 2 + i + 1) * 128),
                         srcT[:, c * ML + st * 128: c * ML + (st + 1) * 128],
                         [(rgn, c)], f"m{m}bt{st}{h}{bi}{i}")
        yield
        dst = branches[:, base: base + 512]
        V(lambda p, d=dst: p.tensor_copy(d, pbB[:, 0:512]),
          [rpB], [rbr], f"m{m}be{st}{h}")
        bg = self.nbank()
        for i in range(2):
            self.MMb(bg, slice(0, 128), kTs[i], kTs[i], i == 0, i == 1,
                     [("kT", 2 * h + i)], f"m{m}G{st}{h}{i}")
        for i in range(2):
            self.MMb(bg, slice(128, 256), kTs[i], qTs[i], i == 0, i == 1,
                     [("kT", 2 * h + i), ("qT", 2 * h + i)], f"m{m}A{st}{h}{i}")
        yield
        V(lambda p, b=bg: p.scalar_tensor_tensor(
            ch[0][:, 0:128], self.pb[b][:, 0:128], bcol,
            self.m_stril[:, :], Op.mult, Op.mult),
          [("pb", bg), ("beta_tm",), ("c_mtl",)], [rch(0)], f"m{m}Am{st}{h}")
        V(lambda p, b=bg, d=attnT: p.tensor_tensor(
            d.bitcast(F32R), self.pb[b][:, 128:256],
            self.m_striu[:, :], Op.mult),
          [("pb", bg), ("c_mtu",)], [rat], f"m{m}At{st}{h}")
        self.free_bank(bg)
        V(lambda p, d=S_bf, S=Ssl: p.tensor_copy(d, S),
          [rS], [rsb], f"m{m}Sb{st}{h}")
        yield
        self.TRb(sl, slice(512, 640), ch[0][:, 0:128], [rch(0)], f"m{m}Y0{st}{h}")
        yield
        V(lambda p: p.tensor_copy(ch[0][:, 128:256], pbB[:, 512:640]),
          [rpB], [rch(0)], f"m{m}Ye{st}{h}")
        V(lambda p: p.scalar_tensor_tensor(
            ch[0][:, 256:384], ch[0][:, 128:256], -1.0,
            self.ident[:, :], Op.mult, Op.add),
          [rch(0), ("c_ident",), ("c_c",)], [rch(0)], f"m{m}M0{st}{h}")
        yield
        par = 0
        for lvl in range(1, 7):
            nxt = par ^ 1
            bc = self.nbank()
            self.MMb(bc, slice(0, 128), ch[par][:, 128:256], ch[par][:, 0:128],
                     True, True, [rch(par)], f"m{m}c1_{st}{h}{lvl}")
            self.MMb(bc, slice(128, 256), ch[par][:, 0:128], ch[par][:, 128:256],
                     True, True, [rch(par)], f"m{m}c2_{st}{h}{lvl}")
            yield
            V(lambda p, b=bc, n=nxt: p.tensor_copy(ch[n][:, 0:256],
                                                   self.pb[b][:, 0:256]),
              [("pb", bc)], [rch(nxt)], f"m{m}ce{st}{h}{lvl}")
            self.MMb(bc, slice(256, 384), ch[nxt][:, 0:128], ch[par][:, 256:384],
                     True, True, [rch(nxt), rch(par)], f"m{m}c3_{st}{h}{lvl}")
            yield
            V(lambda p, b=bc, n=nxt, o=par: p.tensor_tensor(
                ch[n][:, 256:384], self.pb[b][:, 256:384],
                ch[o][:, 256:384], Op.add),
              [("pb", bc), rch(par)], [rch(nxt)], f"m{m}cm{st}{h}{lvl}")
            self.free_bank(bc)
            par = nxt
            yield
        Tt = ch[par][:, 256:384]
        for i in range(2):
            V(lambda p, i=i: p.tensor_copy(
                vbk[:, i * 128:(i + 1) * 128].bitcast(F32R), kTs[i]),
              [("kT", 2 * h + i)], [rvb], f"m{m}ku{st}{h}{i}")
        yield
        bu = self.nbank()
        for i in range(2):
            self.MMr(bu, slice(0, 256), vbk[:, i * 128:(i + 1) * 128],
                     Ssl[:, i * 256:(i + 1) * 256], i == 0, i == 1,
                     [rvb, rS], f"m{m}kS{st}{h}{i}")
        bo = self.nbank()
        for i in range(2):
            self.MMb(bo, slice(0, 256), qTs[i], S_bf[:, i * 256:(i + 1) * 256],
                     i == 0, False, [("qT", 2 * h + i), rsb], f"m{m}oS{st}{h}{i}")
        yield
        dstvb = branches[:, base + 512: base + 768]
        V(lambda p, b=bu, d=vbk, s=dstv: p.tensor_tensor(
            d.bitcast(F32R), s, self.pb[b][:, 0:256], Op.subtract),
          [("pb", bu), rbr], [rvb], f"m{m}vk0{st}{h}")
        V(lambda p, d=dstvb, s=vbk: p.tensor_scalar(
            d, s, bcol, None, Op.mult),
          [rvb, ("beta_tm",)], [rbr], f"m{m}vk{st}{h}")
        yield
        self.MMb(bu, slice(256, 512), Tt, dstvb, True, True,
                 [rch(par), rbr], f"m{m}u{st}{h}")
        yield
        V(lambda p, b=bu, d=ui: p.tensor_copy(d.bitcast(F32R),
                                              self.pb[b][:, 256:512]),
          [("pb", bu)], [ru], f"m{m}ue{st}{h}")
        self.free_bank(bu)
        yield
        self.MMr(bo, slice(0, 256), attnT, ui, False, True,
                 [rat, ru], f"m{m}oA{st}{h}")
        yield
        dsto = branches[:, base + 512: base + 768]
        V(lambda p, d=dsto, b=bo: p.tensor_scalar(
            d, self.pb[b][:, 0:256], sqcol, None, Op.mult),
          [("pb", bo), ("sq_tm",)], [rbr], f"m{m}oe{st}{h}")
        self.free_bank(bo)
        yield
        bs = self.nbank()
        for i in range(2):
            self.MMr(bs, slice(i * 256, (i + 1) * 256), self.ident_r[:, :],
                     Ssl[:, i * 256:(i + 1) * 256], True, False,
                     [("c_identr",), rS], f"m{m}Sr{st}{h}{i}")
            self.MMr(bs, slice(i * 256, (i + 1) * 256),
                     ktm[:, i * 128:(i + 1) * 128], ui,
                     False, True, [rk, ru], f"m{m}Su{st}{h}{i}")
        yield
        V(lambda p, b=bs, S=Ssl: p.tensor_copy(S.bitcast(F32R),
                                               self.pb[b][:, 0:512]),
          [("pb", bs)], [rS], f"m{m}Sc{st}{h}")
        self.free_bank(bs)
        yield
        br = branches[:, base: base + 1024]
        s8, s16 = sl * 8, sl * 16
        stv = self.stats_tm
        rst, rcs, rtm = ("stat", sl), ("colsc", sl), ("stats_tm", sl)
        V(lambda p, br=br: p.tensor_reduce(
            self.stat_sc[:, s8:s8 + 4], br.rearrange("p (b n) -> p b n", b=4),
            AXX, Op.add),
          [rbr], [rst], f"m{m}ts{st}{h}")
        V(lambda p, br=br: p.tensor_reduce(
            self.stat_sc[:, s8 + 4:s8 + 8], br.rearrange("p (b n) -> p b n", b=4),
            AXX, Op.add, apply_absolute_value=True),
          [rbr], [rst], f"m{m}ta{st}{h}")
        for b4 in range(4):
            S_(lambda p, br=br, b4=b4: p.activation(
                self.sqsc[:, 0:256].bitcast(F32R), br[:, b4 * 256:(b4 + 1) * 256],
                AF.Square, accum_out=self.colsc[:, s8 + b4:s8 + b4 + 1]),
               [rbr], [("sqsc", 0), rcs], f"m{m}tq{st}{h}{b4}")
        yield
        stv16 = stv[:, s16:s16 + 16].rearrange("p (b j) -> p b j", j=4)
        jm, jv = stv16[:, :, 0:1], stv16[:, :, 1:2]
        ja, jl = stv16[:, :, 2:3], stv16[:, :, 3:4]
        sum4 = self.stat_sc[:, s8:s8 + 4].unsqueeze(2)
        abs4 = self.stat_sc[:, s8 + 4:s8 + 8].unsqueeze(2)
        sq4 = self.colsc[:, s8:s8 + 4].unsqueeze(2)
        m2t = self.colsc[:, s8 + 4:s8 + 8].unsqueeze(2)
        V(lambda p: p.tensor_scalar(jm, sum4, 1.0 / 256.0, None, Op.mult),
          [rst, ("c_c",)], [rtm], f"m{m}tm{st}{h}")
        V(lambda p: p.tensor_tensor(m2t, jm, jm, Op.mult),
          [rtm], [rcs], f"m{m}t2{st}{h}")
        V(lambda p: p.scalar_tensor_tensor(
            jv, sq4, 1.0 / 256.0, m2t, Op.mult, Op.subtract),
          [rcs, ("c_c",)], [rtm], f"m{m}tv{st}{h}")
        V(lambda p: p.tensor_scalar(ja, abs4, 1.0 / 256.0, None, Op.mult),
          [rst, ("c_c",)], [rtm], f"m{m}tb{st}{h}")
        S_(lambda p: p.activation(jl, sq4, AF.Sqrt),
           [rcs], [rtm], f"m{m}tl{st}{h}")
        yield
        bt = self.nbank()
        self.TRf(bt, slice(0, 128), stv[:, s16:s16 + 16], [rtm],
                 f"m{m}tt{st}{h}", rows=16)
        yield
        V(lambda p, b=bt, h=h, sub=sub: p.tensor_copy(
            self.statsT[0:16, sub * 512 + h * 128: sub * 512 + (h + 1) * 128],
            self.pb[b][0:16, 0:128]),
          [("pb", bt)], [("statsT", sub, h)], f"m{m}te{st}{h}")
        self.free_bank(bt)
        yield "free"

    def _sub_epi(self, m, st):
        V, S_ = self.V, self.S_
        P0 = st * 128
        sub = st % 2
        branches = self.branches[sub]
        rstT = [("statsT", sub, h) for h in range(H)]
        blg = self.nbank()
        for fc in range(NCH):
            hb = self.nbank()
            self.s.op("T", lambda p, fc=fc, b=hb: p.matmul(
                self.pb[b][:, 0:512],
                self.w1b[0:16, fc * 128:(fc + 1) * 128],
                self.statsT[0:16, sub * 512: sub * 512 + 512],
                start=True, stop=True),
                reads=[("c_w1b",)] + rstT, writes=[("pb", hb)],
                tag=f"m{m}gB{st}{fc}")
            yield
            hgs = self.hg[:, (fc % 2) * 512:(fc % 2 + 1) * 512]
            t1b = self.act_bf[:, fc * ML + P0: fc * ML + P0 + 128]
            t13 = t1b.unsqueeze(1).to_broadcast([128, 4, 128])
            V(lambda p, o=hgs, b=hb, t=t13: p.tensor_tensor(
                o.rearrange("p (h n) -> p h n", h=4),
                self.pb[b][:, 0:512].rearrange("p (h n) -> p h n", h=4),
                t, Op.add),
              [("pb", hb), ("act", fc)], [("hg", fc % 2)], f"m{m}gd{st}{fc}")
            self.free_bank(hb)
            S_(lambda p, o=hgs, fc=fc: p.activation(
                o, o, AF.Gelu, bias=self.b1g[:, fc:fc + 1]),
               [("hg", fc % 2), ("c_b1g",)], [("hg", fc % 2)], f"m{m}gg{st}{fc}")
            yield
            self.s.op("T", lambda p, fc=fc, o=hgs, b=blg: p.matmul(
                self.pb[b][0:4, 0:512],
                self.w2g_bf[:, fc * 4:(fc + 1) * 4],
                o, start=(fc == 0), stop=(fc == NCH - 1)),
                reads=[("c_w2gb",), ("hg", fc % 2)],
                writes=[("pb", blg)], tag=f"m{m}gl{st}{fc}")
            yield
        lgs = self.beTa[0:4, 0:512]
        S_(lambda p, b=blg, d=lgs: p.activation(
            d, self.pb[b][0:4, 0:512], AF.Identity, bias=self.b2g[0:4, 0:1]),
           [("pb", blg), ("c_b2g",)], [("beTa",)], f"m{m}lb{st}")
        self.free_bank(blg)
        yield
        bt2 = self.nbank()
        for h in range(H):
            self.TRf(bt2, slice(h * 4, (h + 1) * 4),
                     lgs[0:4, h * 128:(h + 1) * 128],
                     [("beTa",)], f"m{m}lt{st}{h}", pk=4)
        yield
        c = self.scz
        rcz = ("sczS",)
        V(lambda p, b=bt2: p.tensor_copy(c[:, 0:16], self.pb[b][:, 0:16]),
          [("pb", bt2)], [rcz], f"m{m}le{st}")
        self.free_bank(bt2)
        z3 = c[:, 0:16].rearrange("p (h b) -> p h b", h=4)
        V(lambda p: p.tensor_reduce(c[:, 16:20], z3, AXX, Op.max, negate=True),
          [rcz], [rcz], f"m{m}l1{st}")
        nm3 = c[:, 16:20].unsqueeze(2).to_broadcast([128, 4, 4])
        V(lambda p: p.tensor_tensor(
            c[:, 20:36].rearrange("p (h b) -> p h b", h=4), z3, nm3, Op.add),
          [rcz], [rcz], f"m{m}l2{st}")
        V(lambda p: p.tensor_scalar(c[:, 20:36], c[:, 20:36],
                                    self.invt[:, 0:1], None, Op.mult),
          [rcz, ("c_invt",)], [rcz], f"m{m}l2b{st}")
        yield
        S_(lambda p: p.activation(c[:, 20:36], c[:, 20:36], AF.Exp),
           [rcz], [rcz], f"m{m}l3{st}")
        e3 = c[:, 20:36].rearrange("p (h b) -> p h b", h=4)
        V(lambda p: p.tensor_reduce(c[:, 36:40], e3, AXX, Op.add),
          [rcz], [rcz], f"m{m}l4{st}")
        V(lambda p: p.reciprocal(c[:, 36:40], c[:, 36:40]),
          [rcz], [rcz], f"m{m}l5{st}")
        rc3 = c[:, 36:40].unsqueeze(2).to_broadcast([128, 4, 4])
        V(lambda p: p.scalar_tensor_tensor(
            c[:, 20:36].rearrange("p (h b) -> p h b", h=4), e3,
            1.0 - 2.0 * FLOOR_EPS, rc3, Op.mult, Op.mult),
          [rcz, ("c_c",)], [rcz], f"m{m}l6{st}")
        fl3 = self.floor_row[:, 0:4].unsqueeze(1).to_broadcast([128, 4, 4])
        V(lambda p: p.tensor_tensor(
            self.probs[:, 0:16].rearrange("p (h b) -> p h b", h=4), e3, fl3,
            Op.add),
          [rcz, ("c_floor",)], [("probs", h2) for h2 in range(H)],
          f"m{m}l7{st}")
        yield
        c3 = c[:, 96:128].rearrange("p (h x) -> p h x", h=4)
        V(lambda p: p.tensor_tensor(
            c3[:, :, 0:1], self.probs[:, 0:16].rearrange(
                "p (h b) -> p h b", h=4)[:, :, 0:1],
            self.sigc[:, 0:4].unsqueeze(2), Op.add),
          [("probs", h2) for h2 in range(H)] + [("c_sigc",)], [rcz],
          f"m{m}mc{st}")
        for h in range(H):
            base = h * 1024
            br = branches[:, base: base + 1024]
            pr = self.probs[:, h * 4:h * 4 + 4]
            cb = 96 + h * 8
            don = self.o_n[:, h * 256:(h + 1) * 256]
            V(lambda p, br=br, cb=cb, don=don: p.tensor_scalar(
                don, br[:, 0:256], c[:, cb:cb + 1], None, Op.mult),
              [("br", sub, h), rcz], [("o_n", h)], f"m{m}m0{st}{h}")
            for bi in range(1, 4):
                V(lambda p, br=br, pr=pr, bi=bi, don=don: p.scalar_tensor_tensor(
                    don, br[:, bi * 256:(bi + 1) * 256],
                    pr[:, bi:bi + 1], don, Op.mult, Op.add),
                  [("br", sub, h), ("probs", h), ("o_n", h)], [("o_n", h)],
                  f"m{m}m{bi}{st}{h}")
            yield
        for h in range(H):
            cb = 96 + h * 8
            don = self.o_n[:, h * 256:(h + 1) * 256]
            S_(lambda p, cb=cb, don=don: p.activation(
                self.sqsc[:, (h % 2) * 256:(h % 2) * 256 + 256].bitcast(F32R),
                don, AF.Square, accum_out=c[:, cb + 1:cb + 2]),
               [("o_n", h)], [("sqsc", h % 2), rcz], f"m{m}mq{st}{h}")
        yield
        S_(lambda p: p.activation(c3[:, :, 2:3], c3[:, :, 1:2],
                                  AF.Sqrt, scale=1.0 / 256.0, bias=1e-5),
           [rcz, ("c_c",)], [rcz], f"m{m}ms{st}")
        V(lambda p: p.reciprocal(c3[:, :, 3:4], c3[:, :, 2:3]),
          [rcz], [rcz], f"m{m}mr{st}")
        yield
        for h in range(H):
            cb = 96 + h * 8
            don = self.o_n[:, h * 256:(h + 1) * 256]
            V(lambda p, cb=cb, don=don: p.scalar_tensor_tensor(
                don, don, c[:, cb + 3:cb + 4],
                self.rms_bc[:, :], Op.mult, Op.mult),
              [rcz, ("c_rmsbc",), ("o_n", h)], [("o_n", h)], f"m{m}mo{st}{h}")
            yield
            bo2 = self.nbank()
            for i, cchunk in enumerate((2 * h, 2 * h + 1)):
                self.TRf(bo2, slice(i * 128, (i + 1) * 128),
                         self.o_n[:, cchunk * 128:(cchunk + 1) * 128],
                         [("o_n", h)], f"m{m}ot{st}{cchunk}")
            yield
            for i, cchunk in enumerate((2 * h, 2 * h + 1)):
                dstT = self.onT[:, cchunk * 256 + (st % 2) * 128:
                                cchunk * 256 + (st % 2 + 1) * 128]
                S_(lambda p, d=dstT, b=bo2, i=i: p.copy(
                    d, self.pb[b][:, i * 128:(i + 1) * 128]),
                   [("pb", bo2)], [("onT", cchunk)], f"m{m}oe{st}{cchunk}")
            self.free_bank(bo2)
            yield

    def _ph6_gen(self, sq, m, pair):
        V, S_ = self.V, self.S_
        t0 = m * ML
        P0 = pair * 256
        NT = 256
        for c in range(NCH):
            self.ld(self.xT_pair[:, c * 256:(c + 1) * 256],
                    self.x_in[sq, c, :, t0 + P0: t0 + P0 + 256],
                    [("xTp", c)], f"m{m}rx{pair}{c}")
        yield
        for dc in range(2):
            piece = self.r_wo[:, (dc % 2) * 1024:((dc % 2) + 1) * 1024]
            self.ld(piece, self.scr["wo_b"][dc], [("rwo", dc % 2)],
                    f"m{m}o{pair}w{dc}", reads=[("sc_wo",)])
        yield "hold"
        bsum = self.nbank()
        bsq = self.nbank()
        for dc in range(NCH):
            par = dc % 2
            piece = self.r_wo[:, par * 1024:(par + 1) * 1024]
            if dc >= 2:
                self.ld(piece, self.scr["wo_b"][dc], [("rwo", par)],
                        f"m{m}o{pair}w{dc}", reads=[("sc_wo",)])
            bo = self.nbank()
            for kc in range(NCH):
                self.MMb(bo, slice(0, NT), piece[:, kc * 128:(kc + 1) * 128],
                         self.onT[:, kc * 256: kc * 256 + NT],
                         kc == 0, kc == NCH - 1,
                         [("rwo", par), ("onT", kc)], f"m{m}wo{pair}{dc}{kc}")
            x1 = self.acc[:, dc * ML + P0: dc * ML + P0 + NT]
            V(lambda p, x1=x1, b=bo, dc=dc: p.tensor_tensor(
                x1, self.pb[b][:, 0:NT], self.xT_pair[:, dc * 256:(dc + 1) * 256],
                Op.add),
              [("pb", bo), ("xTp", dc)], [("acc", dc)], f"m{m}x1{pair}{dc}")
            self.free_bank(bo)
            S_(lambda p, x1=x1: p.copy(self.lnsc[:, 0:NT].bitcast(F32R), x1),
               [("acc", dc)], [("lnsc",)], f"m{m}sc{pair}{dc}")
            self.MMr(bsum, slice(0, NT), self.onesc_r[:, 0:1], self.lnsc[:, 0:NT],
                     dc == 0, dc == NCH - 1, [("c_onescr",), ("lnsc",)],
                     f"m{m}s1{pair}{dc}", mrows=1)
            S_(lambda p, x1=x1: p.activation(self.sqsc[:, 0:NT].bitcast(F32R),
                                             x1, AF.Square),
               [("acc", dc)], [("sqsc", 0), ("sqsc", 1)], f"m{m}sq{pair}{dc}")
            self.MMr(bsq, slice(0, NT), self.onesc_r[:, 0:1], self.sqsc[:, 0:NT],
                     dc == 0, dc == NCH - 1, [("c_onescr",), ("sqsc", 0), ("sqsc", 1)],
                     f"m{m}s2{pair}{dc}", mrows=1)
        self._ln_rows(m, pair, bsum, bsq, NT, ffn_r=True)
        self.free_bank(bsum)
        self.free_bank(bsq)
        bs1 = self.nbank()
        bs2 = self.nbank()
        bs3 = self.nbank()
        r = self.row1
        self.bc_row(bs1, slice(0, NT), r[0:1, 768:768 + NT], [("row1",)],
                    f"m{m}bs{pair}")
        self.bc_row(bs2, slice(0, NT), r[0:1, 1024:1024 + NT], [("row1",)],
                    f"m{m}bm{pair}")
        self.bc_row(bs3, slice(0, NT), r[0:1, 256:256 + NT], [("row1",)],
                    f"m{m}br{pair}")
        bg = self.nbank()
        for dc in range(NCH):
            hpar = dc % 2
            x1 = self.acc[:, dc * ML + P0: dc * ML + P0 + NT]
            V(lambda p, x1=x1: p.tensor_tensor(
                self.scx[:, 0:NT], x1, self.pb[bs1][:, 0:NT], Op.mult),
              [("pb", bs1), ("acc", dc)], [("scx", 0)], f"m{m}n1{pair}{dc}")
            V(lambda p, x1=x1: p.tensor_tensor(
                x1, self.scx[:, 0:NT], self.pb[bs2][:, 0:NT], Op.subtract),
              [("pb", bs2), ("scx", 0)], [("acc", dc)], f"m{m}n2{pair}{dc}")
            hsl = self.hrot[:, hpar * 256:(hpar + 1) * 256]
            V(lambda p, x1=x1, o=hsl, dc=dc: p.scalar_tensor_tensor(
                o.bitcast(F32R), x1, self.normw[:, dc:dc + 1],
                self.pb[bs3][:, 0:NT], Op.mult, Op.mult),
              [("pb", bs3), ("acc", dc), ("c_nw",)], [("hrot", hpar)],
              f"m{m}h{pair}{dc}")
            hb = self.h_bf[:, dc * ML + P0: dc * ML + P0 + NT]
            S_(lambda p, o=hb, i=hsl: p.copy(o, i),
               [("hrot", hpar)], [("h_bf", dc)], f"m{m}hb{pair}{dc}")
            self.MMr(bg, slice(0, NT), self.gatewt_r[:, dc * 8:(dc + 1) * 8], hsl,
                     dc == 0, dc == NCH - 1, [("c_gwr",), ("hrot", hpar)],
                     f"m{m}fg{pair}{dc}", mrows=8)
        self.free_bank(bs1)
        self.free_bank(bs2)
        self.free_bank(bs3)
        S_(lambda p, b=bg: p.activation(self.fglb[0:8, 0:NT], self.pb[b][0:8, 0:NT],
                                        AF.Identity, bias=self.gateb[0:8, 0:1]),
           [("pb", bg), ("c_gb",)], [("fglb",)], f"m{m}fb{pair}")
        self.free_bank(bg)
        for sub in range(2):
            st = pair * 2 + sub
            bt = self.nbank()
            self.TRf(bt, slice(0, 8), self.fglb[0:8, sub * 128:(sub + 1) * 128],
                     [("fglb",)], f"m{m}ft{pair}{sub}", pk=8)
            gl = self.gl_sb[:, st * 8:(st + 1) * 8]
            V(lambda p, d=gl, b=bt: p.tensor_copy(d, self.pb[b][:, 0:8]),
              [("pb", bt)], [("gl", st)], f"m{m}fe{pair}{sub}")
            self.free_bank(bt)
            c = self.scz
            mk1 = self.scz[:, 72:80]
            l2 = self.scz[:, 80:88]
            mk2 = self.scz[:, 88:96]
            V(lambda p, gl=gl: p.tensor_reduce(c[:, 64:65], gl, AXX, Op.max),
              [("gl", st)], [("scz6",)], f"m{m}u1{st}")
            V(lambda p, gl=gl: p.tensor_scalar(mk1, gl, c[:, 64:65], None,
                                               Op.is_equal),
              [("gl", st), ("scz6",)], [("scz6",)], f"m{m}u2{st}")
            V(lambda p, gl=gl: p.scalar_tensor_tensor(l2, mk1, -1e30, gl,
                                                      Op.mult, Op.add),
              [("scz6",), ("gl", st), ("c_c",)], [("scz6",)], f"m{m}u3{st}")
            V(lambda p: p.tensor_reduce(c[:, 65:66], l2, AXX, Op.max),
              [("scz6",)], [("scz6",)], f"m{m}u4{st}")
            V(lambda p: p.tensor_scalar(mk2, l2, c[:, 65:66], None, Op.is_equal),
              [("scz6",)], [("scz6",)], f"m{m}u5{st}")
            V(lambda p: p.tensor_tensor(c[:, 66:67], c[:, 65:66], c[:, 64:65],
                                        Op.subtract),
              [("scz6",)], [("scz6",)], f"m{m}u6{st}")
            S_(lambda p: p.activation(c[:, 67:68], c[:, 66:67], AF.Exp),
               [("scz6",)], [("scz6",)], f"m{m}u7{st}")
            V(lambda p: p.tensor_scalar(c[:, 68:69], c[:, 67:68], 1.0, None, Op.add),
              [("scz6",), ("c_c",)], [("scz6",)], f"m{m}u8{st}")
            V(lambda p: p.reciprocal(c[:, 69:70], c[:, 68:69]),
              [("scz6",)], [("scz6",)], f"m{m}u9{st}")
            V(lambda p: p.tensor_tensor(c[:, 70:71], c[:, 67:68], c[:, 69:70],
                                        Op.mult),
              [("scz6",)], [("scz6",)], f"m{m}ua{st}")
            ps = self.p8[:, st * 8:(st + 1) * 8]
            V(lambda p, d=ps: p.tensor_scalar(d, mk1, c[:, 69:70], None, Op.mult),
              [("scz6",)], [("p8", st)], f"m{m}ub{st}")
            V(lambda p, d=ps: p.scalar_tensor_tensor(d, mk2, c[:, 70:71], d,
                                                     Op.mult, Op.add),
              [("scz6",), ("p8", st)], [("p8", st)], f"m{m}uc{st}")
            bt2 = self.nbank()
            self.TRf(bt2, slice(0, 128), ps, [("p8", st)], f"m{m}ud{st}", rows=8)
            V(lambda p, b=bt2, st=st: p.tensor_copy(
                self.p8T_bf[0:8, st * 128:(st + 1) * 128], self.pb[b][0:8, 0:128]),
              [("pb", bt2)], [("p8Tb",)], f"m{m}ue2{st}")
            self.free_bank(bt2)
        if m == 0 and sq == 0 and pair == 1:
            self.tap("acc1", self.acc[:, :], [128, NCH * ML],
                     [("acc", dc2) for dc2 in range(NCH)], F32)
            self.tap("hbf", self.h_bf[:, :], [128, NCH * ML],
                     [("h_bf", dc2) for dc2 in range(NCH)], BF16)
            self.tap("p8t", self.p8T_bf[0:8, :], [8, ML], [("p8Tb",)], BF16)

    def _ln_rows(self, m, pair, bsum, bsq, NT, ffn_r):
        V, S_ = self.V, self.S_
        r = self.row1
        V(lambda p: p.tensor_scalar(r[0:1, 0:NT], self.pb[bsum][0:1, 0:NT],
                                    1.0 / 1024.0, None, Op.mult),
          [("pb", bsum), ("c_c",)], [("row1",)], f"m{m}ln1{pair}")
        V(lambda p: p.tensor_tensor(r[0:1, 256:256 + NT], r[0:1, 0:NT],
                                    r[0:1, 0:NT], Op.mult),
          [("row1",)], [("row1",)], f"m{m}ln2{pair}")
        V(lambda p: p.scalar_tensor_tensor(
            r[0:1, 512:512 + NT], self.pb[bsq][0:1, 0:NT], 1.0 / 1024.0,
            r[0:1, 256:256 + NT], Op.mult, Op.subtract),
          [("pb", bsq), ("row1",), ("c_c",)], [("row1",)], f"m{m}ln3{pair}")
        S_(lambda p: p.activation(r[0:1, 768:768 + NT], r[0:1, 512:512 + NT],
                                  AF.Sqrt, bias=1e-5),
           [("row1",), ("c_c",)], [("row1",)], f"m{m}ln4{pair}")
        V(lambda p: p.reciprocal(r[0:1, 768:768 + NT], r[0:1, 768:768 + NT]),
          [("row1",)], [("row1",)], f"m{m}ln5{pair}")
        S_(lambda p: p.copy(r[0:1, 768:768 + NT].bitcast(F32R),
                            r[0:1, 768:768 + NT]),
           [("row1",)], [("row1",)], f"m{m}ln5r{pair}")
        V(lambda p: p.tensor_tensor(r[0:1, 1024:1024 + NT].bitcast(F32R),
                                    r[0:1, 0:NT], r[0:1, 768:768 + NT], Op.mult),
          [("row1",)], [("row1",)], f"m{m}ln6{pair}")
        if ffn_r:
            V(lambda p: p.tensor_tensor(r[0:1, 256:256 + NT],
                                        r[0:1, 768:768 + NT],
                                        r[0:1, 768:768 + NT], Op.mult),
              [("row1",)], [("row1",)], f"m{m}ln7{pair}")
            V(lambda p: p.tensor_tensor(r[0:1, 256:256 + NT],
                                        r[0:1, 256:256 + NT],
                                        r[0:1, 512:512 + NT], Op.mult),
              [("row1",)], [("row1",)], f"m{m}ln8{pair}")
            S_(lambda p: p.activation(r[0:1, 256:256 + NT],
                                      r[0:1, 256:256 + NT], AF.Sqrt, bias=F32EPS),
               [("row1",), ("c_c",)], [("row1",)], f"m{m}ln9{pair}")
            V(lambda p: p.reciprocal(r[0:1, 256:256 + NT], r[0:1, 256:256 + NT]),
              [("row1",)], [("row1",)], f"m{m}lnA{pair}")
            V(lambda p: p.tensor_scalar(r[0:1, 768:768 + NT].bitcast(F32R),
                                        r[0:1, 768:768 + NT], 2.0, None, Op.mult),
              [("row1",), ("c_c",)], [("row1",)], f"m{m}lnB{pair}")
            V(lambda p: p.tensor_scalar(r[0:1, 1024:1024 + NT].bitcast(F32R),
                                        r[0:1, 1024:1024 + NT], 2.0, None,
                                        Op.mult),
              [("row1",), ("c_c",)], [("row1",)], f"m{m}lnC{pair}")
            V(lambda p: p.tensor_scalar(r[0:1, 256:256 + NT].bitcast(F32R),
                                        r[0:1, 256:256 + NT], 0.5, None, Op.mult),
              [("row1",), ("c_c",)], [("row1",)], f"m{m}lnD{pair}")

    def _ph7(self, sq, m):
        V, S_ = self.V, self.S_
        for e in range(NEXP):
            if True:
                bpe = self.nbank()
                self.MMb(bpe, slice(0, ML), self.sel8[0:8, e * 128:(e + 1) * 128],
                         self.p8T_bf[0:8, :], True, True,
                         [("c_sel8",), ("p8Tb",)], f"m{m}e{e}b")
                V(lambda p, b=bpe: p.tensor_scalar(
                    self.p_be[:, :], self.pb[b][:, 0:ML], 1.0 / ALPHA, None,
                    Op.mult),
                  [("pb", bpe), ("c_c",)], [("p_be",)], f"m{m}e{e}p")
                self.free_bank(bpe)
                for fc in range(NCH):
                    s1 = (e * NCH * 2 + 2 * fc) % 3
                    pg = self.r_f1[:, s1 * 1024:(s1 + 1) * 1024]
                    self.ld(pg, self.scr["fw1_b"][e, fc], [("rf1", s1)],
                            f"m{m}e{e}wg{fc}", reads=[("sc_f1",)])
                    bG = self.nbank()
                    for kc in range(NCH):
                        self.MMb(bG, slice(0, ML), pg[:, kc * 128:(kc + 1) * 128],
                                 self.h_bf[:, kc * ML:(kc + 1) * ML],
                                 kc == 0, kc == NCH - 1,
                                 [("rf1", s1), ("h_bf", kc)], f"m{m}e{e}G{fc}{kc}")
                    s1b = (e * NCH * 2 + 2 * fc + 1) % 3
                    pl = self.r_f1[:, s1b * 1024:(s1b + 1) * 1024]
                    self.ld(pl, self.scr["fw1_b"][e, NCH + fc], [("rf1", s1b)],
                            f"m{m}e{e}wl{fc}", reads=[("sc_f1",)])
                    bL = self.nbank()
                    for kc in range(NCH):
                        self.MMb(bL, slice(0, ML), pl[:, kc * 128:(kc + 1) * 128],
                                 self.h_bf[:, kc * ML:(kc + 1) * ML],
                                 kc == 0, kc == NCH - 1,
                                 [("rf1", s1b), ("h_bf", kc)], f"m{m}e{e}L{fc}{kc}")
                    S_(lambda p, b=bG, e=e, fc=fc: p.activation(
                        self.scx[:, :], self.pb[b][:, 0:ML], AF.Silu, scale=ALPHA,
                        bias=self.fb1s[:, e * 8 + fc:e * 8 + fc + 1]),
                       [("pb", bG), ("c_fb1s",)], [("scx", 0), ("scx", 1)],
                       f"m{m}e{e}s1{fc}")
                    self.free_bank(bG)
                    S_(lambda p, b=bL, e=e, fc=fc: p.activation(
                        self.scy[:, :], self.pb[b][:, 0:ML], AF.Identity,
                        bias=self.fb1p[:, e * 8 + fc:e * 8 + fc + 1]),
                       [("pb", bL), ("c_fb1p",)], [("scy",)], f"m{m}e{e}s2{fc}")
                    self.free_bank(bL)
                    V(lambda p: p.tensor_scalar(self.scy[:, :], self.scy[:, :],
                                                8.0, -6.0, Op.min, Op.max),
                      [("scy",), ("c_c",)], [("scy",)], f"m{m}e{e}s3{fc}")
                    V(lambda p: p.tensor_tensor(
                        self.scy[:, :], self.scy[:, :], self.p_be[:, :], Op.mult),
                      [("scy",), ("p_be",)], [("scy",)], f"m{m}e{e}s4{fc}")
                    act = self.act_bf[:, fc * ML:(fc + 1) * ML]
                    V(lambda p, a=act: p.scalar_tensor_tensor(
                        a, self.scx[:, :], SWISH_CAP, self.scy[:, :],
                        Op.min, Op.mult),
                      [("scx", 0), ("scx", 1), ("scy",), ("c_c",)], [("act", fc)],
                      f"m{m}e{e}s5{fc}")
                for g in range(2):
                    m2banks = [self.nbank() for _ in range(4)]
                    for dj in range(4):
                        dd = g * 4 + dj
                        s2 = (e * 4 + dj) % 2
                        pw = self.r_fw2[:, s2 * 1024:(s2 + 1) * 1024]
                        self.ld(pw, self.scr["fw2_b"][e, dd], [("rfw2", s2)],
                                f"m{m}e{e}w2{dd}", reads=[("sc_f2",)])
                        bank = m2banks[dj]
                        cols = slice(0, 512)
                        if e == 0:
                            self.MMb(bank, cols,
                                     self.fb2_bf[0:8, dd * 128:(dd + 1) * 128],
                                     self.p8T_bf[0:8, 0:512],
                                     True, False,
                                     [("c_fb2b",), ("p8Tb",)],
                                     f"m{m}b2_{dd}")
                        for ic in range(NCH):
                            first = (e != 0 and ic == 0)
                            last = (ic == NCH - 1)
                            self.MMb(bank, cols,
                                     pw[:, ic * 128:(ic + 1) * 128],
                                     self.act_bf[:, ic * ML: ic * ML + 512],
                                     first, last,
                                     [("rfw2", s2), ("act", ic)],
                                     f"m{m}e{e}m2_{dd}_{ic}")
                    for dj in range(4):
                        dd = g * 4 + dj
                        bank = m2banks[dj]
                        x2 = self.acc[:, dd * ML: dd * ML + 512]
                        V(lambda p, x2=x2, b=bank: p.tensor_tensor(
                            x2, x2, self.pb[b][:, 0:512], Op.add),
                          [("pb", bank), ("acc", dd)], [("acc", dd)],
                          f"m{m}fa{e}_{dd}")
                        self.free_bank(bank)

    def _ph8(self, sq, m):
        V, S_ = self.V, self.S_
        t0 = m * ML
        if m == 0 and sq == 0:
            self.tap("acc2", self.acc[:, :], [128, NCH * ML],
                     [("acc", dc2) for dc2 in range(NCH)], F32)
        for pair in range(2):
            P0 = pair * 256
            NT = 256
            bsum = self.nbank()
            bsq = self.nbank()
            for dc in range(NCH):
                x2 = self.acc[:, dc * ML + P0: dc * ML + P0 + NT]
                S_(lambda p, x2=x2: p.copy(self.lnsc[:, 0:NT].bitcast(F32R), x2),
                   [("acc", dc)], [("lnsc",)], f"m{m}zc{pair}{dc}")
                self.MMr(bsum, slice(0, NT), self.onesc_r[:, 0:1],
                         self.lnsc[:, 0:NT], dc == 0, dc == NCH - 1,
                         [("c_onescr",), ("lnsc",)], f"m{m}z1{pair}{dc}", mrows=1)
                S_(lambda p, x2=x2: p.activation(self.sqsc[:, 0:NT].bitcast(F32R),
                                                 x2, AF.Square),
                   [("acc", dc)], [("sqsc", 0), ("sqsc", 1)], f"m{m}zq{pair}{dc}")
                self.MMr(bsq, slice(0, NT), self.onesc_r[:, 0:1], self.sqsc[:, 0:NT],
                         dc == 0, dc == NCH - 1, [("c_onescr",), ("sqsc", 0), ("sqsc", 1)],
                         f"m{m}z2{pair}{dc}", mrows=1)
            self._ln_rows(m, 10 + pair, bsum, bsq, NT, ffn_r=False)
            self.free_bank(bsum)
            self.free_bank(bsq)
            bs1 = self.nbank()
            bs2 = self.nbank()
            r = self.row1
            self.bc_row(bs1, slice(0, NT), r[0:1, 768:768 + NT], [("row1",)],
                        f"m{m}zb{pair}")
            self.bc_row(bs2, slice(0, NT), r[0:1, 1024:1024 + NT], [("row1",)],
                        f"m{m}zm{pair}")
            for sub in range(2):
                st = pair * 2 + sub
                for half in range(2):
                    bank = self.nbank()
                    accs = self.acc[:, half * 4 * ML:(half + 1) * 4 * ML]
                    x23 = accs.rearrange("p (d n) -> p d n", n=ML)[
                        :, :, st * 128:(st + 1) * 128]
                    racc = [("acc", half * 4 + c4) for c4 in range(4)]
                    bb1 = self.pb[bs1][:, sub * 128:(sub + 1) * 128]
                    bb2 = self.pb[bs2][:, sub * 128:(sub + 1) * 128]
                    V(lambda p, x23=x23, bb1=bb1: p.tensor_tensor(
                        self.scx[:, 0:512].rearrange("p (d n) -> p d n", d=4),
                        x23, bb1.unsqueeze(1).to_broadcast([128, 4, 128]),
                        Op.mult),
                      [("pb", bs1)] + racc, [("scx", 0), ("scx", 1)],
                      f"m{m}za{st}{half}")
                    V(lambda p, bb2=bb2: p.tensor_tensor(
                        self.scy[:, 0:512].rearrange("p (d n) -> p d n", d=4),
                        self.scx[:, 0:512].rearrange("p (d n) -> p d n", d=4),
                        bb2.unsqueeze(1).to_broadcast([128, 4, 128]),
                        Op.subtract),
                      [("pb", bs2), ("scx", 0), ("scx", 1)], [("scy",)],
                      f"m{m}zs{st}{half}")
                    for c4 in range(4):
                        self.TRf(bank, slice(c4 * 128, (c4 + 1) * 128),
                                 self.scy[:, c4 * 128:(c4 + 1) * 128],
                                 [("scy",)], f"m{m}zt{st}{half}{c4}")
                    if half == 0:
                        stg, rstg = self.scy, [("scy",)]
                    else:
                        stg, rstg = self.scx, [("scx", 0), ("scx", 1)]
                    V(lambda p, b=bank, stg=stg: p.tensor_copy(
                        stg[:, 0:512], self.pb[b][:, 0:512]),
                      [("pb", bank)], rstg, f"m{m}zo{st}{half}")
                    self.free_bank(bank)
                    self.st_out(self.y_out[sq, t0 + st * 128: t0 + (st + 1) * 128,
                                           half * 512:(half + 1) * 512],
                                stg[:, 0:512], rstg, f"m{m}so{st}{half}")
            self.free_bank(bs1)
            self.free_bank(bs2)

def build_programs(nseq, seqlen, taps=()):
    mk = MK(nseq, seqlen, taps)
    nc = mk.build()
    return mk, nc


def kernel(**inputs):
    x = np.ascontiguousarray(inputs["L0_attn_hidden_states"], dtype=np.float32)
    B, S, _ = x.shape
    n_cores = 8
    per = B // n_cores
    host = prepare_host({k: np.asarray(v) for k, v in inputs.items()})
    mk, nc = build_programs(per, S)
    in_maps = []
    for c in range(n_cores):
        xc = x[c * per:(c + 1) * per].reshape(per, S, NCH, 128)
        im = {"x": np.ascontiguousarray(xc.transpose(0, 2, 3, 1))}
        im.update({k + "_d": v for k, v in host.items()})
        in_maps.append(im)
    res = run_bass_kernel_spmd(nc, in_maps, core_ids=list(range(n_cores)))
    out = np.concatenate([res.results[c]["y"] for c in range(n_cores)], axis=0)
    return out.astype(np.float32)


# version 38
# speedup vs baseline: 1.1742x; 1.0158x over previous
import numpy as np
import ml_dtypes
from contextlib import ExitStack

import concourse.bass as bass
import concourse.mybir as mybir
from concourse.bass_utils import run_bass_kernel_spmd
from concourse.alu_op_type import AluOpType as Op

F32 = mybir.dt.float32
F32R = mybir.dt.float32r
BF16 = mybir.dt.bfloat16
AF = mybir.ActivationFunctionType
AXX = mybir.AxisListType.X

D = 1024
H = 4
NCH = 8
CONV_K = 4
FIR_S = 5
FIR_L = 64
NTAP = FIR_S + FIR_L
NEXP = 8
ML = 512
FLOOR_EPS = 0.08
ALPHA = 1.702
SWISH_CAP = float(ALPHA * 7.0 / (1.0 + np.exp(-ALPHA * 7.0)))
F32EPS = float(np.finfo(np.float32).eps)
VL = FIR_L - 1 + ML
CL = CONV_K - 1 + ML

ENGINES = ("T", "V", "S", "G", "D")
ND_SEM = 6
SEM_NAMES = ["T", "V", "S", "G"] + [f"D{i}" for i in range(ND_SEM)] + ["GD"]


class _Rec:
    __slots__ = ("idx", "eng", "seq", "fn", "deps", "clock", "inc", "is_dma",
                 "sem", "wv", "self_wait", "tag")


class Sched:

    def __init__(self):
        self.ops = []
        self.by_eng = {e: [] for e in ENGINES}
        self.eng_seq = {e: 0 for e in ENGINES}
        self.eng_clock = {e: {n: 0 for n in SEM_NAMES} for e in ENGINES}
        self.last_w = {}
        self.readers = {}
        self.final_deps = []
        self.d_rank = 0
        self.sem_counts = {n: 0 for n in SEM_NAMES}

    def op(self, eng, fn, reads=(), writes=(), tag="", is_dma=False):
        idx = len(self.ops)
        seq = self.eng_seq[eng] + 1
        self.eng_seq[eng] = seq
        deps = set()
        nopr = set()
        for r in reads:
            w = self.last_w.get(r)
            if w is not None:
                deps.add(w)
                if r[0] in ("pb", "pbB"):
                    nopr.add(w)
            self.readers.setdefault(r, []).append(idx)
        for r in writes:
            w = self.last_w.get(r)
            if w is not None:
                deps.add(w)
                if r[0] in ("pb", "pbB"):
                    nopr.add(w)
            for rd in self.readers.get(r, ()):
                if rd != idx:
                    deps.add(rd)
                    if r[0] in ("pb", "pbB"):
                        nopr.add(rd)
            self.readers[r] = []
            self.last_w[r] = idx
        ec = self.eng_clock[eng]
        real = []
        for d in sorted(deps):
            od = self.ops[d]
            if od.eng == "T" and eng == "T":
                for k, v in od.clock.items():
                    if v > ec[k]:
                        ec[k] = v
                continue
            kv = od.wv if od.is_dma else od.seq
            if d not in nopr and ec[od.sem] >= kv:
                continue
            real.append(d)
        for d in real:
            for k, v in self.ops[d].clock.items():
                if v > ec[k]:
                    ec[k] = v
        rec = _Rec()
        rec.idx, rec.eng, rec.seq, rec.fn = idx, eng, seq, fn
        rec.is_dma = is_dma or eng == "D"
        rec.inc = rec.is_dma
        rec.self_wait = None
        if rec.is_dma:
            if eng == "D":
                self.d_rank += 1
                rec.sem = f"D{(self.d_rank - 1) % ND_SEM}"
            else:
                rec.sem = "GD"
            n = self.sem_counts[rec.sem] + 1
            self.sem_counts[rec.sem] = n
            rec.wv = 16 * n
            if n > 1:
                rec.self_wait = (rec.sem, 16 * (n - 1))
                if ec[rec.sem] < 16 * (n - 1):
                    ec[rec.sem] = 16 * (n - 1)
        else:
            rec.sem = eng
            rec.wv = 0
        if eng == "T":
            ec["T"] = seq
        clock = dict(ec)
        kown = rec.wv if rec.is_dma else seq
        if clock[rec.sem] < kown:
            clock[rec.sem] = kown
        rec.deps, rec.clock = real, clock
        rec.tag = tag
        self.ops.append(rec)
        self.by_eng[eng].append(rec)
        return idx

    def dma(self, fn, reads=(), writes=(), tag="", queue="D"):
        return self.op(queue, fn, reads, writes, tag, is_dma=True)

    def barrier_on(self, idxs):
        self.final_deps.extend(idxs)

    def finalize(self, sems):
        for rec in self.ops:
            for d in rec.deps:
                self.ops[d].inc = True
        for d in self.final_deps:
            self.ops[d].inc = True
        cnt = {e: 0 for e in ENGINES}
        for rec in self.ops:
            if rec.inc and not rec.is_dma:
                cnt[rec.eng] += 1
                rec.wv = cnt[rec.eng]
        self.sems = sems

    def emit(self, eng, proxy, final_extra=None):
        for rec in self.by_eng[eng]:
            waits = {}
            for d in rec.deps:
                dr = self.ops[d]
                if waits.get(dr.sem, -1) < dr.wv:
                    waits[dr.sem] = dr.wv
            if rec.self_wait is not None:
                snm, sv = rec.self_wait
                if waits.get(snm, -1) < sv:
                    waits[snm] = sv
            for key in sorted(waits):
                proxy.wait_ge(self.sems[key], waits[key])
            inst = rec.fn(proxy)
            if rec.inc:
                inst.then_inc(self.sems[rec.sem], 16 if rec.is_dma else 1)
        if final_extra is not None:
            final_extra(proxy)


CONST_VALUES = [0.0, 1.0, -1.0, 1e-12, 1e-5, F32EPS, 1e-6, 1.0 / 256.0,
                1.0 / 1024.0, 1.0 - 2.0 * FLOOR_EPS, SWISH_CAP, ALPHA,
                1.0 / ALPHA, -6.0, 8.0, -1e30, 7.0, 0.5, 2.0, 1.0 + 1e-30]



def _stream_w(w):
    d_in, d_out = w.shape
    nk, nd = d_in // 128, d_out // 128
    v = w.reshape(nk, 128, nd, 128).transpose(2, 1, 0, 3)
    return np.ascontiguousarray(v.reshape(nd, 128, nk * 128))


def _diag_tiles(w_taps):
    T = w_taps.shape[1]
    out = np.zeros((NCH, 128, T * 128), np.float32)
    ii = np.arange(128)
    for c in range(NCH):
        for t in range(T):
            out[c, ii, t * 128 + ii] = w_taps[c * 128:(c + 1) * 128, t]
    return out


def prepare_host(inp):
    g = {}
    g["wq_s"] = _stream_w(inp["L0_attn_Wq"])
    g["wk_s"] = _stream_w(inp["L0_attn_Wk"])
    g["wv_s"] = _stream_w(inp["L0_attn_Wv"])
    g["wb_s"] = np.ascontiguousarray(
        inp["L0_attn_Wb"].reshape(NCH, 128, H).transpose(1, 0, 2).reshape(128, NCH * H))
    w1 = inp["L0_attn_mlp_w1"]
    g["w1a_s"] = _stream_w(w1[0:1024, :])
    g["w1b"] = np.ascontiguousarray(w1[1024:1040, :])
    g["w2g"] = np.ascontiguousarray(
        inp["L0_attn_mlp_w2"].reshape(NCH, 128, H).transpose(1, 0, 2).reshape(128, NCH * H))
    g["b1g"] = np.ascontiguousarray(inp["L0_attn_mlp_b1"].reshape(NCH, 128).T)
    g["b2g"] = np.ascontiguousarray(inp["L0_attn_mlp_b2"].reshape(H, 1))
    g["logit_temp"] = np.ascontiguousarray(inp["L0_attn_logit_temp"].reshape(1, 1))
    g["crl"] = np.ascontiguousarray(inp["L0_attn_conv_res_logit"].reshape(1, H))
    g["rms_row"] = np.ascontiguousarray(inp["L0_attn_rms_w"].reshape(1, 256))
    g["wo_s"] = _stream_w(inp["L0_attn_Wo"])
    cw = np.concatenate([inp["L0_attn_conv_w_q"], inp["L0_attn_conv_w_k"],
                         inp["L0_attn_conv_w_v"]], axis=1)
    g["conv_diag"] = _diag_tiles(cw)
    fir = np.concatenate([inp["L0_attn_fir_short_w"].reshape(1024, FIR_S),
                          inp["L0_attn_fir_long_w"].reshape(1024, FIR_L)], 1)
    g["fir_diag"] = _diag_tiles(fir)
    g["norm_w"] = np.ascontiguousarray(inp["L0_ffn_norm_w"].reshape(NCH, 128).T)
    g["gate_w_t"] = np.ascontiguousarray(
        inp["L0_ffn_gate_w"].T.reshape(NCH, 128, NEXP).transpose(1, 0, 2)
        .reshape(128, NCH * NEXP))
    g["gate_b"] = np.ascontiguousarray(inp["L0_ffn_gate_b"].reshape(NEXP, 1))
    m1 = inp["L0_ffn_mlp1_w"]
    w1s = np.empty((NEXP, 16, 128, 1024), np.float32)
    for e in range(NEXP):
        w1s[e, 0:8] = _stream_w(np.ascontiguousarray(m1[e, 0::2, :].T))
        w1s[e, 8:16] = _stream_w(np.ascontiguousarray(m1[e, 1::2, :].T))
    g["ffn_w1_s"] = w1s
    m2 = inp["L0_ffn_mlp2_w"]
    w2s = np.empty((NEXP, 8, 128, 1024), np.float32)
    for e in range(NEXP):
        w2s[e] = _stream_w(np.ascontiguousarray(m2[e].T))
    g["ffn_w2_s"] = w2s
    b1 = inp["L0_ffn_mlp1_b"]
    bb = np.concatenate([b1[:, 0::2].reshape(NEXP, 8, 128),
                         b1[:, 1::2].reshape(NEXP, 8, 128)], axis=1)
    g["ffn_b1"] = np.ascontiguousarray(bb.transpose(2, 0, 1).reshape(128, NEXP * 16))
    g["ffn_b2"] = np.ascontiguousarray(inp["L0_ffn_mlp2_b"])
    g["ident"] = np.eye(128, dtype=np.float32)
    g["ident_bf"] = np.eye(128, dtype=ml_dtypes.bfloat16)
    g["ones_row"] = np.ones((1, 128), np.float32)
    g["ones_col"] = np.ones((128, 1), np.float32)
    g["m_stril"] = np.tril(np.ones((128, 128), ml_dtypes.bfloat16), -1)
    g["m_striu"] = np.triu(np.ones((128, 128), np.float32), 0)
    g["floor_row"] = np.tile(np.array([FLOOR_EPS, FLOOR_EPS, 0, 0], np.float32),
                             (128, 1))
    g["consts"] = np.tile(np.array(CONST_VALUES, np.float32), (128, 1))
    sel = np.zeros((8, 8 * 128), ml_dtypes.bfloat16)
    for e in range(8):
        sel[e, e * 128:(e + 1) * 128] = 1.0
    g["sel8"] = sel
    return g


class MK:
    def __init__(self, nseq, seqlen, taps=()):
        assert seqlen % ML == 0
        self.nseq, self.seqlen = nseq, seqlen
        self.nmac = seqlen // ML
        self.taps = list(taps)
        self.tap_specs = []
        self.nc = bass.Bass("TRN2", target_bir_lowering=False)
        self.s = Sched()
        self.es = ExitStack()
        self._nb = 0
        self._pool = [0, 1, 2, 3, 4, 5]
        self._held = set()

    def sb(self, name, shape, dtype):
        return self.es.enter_context(self.nc.sbuf_tensor(name, list(shape), dtype))

    def nbank(self):
        for _ in range(len(self._pool)):
            self._nb = (self._nb + 1) % len(self._pool)
            b = self._pool[self._nb]
            if b not in self._held:
                self._held.add(b)
                return b
        raise RuntimeError("no free psum bank")

    def free_bank(self, b):
        self._held.discard(b)

    def set_pool(self, pool):
        self._pool = list(pool)
        self._nb = 0

    def ld(self, dst, src, writes, tag, reads=()):
        return self.s.op("D", lambda p, d=dst, s_=src: p.dma_start(out=d, in_=s_),
                         reads=list(reads), writes=list(writes), tag=tag)

    def st_out(self, dst, src, reads, tag):
        i = self.s.op("D", lambda p, d=dst, s_=src: p.dma_start(out=d, in_=s_),
                      reads=list(reads), writes=[], tag=tag)
        self.s.barrier_on([i])

    def V(self, fn, reads, writes, tag):
        return self.s.op("V", fn, reads, writes, tag)

    def S_(self, fn, reads, writes, tag):
        return self.s.op("S", fn, reads, writes, tag)

    def MMb(self, bank, cols, lhsT, rhs, first, last, reads, tag):
        o = self.pb[bank][:, cols]
        return self.s.op("T", lambda p, o=o, a=lhsT, b=rhs, f=first, l=last:
                         p.matmul(o, a, b, start=f, stop=l),
                         reads=list(reads), writes=[("pb", bank)], tag=tag)

    def MMr(self, bank, cols, lhsT, rhs, first, last, reads, tag, mrows=None):
        o = self.pb[bank][0:mrows, cols] if mrows else self.pb[bank][:, cols]
        a, b = lhsT.bitcast(F32R), rhs.bitcast(F32R)
        return self.s.op("T", lambda p, o=o, a=a, b=b, f=first, l=last:
                         p.matmul(o, a, b, start=f, stop=l),
                         reads=list(reads), writes=[("pb", bank)], tag=tag)

    def TRf(self, bank, cols, in_ap, reads, tag, rows=None, pk=128):
        o = self.pb[bank][0:rows, cols] if rows else self.pb[bank][:, cols]
        ident = self.ident[0:pk, 0:pk]
        return self.s.op("T", lambda p, o=o, a=in_ap, i=ident:
                         p.transpose(o, a, i),
                         reads=list(reads) + [("c_ident",)],
                         writes=[("pb", bank)], tag=tag)

    def TRb(self, slot, cols, in_ap, reads, tag, rows=None):
        o = self.pbB[slot][0:rows, cols] if rows else self.pbB[slot][:, cols]
        return self.s.op("T", lambda p, o=o, a=in_ap:
                         p.transpose(o, a, self.ident_bf[:, :]),
                         reads=list(reads) + [("c_identb",)],
                         writes=[("pbB", slot)], tag=tag)

    def bc_row(self, bank, cols, row_ap, reads, tag):
        o = self.pb[bank][:, cols]
        a = self.m_striu[0:1, :]
        return self.s.op("T", lambda p, o=o, a=a, b=row_ap:
                         p.matmul(o, a, b, start=True, stop=True),
                         reads=list(reads) + [("c_mtu",)],
                         writes=[("pb", bank)], tag=tag)

    def tap(self, name, ap, shape, reads, dtype=F32):
        if name not in self.taps:
            return
        d = self.nc.dram_tensor("tap_" + name, list(shape), dtype,
                                kind="ExternalOutput").ap()
        self.tap_specs.append(name)
        i = self.s.dma(lambda p, d=d, a=ap: p.dma_start(out=d, in_=a),
                       reads=list(reads), writes=[], tag="tap_" + name)
        self.s.barrier_on([i])

    def build(self):
        nc, s = self.nc, self.s
        NS, SL = self.nseq, self.seqlen
        self.x_in = nc.dram_tensor("x", [NS, NCH, 128, SL], F32, kind="ExternalInput").ap()
        self.y_out = nc.dram_tensor("y", [NS, SL, D], F32, kind="ExternalOutput").ap()
        din = {}
        for nm, shp, dt in [
            ("wq_s", [8, 128, 1024], F32), ("wk_s", [8, 128, 1024], F32),
            ("wv_s", [8, 128, 1024], F32), ("wb_s", [128, 32], F32),
            ("w1a_s", [8, 128, 1024], F32), ("w1b", [16, 1024], F32),
            ("w2g", [128, 32], F32), ("b1g", [128, 8], F32), ("b2g", [4, 1], F32),
            ("logit_temp", [1, 1], F32), ("crl", [1, 4], F32),
            ("rms_row", [1, 256], F32), ("wo_s", [8, 128, 1024], F32),
            ("conv_diag", [8, 128, 12 * 128], F32),
            ("fir_diag", [8, 128, NTAP * 128], F32),
            ("norm_w", [128, 8], F32), ("gate_w_t", [128, 64], F32),
            ("gate_b", [8, 1], F32),
            ("ffn_w1_s", [NEXP, 16, 128, 1024], F32),
            ("ffn_w2_s", [NEXP, 8, 128, 1024], F32),
            ("ffn_b1", [128, NEXP * 16], F32), ("ffn_b2", [8, 1024], F32),
            ("ident", [128, 128], F32), ("ident_bf", [128, 128], BF16),
            ("ones_row", [1, 128], F32), ("ones_col", [128, 1], F32),
            ("m_stril", [128, 128], BF16), ("m_striu", [128, 128], F32),
            ("floor_row", [128, 4], F32), ("consts", [128, len(CONST_VALUES)], F32),
            ("sel8", [8, 8 * 128], BF16),
        ]:
            din[nm] = nc.dram_tensor(nm + "_d", shp, dt, kind="ExternalInput").ap()
        self.din = din
        scr = {}
        for nm, shp in [("wq_b", [8, 128, 1024]), ("wk_b", [8, 128, 1024]),
                        ("wv_b", [8, 128, 1024]), ("w1a_b", [8, 128, 1024]),
                        ("wo_b", [8, 128, 1024]), ("conv_b", [8, 128, 12 * 128]),
                        ("fir_b", [8, 128, NTAP * 128]),
                        ("fw1_b", [NEXP, 16, 128, 1024]),
                        ("fw2_b", [NEXP, 8, 128, 1024]),
                        ("w1b_c", [16, 1024])]:
            scr[nm] = nc.dram_tensor(nm, shp, BF16, kind="Internal").ap()
        self.scr = scr

        sb = self.sb
        self.consts = sb("consts", [128, len(CONST_VALUES)], F32)
        self.wb = sb("wb", [128, 32], BF16)
        self.w1b = sb("w1b", [16, 1024], BF16)
        self.w2g = sb("w2g", [128, 32], F32)
        self.b1g = sb("b1g", [128, 8], F32)
        self.b2g = sb("b2g", [4, 1], F32)
        self.lt = sb("lt", [1, 8], F32)
        self.crl = sb("crl", [1, 4], F32)
        self.rmsrow = sb("rmsrow", [1, 256], F32)
        self.normw = sb("normw", [128, 8], F32)
        self.gatewt = sb("gatewt", [128, 64], F32)
        self.gateb = sb("gateb", [8, 1], F32)
        self.fb1 = sb("fb1", [128, NEXP * 16], F32)
        self.fb1s = sb("fb1s", [128, NEXP * 8], F32)
        self.fb1p = sb("fb1p", [128, NEXP * 8], F32)
        self.fb2_bf = sb("fb2_bf", [8, 1024], BF16)
        self.ident = sb("ident", [128, 128], F32)
        self.ident_r = sb("ident_r", [128, 128], F32)
        self.onesc_r = sb("onesc_r", [128, 1], F32)
        self.w2g_bf = sb("w2g_bf", [128, 32], BF16)
        self.gatewt_r = sb("gatewt_r", [128, 64], F32)
        self.lnsc = sb("lnsc", [128, 256], F32)
        self.ident_bf = sb("ident_bf", [128, 128], BF16)
        self.ones_col = sb("ones_col", [128, 1], F32)
        self.m_stril = sb("m_stril", [128, 128], BF16)
        self.m_striu = sb("m_striu", [128, 128], F32)
        self.floor_row = sb("floor_row", [128, 4], F32)
        self.sel8 = sb("sel8", [8, 8 * 128], BF16)
        self.rms_bc = sb("rms_bc", [128, 256], F32)
        self.sigc = sb("sigc", [128, 4], F32)
        self.invt = sb("invt", [128, 1], F32)

        self.x_tm = sb("x_tm", [128, 512], F32)
        self.xT_pair = sb("xT_pair", [128, NCH * 256], F32)
        self.xT_bf = sb("xT_bf", [128, NCH * ML], BF16)
        self.pre = sb("pre", [128, 2 * CL], BF16)
        self.tails = [sb(f"tail{i}", [128, 3 * NCH], BF16) for i in range(3)]
        self.qT = sb("qT", [128, NCH * ML], BF16)
        self.kT = sb("kT", [128, NCH * ML], BF16)
        self.vT = sb("vT", [128, NCH * VL], BF16)
        self.lsT = sb("lsT", [128, NCH * ML], BF16)
        self.llT = sb("llT", [128, NCH * ML], BF16)
        self.beTa = sb("beTa", [4, ML], F32)
        self.beta_tm = sb("beta_tm", [128, 16], F32)
        self.sq_tm = sb("sq_tm", [128, 16], F32)
        self.sqrow = sb("sqrow", [1, ML], F32)
        self.sqsc = sb("sqsc", [128, ML], F32)
        self.branches = [sb(f"branches{j}", [128, H * 1024], BF16)
                         for j in range(2)]
        self.ktm = sb("ktm", [128, 2 * 256], F32)
        self.S_sb = sb("S_sb", [128, H * 512], F32)
        self.S_bf = sb("S_bf", [128, 2 * 512], BF16)
        self.chain = [[sb(f"chain{sl}_{i}", [128, 384], BF16) for i in range(2)]
                      for sl in range(2)]
        self.attnT = sb("attnT", [128, 2 * 128], F32)
        self.vbk = sb("vbk", [128, 2 * 256], F32)
        self.stat_sc = sb("stat_sc", [128, 2 * 8], F32)
        self.colsc = sb("colsc", [128, 2 * 8], F32)
        self.stats_tm = sb("stats_tm", [128, 2 * 16], F32)
        self.qcsc = sb("qcsc", [128, 8], F32)
        self.statsT = sb("statsT", [16, 2 * H * 128], BF16)
        self.hg = sb("hg", [128, 2 * 512], BF16)
        self.lgrb = sb("lgrb", [4, 256], F32)
        self.fglb = sb("fglb", [8, 256], F32)
        self.probs = sb("probs", [128, H * 4], F32)
        self.o_n = sb("o_n", [128, 1024], F32)
        self.onT = sb("onT", [128, NCH * 256], BF16)
        self.hrot = sb("hrot", [128, 2 * 256], F32)
        self.h_bf = sb("h_bf", [128, NCH * ML], BF16)
        self.acc = sb("acc", [128, NCH * ML], F32)
        self.p8 = sb("p8", [128, 4 * NEXP], F32)
        self.p8T_bf = sb("p8T_bf", [8, ML], BF16)
        self.p_be = sb("p_be", [128, ML], F32)
        self.act_bf = sb("act_bf", [128, NCH * ML], BF16)
        self.gl_sb = sb("gl_sb", [128, 4 * 8], F32)
        self.row1 = sb("row1", [1, 5 * 256], F32)
        self.scx = sb("scx", [128, ML], F32)
        self.scy = sb("scy", [128, ML], F32)
        self.scz = sb("scz", [128, 128], F32)

        self.r_w = sb("r_w", [128, 2 * 1024], BF16)
        self.r_w1a = sb("r_w1a", [128, 2 * 1024], BF16)
        self.r_wo = sb("r_wo", [128, 2 * 1024], BF16)
        self.r_conv = sb("r_conv", [128, 12 * 128], BF16)
        self.r_fir = sb("r_fir", [128, 2 * 12 * 128], BF16)
        self.r_f1 = sb("r_f1", [128, 3 * 1024], BF16)
        self.r_fw2 = sb("r_fw2", [128, 2 * 1024], BF16)

        self.pb = [self.es.enter_context(nc.psum_tensor(f"pb{i}", [128, 512], F32))
                   for i in range(6)]
        self.pbB = [self.es.enter_context(nc.psum_tensor(f"pbB{i}", [128, 1024], BF16))
                    for i in range(2)]

        for i, v in enumerate(CONST_VALUES):
            nc.const_aps.register(F32, v, self.consts[:, i:i + 1])

        self._startup()
        for sq in range(NS):
            self._seq(sq)

        sems = {e: self.es.enter_context(nc.semaphore(f"sem_{e}"))
                for e in SEM_NAMES}
        s.finalize(sems)
        maxv = {}
        for d in set(s.final_deps):
            dr = s.ops[d]
            if maxv.get(dr.sem, -1) < dr.wv:
                maxv[dr.sem] = dr.wv

        def final_waits(proxy):
            for e in sorted(maxv):
                proxy.wait_ge(sems[e], maxv[e])

        with self.es, nc.Block() as block:
            @block.sync
            def _(p):
                s.emit("D", p, final_extra=final_waits)

            @block.tensor
            def _(p):
                s.emit("T", p)

            @block.vector
            def _(p):
                s.emit("V", p)

            @block.scalar
            def _(p):
                s.emit("S", p)

            @block.gpsimd
            def _(p):
                s.emit("G", p)
        return nc

    def _startup(self):
        s, din, scr = self.s, self.din, self.scr
        V = self.V
        for src, dst, rg in [("wq_s", "wq_b", "sc_wq"), ("wk_s", "wk_b", "sc_wk"),
                             ("wv_s", "wv_b", "sc_wv"), ("w1a_s", "w1a_b", "sc_w1a"),
                             ("wo_s", "wo_b", "sc_wo"), ("conv_diag", "conv_b", "sc_cv"),
                             ("fir_diag", "fir_b", "sc_fir"),
                             ("ffn_w1_s", "fw1_b", "sc_f1"),
                             ("ffn_w2_s", "fw2_b", "sc_f2"),
                             ("w1b", "w1b_c", "sc_w1b")]:
            s.op("G", lambda p, a=din[src], b=scr[dst]: p.dma_start(out=b, in_=a),
                 reads=[], writes=[(rg,)], tag=f"cast_{dst}", is_dma=True)

        L = self.ld
        L(self.consts[:, :], din["consts"], [("c_c",)], "c_c")
        s.op("G", lambda p: p.dma_start(out=self.wb[:, :], in_=din["wb_s"]),
             reads=[], writes=[("c_wb",)], tag="c_wb", is_dma=True)
        L(self.w1b[:, :], scr["w1b_c"], [("c_w1b",)], "c_w1b",
          reads=[("sc_w1b",)])
        L(self.w2g[:, :], din["w2g"], [("c_w2g",)], "c_w2g")
        L(self.b1g[:, :], din["b1g"], [("c_b1g",)], "c_b1g")
        L(self.b2g[:, :], din["b2g"], [("c_b2g",)], "c_b2g")
        L(self.lt[0:1, 0:1], din["logit_temp"], [("lt",)], "c_lt")
        L(self.crl[:, :], din["crl"], [("c_crl",)], "c_crl")
        L(self.rmsrow[:, :], din["rms_row"], [("c_rms",)], "c_rms")
        L(self.normw[:, :], din["norm_w"], [("c_nw",)], "c_nw")
        L(self.gatewt[:, :], din["gate_w_t"], [("c_gw",)], "c_gw")
        L(self.gateb[:, :], din["gate_b"], [("c_gb",)], "c_gb")
        L(self.fb1[:, :], din["ffn_b1"], [("c_fb1",)], "c_fb1")
        s.op("G", lambda p: p.dma_start(out=self.fb2_bf[:, :], in_=din["ffn_b2"]),
             reads=[], writes=[("c_fb2b",)], tag="c_fb2b", is_dma=True)
        L(self.ident[:, :], din["ident"], [("c_ident",)], "c_id")
        L(self.ident_bf[:, :], din["ident_bf"], [("c_identb",)], "c_idb")
        L(self.ones_col[:, :], din["ones_col"], [("c_onesc",)], "c_oc")
        L(self.m_stril[:, :], din["m_stril"], [("c_mtl",)], "c_mtl")
        L(self.m_striu[:, :], din["m_striu"], [("c_mtu",)], "c_mtu")
        L(self.floor_row[:, :], din["floor_row"], [("c_floor",)], "c_fl")
        L(self.sel8[:, :], din["sel8"], [("c_sel8",)], "c_s8")
        S_ = self.S_
        S_(lambda p: p.copy(self.ident_r[:, :].bitcast(F32R), self.ident[:, :]),
           [("c_ident",)], [("c_identr",)], "identr")
        S_(lambda p: p.copy(self.onesc_r[:, :].bitcast(F32R), self.ones_col[:, :]),
           [("c_onesc",)], [("c_onescr",)], "onescr")
        S_(lambda p: p.copy(self.w2g_bf[:, :], self.w2g[:, :]),
           [("c_w2g",)], [("c_w2gb",)], "w2gb")
        S_(lambda p: p.copy(self.gatewt_r[:, :].bitcast(F32R), self.gatewt[:, :]),
           [("c_gw",)], [("c_gwr",)], "gwr")

        for e in range(NEXP):
            V(lambda p, e=e: p.tensor_scalar(
                self.fb1s[:, e * 8:(e + 1) * 8], self.fb1[:, e * 16:e * 16 + 8],
                ALPHA, None, Op.mult),
              [("c_fb1",), ("c_c",)], [("c_fb1s",)], f"b1s{e}")
            V(lambda p, e=e: p.tensor_scalar(
                self.fb1p[:, e * 8:(e + 1) * 8], self.fb1[:, e * 16 + 8:e * 16 + 16],
                1.0, None, Op.add),
              [("c_fb1",), ("c_c",)], [("c_fb1p",)], f"b1p{e}")
        b_ = self.nbank()
        self.bc_row(b_, slice(0, 256), self.rmsrow[0:1, :], [("c_rms",)], "rmsbc")
        V(lambda p, b=b_: p.tensor_copy(self.rms_bc[:, :], self.pb[b][:, 0:256]),
          [("pb", b_)], [("c_rmsbc",)], "rmsbc_e")
        self.free_bank(b_)
        S_(lambda p: p.activation(self.crl[0:1, :].bitcast(F32R), self.crl[0:1, :], AF.Sigmoid),
           [("c_crl",)], [("c_crl",)], "crlsig")
        b_ = self.nbank()
        self.bc_row(b_, slice(0, 4), self.crl[0:1, :], [("c_crl",)], "crlbc")
        V(lambda p, b=b_: p.tensor_copy(self.sigc[:, :], self.pb[b][:, 0:4]),
          [("pb", b_)], [("c_sigc",)], "sigc_e")
        self.free_bank(b_)
        lt = self.lt
        S_(lambda p: p.activation(lt[0:1, 1:2], lt[0:1, 0:1], AF.Abs),
           [("lt",)], [("lt",)], "lt1")
        S_(lambda p: p.activation(lt[0:1, 2:3], lt[0:1, 1:2], AF.Exp, scale=-1.0),
           [("lt",), ("c_c",)], [("lt",)], "lt2")
        S_(lambda p: p.activation(lt[0:1, 3:4], lt[0:1, 2:3], AF.Ln, bias=1.0),
           [("lt",), ("c_c",)], [("lt",)], "lt3")
        V(lambda p: p.tensor_scalar(lt[0:1, 4:5], lt[0:1, 0:1], 0.0, None, Op.max),
          [("lt",), ("c_c",)], [("lt",)], "lt4")
        V(lambda p: p.tensor_tensor(lt[0:1, 5:6], lt[0:1, 3:4], lt[0:1, 4:5], Op.add),
          [("lt",)], [("lt",)], "lt5")
        V(lambda p: p.tensor_scalar(lt[0:1, 6:7], lt[0:1, 5:6], 1e-6, None, Op.add),
          [("lt",), ("c_c",)], [("lt",)], "lt6")
        V(lambda p: p.reciprocal(lt[0:1, 7:8], lt[0:1, 6:7]),
          [("lt",)], [("lt",)], "lt7")
        S_(lambda p: p.copy(lt[0:1, 7:8].bitcast(F32R), lt[0:1, 7:8]),
           [("lt",)], [("lt",)], "lt8")
        b_ = self.nbank()
        self.bc_row(b_, slice(0, 1), lt[0:1, 7:8], [("lt",)], "invtbc")
        V(lambda p, b=b_: p.tensor_copy(self.invt[:, :], self.pb[b][:, 0:1]),
          [("pb", b_)], [("c_invt",)], "invt_e")
        self.free_bank(b_)

    def _seq(self, sq):
        V = self.V
        rS_all = [("S_sb", h) for h in range(H)]
        V(lambda p: p.memset(self.S_sb[:, :], 0.0), [], rS_all, f"q{sq}S0")
        self.S_(lambda p: p.copy(self.S_sb[:, :].bitcast(F32R), self.S_sb[:, :]),
                rS_all, rS_all, f"q{sq}S0r")
        for i in range(3):
            V(lambda p, i=i: p.memset(self.tails[i][:, :], 0.0),
              [], [("tail", i)], f"q{sq}t{i}")
        V(lambda p: p.memset(self.vT[:, :], 0.0), [], [("vT",)], f"q{sq}v0")
        for m in range(self.nmac):
            self._ph1(sq, m)
            self._ph2(sq, m)
            self._ph4a(sq, m)
            for pair in range(2):
                self._ph3_pair(sq, m, pair)
            self._ph7(sq, m)
            self._ph8(sq, m)

    def _ph1(self, sq, m):
        V, S_ = self.V, self.S_
        t0 = m * ML
        for dc in range(NCH):
            xa = self.x_tm[:, :]
            self.ld(xa, self.x_in[sq, dc, :, t0:t0 + ML],
                    [("x_tm",)], f"m{m}x{dc}")
            S_(lambda p, dc=dc, xa=xa: p.copy(
                self.xT_bf[:, dc * ML:(dc + 1) * ML], xa),
               [("x_tm",)], [("xTb", dc)], f"m{m}xTb{dc}")
        for dc in range(NCH):
            cpiece = self.r_conv[:, :]
            self.ld(cpiece, self.scr["conv_b"][dc], [("rconv",)],
                    f"m{m}cd{dc}", reads=[("sc_cv",)])
            for pi in range(3):
                par = (dc * 3 + pi) % 2
                scrn = ["wq_b", "wk_b", "wv_b"][pi]
                piece = self.r_w[:, par * 1024:(par + 1) * 1024]
                self.ld(piece, self.scr[scrn][dc], [("rw", par)],
                        f"m{m}w{pi}l{dc}", reads=[("sc_" + ["wq", "wk", "wv"][pi],)])
                bank = self.nbank()
                for kc in range(NCH):
                    self.MMb(bank, slice(0, ML), piece[:, kc * 128:(kc + 1) * 128],
                             self.xT_bf[:, kc * ML:(kc + 1) * ML],
                             kc == 0, kc == NCH - 1,
                             [("rw", par), ("xTb", kc)], f"m{m}p{pi}_{dc}_{kc}")
                prebuf = self.pre[:, (dc % 2) * CL:((dc % 2) + 1) * CL]
                tail = self.tails[pi]
                V(lambda p, pb_=prebuf, b=bank: p.tensor_copy(
                    pb_[:, 3:], self.pb[b][:, 0:ML]),
                    [("pb", bank)], [("pre", dc % 2)], f"m{m}pe{pi}_{dc}")
                self.free_bank(bank)
                V(lambda p, pb_=prebuf, t=tail, dc=dc: p.tensor_copy(
                    pb_[:, 0:3], t[:, dc * 3:dc * 3 + 3]),
                    [("tail", pi)], [("pre", dc % 2)], f"m{m}ph{pi}_{dc}")
                V(lambda p, pb_=prebuf, t=tail, dc=dc: p.tensor_copy(
                    t[:, dc * 3:dc * 3 + 3], pb_[:, ML:ML + 3]),
                    [("pre", dc % 2)], [("tail", pi)], f"m{m}pt{pi}_{dc}")
                bank2 = self.nbank()
                for tap in range(CONV_K):
                    ti = pi * 4 + tap
                    self.MMb(bank2, slice(0, ML),
                             cpiece[:, ti * 128:(ti + 1) * 128],
                             prebuf[:, tap:tap + ML], tap == 0, tap == CONV_K - 1,
                             [("rconv",), ("pre", dc % 2)],
                             f"m{m}c{pi}_{dc}_{tap}")
                if pi == 2:
                    dst = self.vT[:, dc * VL + FIR_L - 1: dc * VL + FIR_L - 1 + ML]
                    S_(lambda p, d=dst, b=bank2: p.activation(
                        d, self.pb[b][:, 0:ML], AF.Silu),
                       [("pb", bank2)], [("vT", dc)], f"m{m}sv{dc}")
                else:
                    outT = self.qT if pi == 0 else self.kT
                    rg = "qT" if pi == 0 else "kT"
                    dst = outT[:, dc * ML:(dc + 1) * ML]
                    S_(lambda p, d=dst, b=bank2: p.activation(
                        d, self.pb[b][:, 0:ML], AF.Silu),
                       [("pb", bank2)], [(rg, dc)], f"m{m}s{pi}_{dc}")
                self.free_bank(bank2)
            fg = self._fir_gen(m, dc)
            if dc % 2 == 1:
                gens = [fg, self._norm_gen(m, dc // 2)]
                while gens:
                    g = gens.pop(0)
                    try:
                        next(g)
                        gens.append(g)
                    except StopIteration:
                        pass
            else:
                for _ in fg:
                    pass

        bb = self.nbank()
        for kc in range(NCH):
            self.s.op("T", lambda p, kc=kc, b=bb: p.matmul(
                self.pb[b][0:4, 0:ML],
                self.wb[:, kc * 4:(kc + 1) * 4],
                self.xT_bf[:, kc * ML:(kc + 1) * ML],
                start=(kc == 0), stop=(kc == NCH - 1)),
                reads=[("c_wb",), ("xTb", kc)], writes=[("pb", bb)],
                tag=f"m{m}be{kc}")
        S_(lambda p, b=bb: p.activation(self.beTa[0:4, :].bitcast(F32R), self.pb[b][0:4, 0:ML],
                                        AF.Sigmoid),
           [("pb", bb)], [("beTa",)], f"m{m}bsig")
        self.free_bank(bb)
        for st in range(4):
            bank = self.nbank()
            self.TRf(bank, slice(0, 4), self.beTa[0:4, st * 128:(st + 1) * 128],
                     [("beTa",)], f"m{m}beT{st}", pk=4)
            V(lambda p, st=st, b=bank: p.tensor_copy(
                self.beta_tm[:, st * 4:(st + 1) * 4], self.pb[b][:, 0:4]),
                [("pb", bank)], [("beta_tm",)], f"m{m}beTe{st}")
            self.free_bank(bank)

        if m == 0 and sq == 0:
            self.tap("qT", self.qT[:, :], [128, NCH * ML],
                     [("qT", c) for c in range(NCH)], BF16)
            self.tap("kT", self.kT[:, :], [128, NCH * ML],
                     [("kT", c) for c in range(NCH)], BF16)
            self.tap("vT", self.vT[:, :], [128, NCH * VL],
                     [("vT", c) for c in range(NCH)], BF16)
            self.tap("beTa", self.beTa[0:4, :], [4, ML], [("beTa",)], F32)

    def _norm_gen(self, m, h):
        V, S_ = self.V, self.S_
        c0, c1 = 2 * h, 2 * h + 1
        if True:
            bank = self.nbank()
            S_(lambda p, c=c0: p.activation(self.sqsc[:, :].bitcast(F32R),
                                            self.kT[:, c * ML:(c + 1) * ML], AF.Square),
               [("kT", c0)], [("sqsc", 0), ("sqsc", 1)], f"m{m}ka{h}")
            yield
            self.MMr(bank, slice(0, ML), self.onesc_r[:, 0:1], self.sqsc[:, :],
                     True, False, [("c_onescr",), ("sqsc", 0), ("sqsc", 1)], f"m{m}kma{h}", mrows=1)
            S_(lambda p, c=c1: p.activation(self.sqsc[:, :].bitcast(F32R),
                                            self.kT[:, c * ML:(c + 1) * ML], AF.Square),
               [("kT", c1)], [("sqsc", 0), ("sqsc", 1)], f"m{m}kb{h}")
            self.MMr(bank, slice(0, ML), self.onesc_r[:, 0:1], self.sqsc[:, :],
                     False, True, [("c_onescr",), ("sqsc", 0), ("sqsc", 1)], f"m{m}kmb{h}", mrows=1)
            yield
            S_(lambda p, b=bank: p.activation(self.sqrow[0:1, :], self.pb[b][0:1, 0:ML],
                                              AF.Sqrt, bias=1e-12),
               [("pb", bank), ("c_c",)], [("sqrow",)], f"m{m}ks{h}")
            self.free_bank(bank)
            yield
            V(lambda p: p.reciprocal(self.sqrow[0:1, :], self.sqrow[0:1, :]),
              [("sqrow",)], [("sqrow",)], f"m{m}kr{h}")
            S_(lambda p: p.copy(self.sqrow[0:1, :].bitcast(F32R),
                                self.sqrow[0:1, :]),
               [("sqrow",)], [("sqrow",)], f"m{m}kr2{h}")
            yield
            bank2 = self.nbank()
            self.bc_row(bank2, slice(0, ML), self.sqrow[0:1, :], [("sqrow",)],
                        f"m{m}kbc{h}")
            yield
            for c in (c0, c1):
                V(lambda p, c=c, b=bank2: p.tensor_tensor(
                    self.kT[:, c * ML:(c + 1) * ML], self.kT[:, c * ML:(c + 1) * ML],
                    self.pb[b][:, 0:ML], Op.mult),
                  [("pb", bank2), ("kT", c)], [("kT", c)], f"m{m}kn{h}_{c}")
            self.free_bank(bank2)
            yield
            bank3 = self.nbank()
            S_(lambda p, c=c0: p.activation(self.sqsc[:, :].bitcast(F32R),
                                            self.qT[:, c * ML:(c + 1) * ML], AF.Square),
               [("qT", c0)], [("sqsc", 0), ("sqsc", 1)], f"m{m}qa{h}")
            self.MMr(bank3, slice(0, ML), self.onesc_r[:, 0:1], self.sqsc[:, :],
                     True, False, [("c_onescr",), ("sqsc", 0), ("sqsc", 1)], f"m{m}qma{h}", mrows=1)
            S_(lambda p, c=c1: p.activation(self.sqsc[:, :].bitcast(F32R),
                                            self.qT[:, c * ML:(c + 1) * ML], AF.Square),
               [("qT", c1)], [("sqsc", 0), ("sqsc", 1)], f"m{m}qb{h}")
            self.MMr(bank3, slice(0, ML), self.onesc_r[:, 0:1], self.sqsc[:, :],
                     False, True, [("c_onescr",), ("sqsc", 0), ("sqsc", 1)], f"m{m}qmb{h}", mrows=1)
            yield
            V(lambda p, b=bank3: p.tensor_copy(self.sqrow[0:1, :].bitcast(F32R), self.pb[b][0:1, 0:ML]),
              [("pb", bank3)], [("sqrow",)], f"m{m}qe{h}")
            self.free_bank(bank3)
            yield
            bank4 = self.nbank()
            for st in range(4):
                self.TRf(bank4, slice(st, st + 1),
                         self.sqrow[0:1, st * 128:(st + 1) * 128],
                         [("sqrow",)], f"m{m}qt{h}_{st}", pk=1)
            yield
            V(lambda p, b=bank4: p.tensor_copy(self.qcsc[:, 0:4], self.pb[b][:, 0:4]),
              [("pb", bank4)], [("qcsc",)], f"m{m}qc{h}")
            self.free_bank(bank4)
            S_(lambda p: p.activation(self.qcsc[:, 4:8], self.qcsc[:, 0:4],
                                      AF.Sqrt, bias=1e-12),
               [("qcsc",), ("c_c",)], [("qcsc",)], f"m{m}qs{h}")
            V(lambda p, h=h: p.reciprocal(self.sq_tm[:, 4 * h:4 * h + 4],
                                          self.qcsc[:, 4:8]),
              [("qcsc",)], [("sq_tm",)], f"m{m}qr{h}")


    def _fir_gen(self, m, dc):
        V = self.V
        sets = [(0, 12), (12, 12), (24, 12), (36, 11), (47, 11), (58, 11)]
        if True:
            bls = self.nbank()
            bll = self.nbank()
            for si, (toff, tcnt) in enumerate(sets):
                par = (dc * 6 + si) % 2
                ring = self.r_fir[:, par * 12 * 128:(par * 12 + 12) * 128]
                self.ld(ring[:, 0:tcnt * 128],
                        self.scr["fir_b"][dc][:, toff * 128:(toff + tcnt) * 128],
                        [("rfir", par)], f"m{m}f{dc}_{si}", reads=[("sc_fir",)])
                for tj in range(tcnt):
                    t = toff + tj
                    if t < FIR_S:
                        tap = t
                        off = (FIR_L - 1) - (FIR_S - 1) + tap
                        self.MMb(bls, slice(0, ML), ring[:, tj * 128:(tj + 1) * 128],
                                 self.vT[:, dc * VL + off: dc * VL + off + ML],
                                 tap == 0, tap == FIR_S - 1,
                                 [("rfir", par), ("vT", dc)], f"m{m}fs{dc}_{t}")
                    else:
                        j = t - FIR_S
                        off = j
                        self.MMb(bll, slice(0, ML), ring[:, tj * 128:(tj + 1) * 128],
                                 self.vT[:, dc * VL + off: dc * VL + off + ML],
                                 j == 0, j == FIR_L - 1,
                                 [("rfir", par), ("vT", dc)], f"m{m}fl{dc}_{j}")
                yield
            V(lambda p, d=dc, b=bls: p.tensor_copy(
                self.lsT[:, d * ML:(d + 1) * ML], self.pb[b][:, 0:ML]),
              [("pb", bls)], [("lsT", dc)], f"m{m}fse{dc}")
            self.free_bank(bls)
            V(lambda p, d=dc, b=bll: p.tensor_copy(
                self.llT[:, d * ML:(d + 1) * ML], self.pb[b][:, 0:ML]),
              [("pb", bll)], [("llT", dc)], f"m{m}fle{dc}")
            self.free_bank(bll)

    def _ph2(self, sq, m):
        V = self.V
        if m == 0 and sq == 0:
            self.tap("lsT", self.lsT[:, :], [128, NCH * ML],
                     [("lsT", c) for c in range(NCH)], BF16)
            self.tap("llT", self.llT[:, :], [128, NCH * ML],
                     [("llT", c) for c in range(NCH)], BF16)
        for dc in range(NCH):
            V(lambda p, d=dc: p.tensor_copy(
                self.vT[:, d * VL: d * VL + FIR_L - 1],
                self.vT[:, d * VL + ML: d * VL + ML + FIR_L - 1]),
              [("vT", dc)], [("vT", dc)], f"m{m}vr{dc}")

    def _ph4a(self, sq, m):
        V = self.V
        for fc in range(NCH):
            par = fc % 2
            piece = self.r_w1a[:, par * 1024:(par + 1) * 1024]
            self.ld(piece, self.scr["w1a_b"][fc], [("rw1a", par)],
                    f"m{m}gw{fc}", reads=[("sc_w1a",)])
            ba = self.nbank()
            for kc in range(NCH):
                self.MMb(ba, slice(0, ML), piece[:, kc * 128:(kc + 1) * 128],
                         self.xT_bf[:, kc * ML:(kc + 1) * ML],
                         kc == 0, kc == NCH - 1,
                         [("rw1a", par), ("xTb", kc)], f"m{m}gA{fc}{kc}")
            V(lambda p, fc=fc, b=ba: p.tensor_copy(
                self.act_bf[:, fc * ML:(fc + 1) * ML], self.pb[b][:, 0:ML]),
              [("pb", ba)], [("act", fc)], f"m{m}gE{fc}")
            self.free_bank(ba)

    def _ph3_pair(self, sq, m, pair):
        pending = [(sub, self._head_gen(m, pair * 2 + sub, h))
                   for sub in range(2) for h in range(H)]
        ph6 = self._ph6_gen(sq, m, pair)
        ph6_in = False
        active = []
        holders = 0
        done_cnt = [0, 0]
        epi_added = [False, False]
        while pending or active:
            while pending and holders < 2 and len(active) < 4:
                sub, g = pending.pop(0)
                active.append([g, True, sub])
                holders += 1
            if not pending and not ph6_in:
                active.append([ph6, False, None])
                ph6_in = True
            rec = active.pop(0)
            try:
                tok = next(rec[0])
                if tok == "free" and rec[1]:
                    rec[1] = False
                    holders -= 1
                if tok == "hold":
                    continue
                active.append(rec)
            except StopIteration:
                if rec[1]:
                    holders -= 1
                sub = rec[2]
                if sub is not None:
                    done_cnt[sub] += 1
                    if done_cnt[sub] == H and not epi_added[sub]:
                        active.append(
                            [self._sub_epi(m, pair * 2 + sub), False, None])
                        epi_added[sub] = True
        for _ in ph6:
            pass
        if m == 0 and sq == 0 and pair == 0:
            self.tap("branches0", self.branches[0][:, :], [128, H * 1024],
                     [("br", h) for h in range(H)], BF16)
            self.tap("statsT", self.statsT[0:16, :], [16, H * 128],
                     [("statsT", h) for h in range(H)], BF16)
            self.tap("probs0", self.probs[:, :], [128, H * 4],
                     [("probs", h2) for h2 in range(H)], F32)
            self.tap("on0", self.o_n[:, :], [128, 1024],
                     [("o_n", h2) for h2 in range(H)], F32)

    def _head_gen(self, m, st, h):
        V, S_ = self.V, self.S_
        sub = st % 2
        branches = self.branches[sub]
        rbr = ("br", sub, h)
        sl = h % 2
        c0, c1 = 2 * h, 2 * h + 1
        base = h * 1024
        qTs = [self.qT[:, c * ML + st * 128: c * ML + (st + 1) * 128] for c in (c0, c1)]
        kTs = [self.kT[:, c * ML + st * 128: c * ML + (st + 1) * 128] for c in (c0, c1)]
        bcol = self.beta_tm[:, st * 4 + h: st * 4 + h + 1]
        sqcol = self.sq_tm[:, 4 * h + st: 4 * h + st + 1]
        Ssl = self.S_sb[:, h * 512:(h + 1) * 512]
        rS = ("S_sb", h)
        ktm = self.ktm[:, sl * 256:(sl + 1) * 256]
        vbk = self.vbk[:, sl * 256:(sl + 1) * 256]
        ui = vbk
        attnT = self.attnT[:, sl * 128:(sl + 1) * 128]
        S_bf = self.S_bf[:, sl * 512:(sl + 1) * 512]
        pbB = self.pbB[sl]
        rk, rvb = ("ktm", sl), ("vbk", sl)
        ru = rvb
        rat, rsb, rpB = ("attnT", sl), ("S_bf", sl), ("pbB", sl)
        ch = self.chain[sl]
        rch = lambda i, sl=sl: ("ch", sl, i)

        for i, c in enumerate((c0, c1)):
            self.TRb(sl, slice(i * 128, (i + 1) * 128),
                     self.vT[:, c * VL + FIR_L - 1 + st * 128:
                             c * VL + FIR_L - 1 + (st + 1) * 128],
                     [("vT", c)], f"m{m}vt{st}{h}{i}")
        yield
        dstv = branches[:, base + 768: base + 1024]
        V(lambda p, d=dstv: p.tensor_copy(d, pbB[:, 0:256]),
          [rpB], [rbr], f"m{m}vb{st}{h}")
        yield
        for i in range(2):
            self.TRb(sl, slice(i * 128, (i + 1) * 128), kTs[i],
                     [("kT", 2 * h + i)], f"m{m}kt{st}{h}{i}")
        yield
        V(lambda p, d=ktm: p.tensor_copy(d.bitcast(F32R), pbB[:, 0:256]),
          [rpB], [rk], f"m{m}ke{st}{h}")
        yield
        for bi, (srcT, rgn) in enumerate([(self.lsT, "lsT"), (self.llT, "llT")]):
            for i, c in enumerate((c0, c1)):
                self.TRb(sl, slice((bi * 2 + i) * 128, (bi * 2 + i + 1) * 128),
                         srcT[:, c * ML + st * 128: c * ML + (st + 1) * 128],
                         [(rgn, c)], f"m{m}bt{st}{h}{bi}{i}")
        yield
        dst = branches[:, base: base + 512]
        V(lambda p, d=dst: p.tensor_copy(d, pbB[:, 0:512]),
          [rpB], [rbr], f"m{m}be{st}{h}")
        bg = self.nbank()
        for i in range(2):
            self.MMb(bg, slice(0, 128), kTs[i], kTs[i], i == 0, i == 1,
                     [("kT", 2 * h + i)], f"m{m}G{st}{h}{i}")
        for i in range(2):
            self.MMb(bg, slice(128, 256), kTs[i], qTs[i], i == 0, i == 1,
                     [("kT", 2 * h + i), ("qT", 2 * h + i)], f"m{m}A{st}{h}{i}")
        yield
        V(lambda p, b=bg: p.scalar_tensor_tensor(
            ch[0][:, 0:128], self.pb[b][:, 0:128], bcol,
            self.m_stril[:, :], Op.mult, Op.mult),
          [("pb", bg), ("beta_tm",), ("c_mtl",)], [rch(0)], f"m{m}Am{st}{h}")
        V(lambda p, b=bg, d=attnT: p.tensor_tensor(
            d.bitcast(F32R), self.pb[b][:, 128:256],
            self.m_striu[:, :], Op.mult),
          [("pb", bg), ("c_mtu",)], [rat], f"m{m}At{st}{h}")
        self.free_bank(bg)
        V(lambda p, d=S_bf, S=Ssl: p.tensor_copy(d, S),
          [rS], [rsb], f"m{m}Sb{st}{h}")
        yield
        self.TRb(sl, slice(512, 640), ch[0][:, 0:128], [rch(0)], f"m{m}Y0{st}{h}")
        yield
        V(lambda p: p.tensor_copy(ch[0][:, 128:256], pbB[:, 512:640]),
          [rpB], [rch(0)], f"m{m}Ye{st}{h}")
        V(lambda p: p.scalar_tensor_tensor(
            ch[0][:, 256:384], ch[0][:, 128:256], -1.0,
            self.ident[:, :], Op.mult, Op.add),
          [rch(0), ("c_ident",), ("c_c",)], [rch(0)], f"m{m}M0{st}{h}")
        yield
        par = 0
        for lvl in range(1, 7):
            nxt = par ^ 1
            bc = self.nbank()
            self.MMb(bc, slice(0, 128), ch[par][:, 128:256], ch[par][:, 0:128],
                     True, True, [rch(par)], f"m{m}c1_{st}{h}{lvl}")
            self.MMb(bc, slice(128, 256), ch[par][:, 0:128], ch[par][:, 128:256],
                     True, True, [rch(par)], f"m{m}c2_{st}{h}{lvl}")
            yield
            V(lambda p, b=bc, n=nxt: p.tensor_copy(ch[n][:, 0:256],
                                                   self.pb[b][:, 0:256]),
              [("pb", bc)], [rch(nxt)], f"m{m}ce{st}{h}{lvl}")
            self.MMb(bc, slice(256, 384), ch[nxt][:, 0:128], ch[par][:, 256:384],
                     True, True, [rch(nxt), rch(par)], f"m{m}c3_{st}{h}{lvl}")
            yield
            V(lambda p, b=bc, n=nxt, o=par: p.tensor_tensor(
                ch[n][:, 256:384], self.pb[b][:, 256:384],
                ch[o][:, 256:384], Op.add),
              [("pb", bc), rch(par)], [rch(nxt)], f"m{m}cm{st}{h}{lvl}")
            self.free_bank(bc)
            par = nxt
            yield
        Tt = ch[par][:, 256:384]
        for i in range(2):
            V(lambda p, i=i: p.tensor_copy(
                vbk[:, i * 128:(i + 1) * 128].bitcast(F32R), kTs[i]),
              [("kT", 2 * h + i)], [rvb], f"m{m}ku{st}{h}{i}")
        yield
        bu = self.nbank()
        for i in range(2):
            self.MMr(bu, slice(0, 256), vbk[:, i * 128:(i + 1) * 128],
                     Ssl[:, i * 256:(i + 1) * 256], i == 0, i == 1,
                     [rvb, rS], f"m{m}kS{st}{h}{i}")
        bo = self.nbank()
        for i in range(2):
            self.MMb(bo, slice(0, 256), qTs[i], S_bf[:, i * 256:(i + 1) * 256],
                     i == 0, False, [("qT", 2 * h + i), rsb], f"m{m}oS{st}{h}{i}")
        yield
        dstvb = branches[:, base + 512: base + 768]
        V(lambda p, b=bu, d=vbk, s=dstv: p.tensor_tensor(
            d.bitcast(F32R), s, self.pb[b][:, 0:256], Op.subtract),
          [("pb", bu), rbr], [rvb], f"m{m}vk0{st}{h}")
        V(lambda p, d=dstvb, s=vbk: p.tensor_scalar(
            d, s, bcol, None, Op.mult),
          [rvb, ("beta_tm",)], [rbr], f"m{m}vk{st}{h}")
        yield
        self.MMb(bu, slice(256, 512), Tt, dstvb, True, True,
                 [rch(par), rbr], f"m{m}u{st}{h}")
        yield
        V(lambda p, b=bu, d=ui: p.tensor_copy(d.bitcast(F32R),
                                              self.pb[b][:, 256:512]),
          [("pb", bu)], [ru], f"m{m}ue{st}{h}")
        self.free_bank(bu)
        yield
        self.MMr(bo, slice(0, 256), attnT, ui, False, True,
                 [rat, ru], f"m{m}oA{st}{h}")
        yield
        dsto = branches[:, base + 512: base + 768]
        V(lambda p, d=dsto, b=bo: p.tensor_scalar(
            d, self.pb[b][:, 0:256], sqcol, None, Op.mult),
          [("pb", bo), ("sq_tm",)], [rbr], f"m{m}oe{st}{h}")
        self.free_bank(bo)
        yield
        bs = self.nbank()
        for i in range(2):
            self.MMr(bs, slice(i * 256, (i + 1) * 256), self.ident_r[:, :],
                     Ssl[:, i * 256:(i + 1) * 256], True, False,
                     [("c_identr",), rS], f"m{m}Sr{st}{h}{i}")
            self.MMr(bs, slice(i * 256, (i + 1) * 256),
                     ktm[:, i * 128:(i + 1) * 128], ui,
                     False, True, [rk, ru], f"m{m}Su{st}{h}{i}")
        yield
        V(lambda p, b=bs, S=Ssl: p.tensor_copy(S.bitcast(F32R),
                                               self.pb[b][:, 0:512]),
          [("pb", bs)], [rS], f"m{m}Sc{st}{h}")
        self.free_bank(bs)
        yield
        br = branches[:, base: base + 1024]
        s8, s16 = sl * 8, sl * 16
        stv = self.stats_tm
        rst, rcs, rtm = ("stat", sl), ("colsc", sl), ("stats_tm", sl)
        V(lambda p, br=br: p.tensor_reduce(
            self.stat_sc[:, s8:s8 + 4], br.rearrange("p (b n) -> p b n", b=4),
            AXX, Op.add),
          [rbr], [rst], f"m{m}ts{st}{h}")
        V(lambda p, br=br: p.tensor_reduce(
            self.stat_sc[:, s8 + 4:s8 + 8], br.rearrange("p (b n) -> p b n", b=4),
            AXX, Op.add, apply_absolute_value=True),
          [rbr], [rst], f"m{m}ta{st}{h}")
        for b4 in range(4):
            S_(lambda p, br=br, b4=b4: p.activation(
                self.sqsc[:, 0:256].bitcast(F32R), br[:, b4 * 256:(b4 + 1) * 256],
                AF.Square, accum_out=self.colsc[:, s8 + b4:s8 + b4 + 1]),
               [rbr], [("sqsc", 0), rcs], f"m{m}tq{st}{h}{b4}")
        yield
        stv16 = stv[:, s16:s16 + 16].rearrange("p (b j) -> p b j", j=4)
        jm, jv = stv16[:, :, 0:1], stv16[:, :, 1:2]
        ja, jl = stv16[:, :, 2:3], stv16[:, :, 3:4]
        sum4 = self.stat_sc[:, s8:s8 + 4].unsqueeze(2)
        abs4 = self.stat_sc[:, s8 + 4:s8 + 8].unsqueeze(2)
        sq4 = self.colsc[:, s8:s8 + 4].unsqueeze(2)
        m2t = self.colsc[:, s8 + 4:s8 + 8].unsqueeze(2)
        V(lambda p: p.tensor_scalar(jm, sum4, 1.0 / 256.0, None, Op.mult),
          [rst, ("c_c",)], [rtm], f"m{m}tm{st}{h}")
        V(lambda p: p.tensor_tensor(m2t, jm, jm, Op.mult),
          [rtm], [rcs], f"m{m}t2{st}{h}")
        V(lambda p: p.scalar_tensor_tensor(
            jv, sq4, 1.0 / 256.0, m2t, Op.mult, Op.subtract),
          [rcs, ("c_c",)], [rtm], f"m{m}tv{st}{h}")
        V(lambda p: p.tensor_scalar(ja, abs4, 1.0 / 256.0, None, Op.mult),
          [rst, ("c_c",)], [rtm], f"m{m}tb{st}{h}")
        S_(lambda p: p.activation(jl, sq4, AF.Sqrt),
           [rcs], [rtm], f"m{m}tl{st}{h}")
        yield
        bt = self.nbank()
        self.TRf(bt, slice(0, 128), stv[:, s16:s16 + 16], [rtm],
                 f"m{m}tt{st}{h}", rows=16)
        yield
        V(lambda p, b=bt, h=h, sub=sub: p.tensor_copy(
            self.statsT[0:16, sub * 512 + h * 128: sub * 512 + (h + 1) * 128],
            self.pb[b][0:16, 0:128]),
          [("pb", bt)], [("statsT", sub, h)], f"m{m}te{st}{h}")
        self.free_bank(bt)
        yield "free"

    def _sub_epi(self, m, st):
        V, S_ = self.V, self.S_
        P0 = st * 128
        sub = st % 2
        branches = self.branches[sub]
        rstT = [("statsT", sub, h) for h in range(H)]
        blg = self.nbank()
        for fc in range(NCH):
            hb = self.nbank()
            self.s.op("T", lambda p, fc=fc, b=hb: p.matmul(
                self.pb[b][:, 0:512],
                self.w1b[0:16, fc * 128:(fc + 1) * 128],
                self.statsT[0:16, sub * 512: sub * 512 + 512],
                start=True, stop=True),
                reads=[("c_w1b",)] + rstT, writes=[("pb", hb)],
                tag=f"m{m}gB{st}{fc}")
            yield
            hgs = self.hg[:, (fc % 2) * 512:(fc % 2 + 1) * 512]
            t1b = self.act_bf[:, fc * ML + P0: fc * ML + P0 + 128]
            t13 = t1b.unsqueeze(1).to_broadcast([128, 4, 128])
            V(lambda p, o=hgs, b=hb, t=t13: p.tensor_tensor(
                o.rearrange("p (h n) -> p h n", h=4),
                self.pb[b][:, 0:512].rearrange("p (h n) -> p h n", h=4),
                t, Op.add),
              [("pb", hb), ("act", fc)], [("hg", fc % 2)], f"m{m}gd{st}{fc}")
            self.free_bank(hb)
            S_(lambda p, o=hgs, fc=fc: p.activation(
                o, o, AF.Gelu, bias=self.b1g[:, fc:fc + 1]),
               [("hg", fc % 2), ("c_b1g",)], [("hg", fc % 2)], f"m{m}gg{st}{fc}")
            yield
            self.s.op("T", lambda p, fc=fc, o=hgs, b=blg: p.matmul(
                self.pb[b][0:4, 0:512],
                self.w2g_bf[:, fc * 4:(fc + 1) * 4],
                o, start=(fc == 0), stop=(fc == NCH - 1)),
                reads=[("c_w2gb",), ("hg", fc % 2)],
                writes=[("pb", blg)], tag=f"m{m}gl{st}{fc}")
            yield
        lgs = self.beTa[0:4, 0:512]
        S_(lambda p, b=blg, d=lgs: p.activation(
            d, self.pb[b][0:4, 0:512], AF.Identity, bias=self.b2g[0:4, 0:1]),
           [("pb", blg), ("c_b2g",)], [("beTa",)], f"m{m}lb{st}")
        self.free_bank(blg)
        yield
        bt2 = self.nbank()
        for h in range(H):
            self.TRf(bt2, slice(h * 4, (h + 1) * 4),
                     lgs[0:4, h * 128:(h + 1) * 128],
                     [("beTa",)], f"m{m}lt{st}{h}", pk=4)
        yield
        c = self.scz
        rcz = ("sczS",)
        V(lambda p, b=bt2: p.tensor_copy(c[:, 0:16], self.pb[b][:, 0:16]),
          [("pb", bt2)], [rcz], f"m{m}le{st}")
        self.free_bank(bt2)
        z3 = c[:, 0:16].rearrange("p (h b) -> p h b", h=4)
        V(lambda p: p.tensor_reduce(c[:, 16:20], z3, AXX, Op.max, negate=True),
          [rcz], [rcz], f"m{m}l1{st}")
        nm3 = c[:, 16:20].unsqueeze(2).to_broadcast([128, 4, 4])
        V(lambda p: p.tensor_tensor(
            c[:, 20:36].rearrange("p (h b) -> p h b", h=4), z3, nm3, Op.add),
          [rcz], [rcz], f"m{m}l2{st}")
        V(lambda p: p.tensor_scalar(c[:, 20:36], c[:, 20:36],
                                    self.invt[:, 0:1], None, Op.mult),
          [rcz, ("c_invt",)], [rcz], f"m{m}l2b{st}")
        yield
        S_(lambda p: p.activation(c[:, 20:36], c[:, 20:36], AF.Exp),
           [rcz], [rcz], f"m{m}l3{st}")
        e3 = c[:, 20:36].rearrange("p (h b) -> p h b", h=4)
        V(lambda p: p.tensor_reduce(c[:, 36:40], e3, AXX, Op.add),
          [rcz], [rcz], f"m{m}l4{st}")
        V(lambda p: p.reciprocal(c[:, 36:40], c[:, 36:40]),
          [rcz], [rcz], f"m{m}l5{st}")
        rc3 = c[:, 36:40].unsqueeze(2).to_broadcast([128, 4, 4])
        V(lambda p: p.scalar_tensor_tensor(
            c[:, 20:36].rearrange("p (h b) -> p h b", h=4), e3,
            1.0 - 2.0 * FLOOR_EPS, rc3, Op.mult, Op.mult),
          [rcz, ("c_c",)], [rcz], f"m{m}l6{st}")
        fl3 = self.floor_row[:, 0:4].unsqueeze(1).to_broadcast([128, 4, 4])
        V(lambda p: p.tensor_tensor(
            self.probs[:, 0:16].rearrange("p (h b) -> p h b", h=4), e3, fl3,
            Op.add),
          [rcz, ("c_floor",)], [("probs", h2) for h2 in range(H)],
          f"m{m}l7{st}")
        yield
        c3 = c[:, 96:128].rearrange("p (h x) -> p h x", h=4)
        V(lambda p: p.tensor_tensor(
            c3[:, :, 0:1], self.probs[:, 0:16].rearrange(
                "p (h b) -> p h b", h=4)[:, :, 0:1],
            self.sigc[:, 0:4].unsqueeze(2), Op.add),
          [("probs", h2) for h2 in range(H)] + [("c_sigc",)], [rcz],
          f"m{m}mc{st}")
        for h in range(H):
            base = h * 1024
            br = branches[:, base: base + 1024]
            pr = self.probs[:, h * 4:h * 4 + 4]
            cb = 96 + h * 8
            don = self.o_n[:, h * 256:(h + 1) * 256]
            V(lambda p, br=br, cb=cb, don=don: p.tensor_scalar(
                don, br[:, 0:256], c[:, cb:cb + 1], None, Op.mult),
              [("br", sub, h), rcz], [("o_n", h)], f"m{m}m0{st}{h}")
            for bi in range(1, 4):
                V(lambda p, br=br, pr=pr, bi=bi, don=don: p.scalar_tensor_tensor(
                    don, br[:, bi * 256:(bi + 1) * 256],
                    pr[:, bi:bi + 1], don, Op.mult, Op.add),
                  [("br", sub, h), ("probs", h), ("o_n", h)], [("o_n", h)],
                  f"m{m}m{bi}{st}{h}")
            yield
        for h in range(H):
            cb = 96 + h * 8
            don = self.o_n[:, h * 256:(h + 1) * 256]
            S_(lambda p, cb=cb, don=don: p.activation(
                self.sqsc[:, (h % 2) * 256:(h % 2) * 256 + 256].bitcast(F32R),
                don, AF.Square, accum_out=c[:, cb + 1:cb + 2]),
               [("o_n", h)], [("sqsc", h % 2), rcz], f"m{m}mq{st}{h}")
        yield
        S_(lambda p: p.activation(c3[:, :, 2:3], c3[:, :, 1:2],
                                  AF.Sqrt, scale=1.0 / 256.0, bias=1e-5),
           [rcz, ("c_c",)], [rcz], f"m{m}ms{st}")
        V(lambda p: p.reciprocal(c3[:, :, 3:4], c3[:, :, 2:3]),
          [rcz], [rcz], f"m{m}mr{st}")
        yield
        for h in range(H):
            cb = 96 + h * 8
            don = self.o_n[:, h * 256:(h + 1) * 256]
            V(lambda p, cb=cb, don=don: p.scalar_tensor_tensor(
                don, don, c[:, cb + 3:cb + 4],
                self.rms_bc[:, :], Op.mult, Op.mult),
              [rcz, ("c_rmsbc",), ("o_n", h)], [("o_n", h)], f"m{m}mo{st}{h}")
            yield
            bo2 = self.nbank()
            for i, cchunk in enumerate((2 * h, 2 * h + 1)):
                self.TRf(bo2, slice(i * 128, (i + 1) * 128),
                         self.o_n[:, cchunk * 128:(cchunk + 1) * 128],
                         [("o_n", h)], f"m{m}ot{st}{cchunk}")
            yield
            for i, cchunk in enumerate((2 * h, 2 * h + 1)):
                dstT = self.onT[:, cchunk * 256 + (st % 2) * 128:
                                cchunk * 256 + (st % 2 + 1) * 128]
                S_(lambda p, d=dstT, b=bo2, i=i: p.copy(
                    d, self.pb[b][:, i * 128:(i + 1) * 128]),
                   [("pb", bo2)], [("onT", cchunk)], f"m{m}oe{st}{cchunk}")
            self.free_bank(bo2)
            yield

    def _ph6_gen(self, sq, m, pair):
        V, S_ = self.V, self.S_
        t0 = m * ML
        P0 = pair * 256
        NT = 256
        for c in range(NCH):
            self.ld(self.xT_pair[:, c * 256:(c + 1) * 256],
                    self.x_in[sq, c, :, t0 + P0: t0 + P0 + 256],
                    [("xTp", c)], f"m{m}rx{pair}{c}")
        yield
        for dc in range(2):
            piece = self.r_wo[:, (dc % 2) * 1024:((dc % 2) + 1) * 1024]
            self.ld(piece, self.scr["wo_b"][dc], [("rwo", dc % 2)],
                    f"m{m}o{pair}w{dc}", reads=[("sc_wo",)])
        yield "hold"
        bsum = self.nbank()
        bsq = self.nbank()
        for dc in range(NCH):
            par = dc % 2
            piece = self.r_wo[:, par * 1024:(par + 1) * 1024]
            if dc >= 2:
                self.ld(piece, self.scr["wo_b"][dc], [("rwo", par)],
                        f"m{m}o{pair}w{dc}", reads=[("sc_wo",)])
            bo = self.nbank()
            for kc in range(NCH):
                self.MMb(bo, slice(0, NT), piece[:, kc * 128:(kc + 1) * 128],
                         self.onT[:, kc * 256: kc * 256 + NT],
                         kc == 0, kc == NCH - 1,
                         [("rwo", par), ("onT", kc)], f"m{m}wo{pair}{dc}{kc}")
            x1 = self.acc[:, dc * ML + P0: dc * ML + P0 + NT]
            V(lambda p, x1=x1, b=bo, dc=dc: p.tensor_tensor(
                x1, self.pb[b][:, 0:NT], self.xT_pair[:, dc * 256:(dc + 1) * 256],
                Op.add),
              [("pb", bo), ("xTp", dc)], [("acc", dc)], f"m{m}x1{pair}{dc}")
            self.free_bank(bo)
            S_(lambda p, x1=x1: p.copy(self.lnsc[:, 0:NT].bitcast(F32R), x1),
               [("acc", dc)], [("lnsc",)], f"m{m}sc{pair}{dc}")
            self.MMr(bsum, slice(0, NT), self.onesc_r[:, 0:1], self.lnsc[:, 0:NT],
                     dc == 0, dc == NCH - 1, [("c_onescr",), ("lnsc",)],
                     f"m{m}s1{pair}{dc}", mrows=1)
            S_(lambda p, x1=x1: p.activation(self.sqsc[:, 0:NT].bitcast(F32R),
                                             x1, AF.Square),
               [("acc", dc)], [("sqsc", 0), ("sqsc", 1)], f"m{m}sq{pair}{dc}")
            self.MMr(bsq, slice(0, NT), self.onesc_r[:, 0:1], self.sqsc[:, 0:NT],
                     dc == 0, dc == NCH - 1, [("c_onescr",), ("sqsc", 0), ("sqsc", 1)],
                     f"m{m}s2{pair}{dc}", mrows=1)
        self._ln_rows(m, pair, bsum, bsq, NT, ffn_r=True)
        self.free_bank(bsum)
        self.free_bank(bsq)
        bs1 = self.nbank()
        bs2 = self.nbank()
        bs3 = self.nbank()
        r = self.row1
        self.bc_row(bs1, slice(0, NT), r[0:1, 768:768 + NT], [("row1",)],
                    f"m{m}bs{pair}")
        self.bc_row(bs2, slice(0, NT), r[0:1, 1024:1024 + NT], [("row1",)],
                    f"m{m}bm{pair}")
        self.bc_row(bs3, slice(0, NT), r[0:1, 256:256 + NT], [("row1",)],
                    f"m{m}br{pair}")
        bg = self.nbank()
        for dc in range(NCH):
            hpar = dc % 2
            x1 = self.acc[:, dc * ML + P0: dc * ML + P0 + NT]
            V(lambda p, x1=x1: p.tensor_tensor(
                self.scx[:, 0:NT], x1, self.pb[bs1][:, 0:NT], Op.mult),
              [("pb", bs1), ("acc", dc)], [("scx", 0)], f"m{m}n1{pair}{dc}")
            V(lambda p, x1=x1: p.tensor_tensor(
                x1, self.scx[:, 0:NT], self.pb[bs2][:, 0:NT], Op.subtract),
              [("pb", bs2), ("scx", 0)], [("acc", dc)], f"m{m}n2{pair}{dc}")
            hsl = self.hrot[:, hpar * 256:(hpar + 1) * 256]
            V(lambda p, x1=x1, o=hsl, dc=dc: p.scalar_tensor_tensor(
                o.bitcast(F32R), x1, self.normw[:, dc:dc + 1],
                self.pb[bs3][:, 0:NT], Op.mult, Op.mult),
              [("pb", bs3), ("acc", dc), ("c_nw",)], [("hrot", hpar)],
              f"m{m}h{pair}{dc}")
            hb = self.h_bf[:, dc * ML + P0: dc * ML + P0 + NT]
            S_(lambda p, o=hb, i=hsl: p.copy(o, i),
               [("hrot", hpar)], [("h_bf", dc)], f"m{m}hb{pair}{dc}")
            self.MMr(bg, slice(0, NT), self.gatewt_r[:, dc * 8:(dc + 1) * 8], hsl,
                     dc == 0, dc == NCH - 1, [("c_gwr",), ("hrot", hpar)],
                     f"m{m}fg{pair}{dc}", mrows=8)
        self.free_bank(bs1)
        self.free_bank(bs2)
        self.free_bank(bs3)
        S_(lambda p, b=bg: p.activation(self.fglb[0:8, 0:NT], self.pb[b][0:8, 0:NT],
                                        AF.Identity, bias=self.gateb[0:8, 0:1]),
           [("pb", bg), ("c_gb",)], [("fglb",)], f"m{m}fb{pair}")
        self.free_bank(bg)
        st0 = pair * 2
        bt = self.nbank()
        for sub in range(2):
            self.TRf(bt, slice(sub * 8, sub * 8 + 8),
                     self.fglb[0:8, sub * 128:(sub + 1) * 128],
                     [("fglb",)], f"m{m}ft{pair}{sub}", pk=8)
        gl2 = self.gl_sb[:, st0 * 8:(st0 + 2) * 8]
        rgl = [("gl", st0), ("gl", st0 + 1)]
        V(lambda p, b=bt: p.tensor_copy(gl2, self.pb[b][:, 0:16]),
          [("pb", bt)], rgl, f"m{m}fe{pair}")
        self.free_bank(bt)
        c = self.scz
        gl3 = gl2.rearrange("p (s e) -> p s e", s=2)
        mk13 = c[:, 40:56].rearrange("p (s e) -> p s e", s=2)
        l23 = c[:, 56:72].rearrange("p (s e) -> p s e", s=2)
        m1b = c[:, 72:74].unsqueeze(2).to_broadcast([128, 2, 8])
        m2b = c[:, 74:76].unsqueeze(2).to_broadcast([128, 2, 8])
        rcb = c[:, 82:84].unsqueeze(2).to_broadcast([128, 2, 8])
        wb_ = c[:, 84:86].unsqueeze(2).to_broadcast([128, 2, 8])
        V(lambda p: p.tensor_reduce(c[:, 72:74], gl3, AXX, Op.max),
          rgl, [("scz6",)], f"m{m}u1{pair}")
        V(lambda p: p.tensor_tensor(mk13, gl3, m1b, Op.is_equal),
          rgl + [("scz6",)], [("scz6",)], f"m{m}u2{pair}")
        V(lambda p: p.scalar_tensor_tensor(l23, mk13, -1e30, gl3,
                                           Op.mult, Op.add),
          rgl + [("scz6",), ("c_c",)], [("scz6",)], f"m{m}u3{pair}")
        V(lambda p: p.tensor_reduce(c[:, 74:76], l23, AXX, Op.max),
          [("scz6",)], [("scz6",)], f"m{m}u4{pair}")
        V(lambda p: p.tensor_tensor(l23, l23, m2b, Op.is_equal),
          [("scz6",)], [("scz6",)], f"m{m}u5{pair}")
        V(lambda p: p.tensor_tensor(c[:, 76:78], c[:, 74:76], c[:, 72:74],
                                    Op.subtract),
          [("scz6",)], [("scz6",)], f"m{m}u6{pair}")
        S_(lambda p: p.activation(c[:, 78:80], c[:, 76:78], AF.Exp),
           [("scz6",)], [("scz6",)], f"m{m}u7{pair}")
        V(lambda p: p.tensor_scalar(c[:, 80:82], c[:, 78:80], 1.0, None, Op.add),
          [("scz6",), ("c_c",)], [("scz6",)], f"m{m}u8{pair}")
        V(lambda p: p.reciprocal(c[:, 82:84], c[:, 80:82]),
          [("scz6",)], [("scz6",)], f"m{m}u9{pair}")
        V(lambda p: p.tensor_tensor(c[:, 84:86], c[:, 78:80], c[:, 82:84],
                                    Op.mult),
          [("scz6",)], [("scz6",)], f"m{m}ua{pair}")
        ps3 = self.p8[:, st0 * 8:(st0 + 2) * 8].rearrange("p (s e) -> p s e",
                                                          s=2)
        rp8 = [("p8", st0), ("p8", st0 + 1)]
        V(lambda p: p.tensor_tensor(ps3, mk13, rcb, Op.mult),
          [("scz6",)], rp8, f"m{m}ub{pair}")
        V(lambda p: p.tensor_tensor(mk13, l23, wb_, Op.mult),
          [("scz6",)], [("scz6",)], f"m{m}ut{pair}")
        V(lambda p: p.tensor_tensor(ps3, ps3, mk13, Op.add),
          [("scz6",)] + rp8, rp8, f"m{m}uc{pair}")
        bt2 = self.nbank()
        for sub in range(2):
            st = st0 + sub
            self.TRf(bt2, slice(sub * 128, (sub + 1) * 128),
                     self.p8[:, st * 8:(st + 1) * 8],
                     [("p8", st)], f"m{m}ud{st}", rows=8)
        for sub in range(2):
            st = st0 + sub
            V(lambda p, b=bt2, st=st, sub=sub: p.tensor_copy(
                self.p8T_bf[0:8, st * 128:(st + 1) * 128],
                self.pb[b][0:8, sub * 128:(sub + 1) * 128]),
              [("pb", bt2)], [("p8Tb",)], f"m{m}ue2{st}")
        self.free_bank(bt2)
        if m == 0 and sq == 0 and pair == 1:
            self.tap("acc1", self.acc[:, :], [128, NCH * ML],
                     [("acc", dc2) for dc2 in range(NCH)], F32)
            self.tap("hbf", self.h_bf[:, :], [128, NCH * ML],
                     [("h_bf", dc2) for dc2 in range(NCH)], BF16)
            self.tap("p8t", self.p8T_bf[0:8, :], [8, ML], [("p8Tb",)], BF16)

    def _ln_rows(self, m, pair, bsum, bsq, NT, ffn_r):
        V, S_ = self.V, self.S_
        r = self.row1
        V(lambda p: p.tensor_scalar(r[0:1, 0:NT], self.pb[bsum][0:1, 0:NT],
                                    1.0 / 1024.0, None, Op.mult),
          [("pb", bsum), ("c_c",)], [("row1",)], f"m{m}ln1{pair}")
        V(lambda p: p.tensor_tensor(r[0:1, 256:256 + NT], r[0:1, 0:NT],
                                    r[0:1, 0:NT], Op.mult),
          [("row1",)], [("row1",)], f"m{m}ln2{pair}")
        V(lambda p: p.scalar_tensor_tensor(
            r[0:1, 512:512 + NT], self.pb[bsq][0:1, 0:NT], 1.0 / 1024.0,
            r[0:1, 256:256 + NT], Op.mult, Op.subtract),
          [("pb", bsq), ("row1",), ("c_c",)], [("row1",)], f"m{m}ln3{pair}")
        S_(lambda p: p.activation(r[0:1, 768:768 + NT], r[0:1, 512:512 + NT],
                                  AF.Sqrt, bias=1e-5),
           [("row1",), ("c_c",)], [("row1",)], f"m{m}ln4{pair}")
        V(lambda p: p.reciprocal(r[0:1, 768:768 + NT], r[0:1, 768:768 + NT]),
          [("row1",)], [("row1",)], f"m{m}ln5{pair}")
        S_(lambda p: p.copy(r[0:1, 768:768 + NT].bitcast(F32R),
                            r[0:1, 768:768 + NT]),
           [("row1",)], [("row1",)], f"m{m}ln5r{pair}")
        V(lambda p: p.tensor_tensor(r[0:1, 1024:1024 + NT].bitcast(F32R),
                                    r[0:1, 0:NT], r[0:1, 768:768 + NT], Op.mult),
          [("row1",)], [("row1",)], f"m{m}ln6{pair}")
        if ffn_r:
            V(lambda p: p.tensor_tensor(r[0:1, 256:256 + NT],
                                        r[0:1, 768:768 + NT],
                                        r[0:1, 768:768 + NT], Op.mult),
              [("row1",)], [("row1",)], f"m{m}ln7{pair}")
            V(lambda p: p.tensor_tensor(r[0:1, 256:256 + NT],
                                        r[0:1, 256:256 + NT],
                                        r[0:1, 512:512 + NT], Op.mult),
              [("row1",)], [("row1",)], f"m{m}ln8{pair}")
            S_(lambda p: p.activation(r[0:1, 256:256 + NT],
                                      r[0:1, 256:256 + NT], AF.Sqrt, bias=F32EPS),
               [("row1",), ("c_c",)], [("row1",)], f"m{m}ln9{pair}")
            V(lambda p: p.reciprocal(r[0:1, 256:256 + NT], r[0:1, 256:256 + NT]),
              [("row1",)], [("row1",)], f"m{m}lnA{pair}")
            V(lambda p: p.tensor_scalar(r[0:1, 768:768 + NT].bitcast(F32R),
                                        r[0:1, 768:768 + NT], 2.0, None, Op.mult),
              [("row1",), ("c_c",)], [("row1",)], f"m{m}lnB{pair}")
            V(lambda p: p.tensor_scalar(r[0:1, 1024:1024 + NT].bitcast(F32R),
                                        r[0:1, 1024:1024 + NT], 2.0, None,
                                        Op.mult),
              [("row1",), ("c_c",)], [("row1",)], f"m{m}lnC{pair}")
            V(lambda p: p.tensor_scalar(r[0:1, 256:256 + NT].bitcast(F32R),
                                        r[0:1, 256:256 + NT], 0.5, None, Op.mult),
              [("row1",), ("c_c",)], [("row1",)], f"m{m}lnD{pair}")

    def _ph7(self, sq, m):
        V, S_ = self.V, self.S_
        for e in range(NEXP):
            if True:
                bpe = self.nbank()
                self.MMb(bpe, slice(0, ML), self.sel8[0:8, e * 128:(e + 1) * 128],
                         self.p8T_bf[0:8, :], True, True,
                         [("c_sel8",), ("p8Tb",)], f"m{m}e{e}b")
                V(lambda p, b=bpe: p.tensor_scalar(
                    self.p_be[:, :], self.pb[b][:, 0:ML], 1.0 / ALPHA, None,
                    Op.mult),
                  [("pb", bpe), ("c_c",)], [("p_be",)], f"m{m}e{e}p")
                self.free_bank(bpe)
                for fc in range(NCH):
                    s1 = (e * NCH * 2 + 2 * fc) % 3
                    pg = self.r_f1[:, s1 * 1024:(s1 + 1) * 1024]
                    self.ld(pg, self.scr["fw1_b"][e, fc], [("rf1", s1)],
                            f"m{m}e{e}wg{fc}", reads=[("sc_f1",)])
                    bG = self.nbank()
                    for kc in range(NCH):
                        self.MMb(bG, slice(0, ML), pg[:, kc * 128:(kc + 1) * 128],
                                 self.h_bf[:, kc * ML:(kc + 1) * ML],
                                 kc == 0, kc == NCH - 1,
                                 [("rf1", s1), ("h_bf", kc)], f"m{m}e{e}G{fc}{kc}")
                    s1b = (e * NCH * 2 + 2 * fc + 1) % 3
                    pl = self.r_f1[:, s1b * 1024:(s1b + 1) * 1024]
                    self.ld(pl, self.scr["fw1_b"][e, NCH + fc], [("rf1", s1b)],
                            f"m{m}e{e}wl{fc}", reads=[("sc_f1",)])
                    bL = self.nbank()
                    for kc in range(NCH):
                        self.MMb(bL, slice(0, ML), pl[:, kc * 128:(kc + 1) * 128],
                                 self.h_bf[:, kc * ML:(kc + 1) * ML],
                                 kc == 0, kc == NCH - 1,
                                 [("rf1", s1b), ("h_bf", kc)], f"m{m}e{e}L{fc}{kc}")
                    S_(lambda p, b=bG, e=e, fc=fc: p.activation(
                        self.scx[:, :], self.pb[b][:, 0:ML], AF.Silu, scale=ALPHA,
                        bias=self.fb1s[:, e * 8 + fc:e * 8 + fc + 1]),
                       [("pb", bG), ("c_fb1s",)], [("scx", 0), ("scx", 1)],
                       f"m{m}e{e}s1{fc}")
                    self.free_bank(bG)
                    S_(lambda p, b=bL, e=e, fc=fc: p.activation(
                        self.scy[:, :], self.pb[b][:, 0:ML], AF.Identity,
                        bias=self.fb1p[:, e * 8 + fc:e * 8 + fc + 1]),
                       [("pb", bL), ("c_fb1p",)], [("scy",)], f"m{m}e{e}s2{fc}")
                    self.free_bank(bL)
                    V(lambda p: p.tensor_scalar(self.scy[:, :], self.scy[:, :],
                                                8.0, -6.0, Op.min, Op.max),
                      [("scy",), ("c_c",)], [("scy",)], f"m{m}e{e}s3{fc}")
                    V(lambda p: p.tensor_tensor(
                        self.scy[:, :], self.scy[:, :], self.p_be[:, :], Op.mult),
                      [("scy",), ("p_be",)], [("scy",)], f"m{m}e{e}s4{fc}")
                    act = self.act_bf[:, fc * ML:(fc + 1) * ML]
                    V(lambda p, a=act: p.scalar_tensor_tensor(
                        a, self.scx[:, :], SWISH_CAP, self.scy[:, :],
                        Op.min, Op.mult),
                      [("scx", 0), ("scx", 1), ("scy",), ("c_c",)], [("act", fc)],
                      f"m{m}e{e}s5{fc}")
                for g in range(2):
                    m2banks = [self.nbank() for _ in range(4)]
                    for dj in range(4):
                        dd = g * 4 + dj
                        s2 = (e * 4 + dj) % 2
                        pw = self.r_fw2[:, s2 * 1024:(s2 + 1) * 1024]
                        self.ld(pw, self.scr["fw2_b"][e, dd], [("rfw2", s2)],
                                f"m{m}e{e}w2{dd}", reads=[("sc_f2",)])
                        bank = m2banks[dj]
                        cols = slice(0, 512)
                        if e == 0:
                            self.MMb(bank, cols,
                                     self.fb2_bf[0:8, dd * 128:(dd + 1) * 128],
                                     self.p8T_bf[0:8, 0:512],
                                     True, False,
                                     [("c_fb2b",), ("p8Tb",)],
                                     f"m{m}b2_{dd}")
                        for ic in range(NCH):
                            first = (e != 0 and ic == 0)
                            last = (ic == NCH - 1)
                            self.MMb(bank, cols,
                                     pw[:, ic * 128:(ic + 1) * 128],
                                     self.act_bf[:, ic * ML: ic * ML + 512],
                                     first, last,
                                     [("rfw2", s2), ("act", ic)],
                                     f"m{m}e{e}m2_{dd}_{ic}")
                    for dj in range(4):
                        dd = g * 4 + dj
                        bank = m2banks[dj]
                        x2 = self.acc[:, dd * ML: dd * ML + 512]
                        V(lambda p, x2=x2, b=bank: p.tensor_tensor(
                            x2, x2, self.pb[b][:, 0:512], Op.add),
                          [("pb", bank), ("acc", dd)], [("acc", dd)],
                          f"m{m}fa{e}_{dd}")
                        self.free_bank(bank)

    def _ph8(self, sq, m):
        V, S_ = self.V, self.S_
        t0 = m * ML
        if m == 0 and sq == 0:
            self.tap("acc2", self.acc[:, :], [128, NCH * ML],
                     [("acc", dc2) for dc2 in range(NCH)], F32)
        for pair in range(2):
            P0 = pair * 256
            NT = 256
            bsum = self.nbank()
            bsq = self.nbank()
            for dc in range(NCH):
                x2 = self.acc[:, dc * ML + P0: dc * ML + P0 + NT]
                S_(lambda p, x2=x2: p.copy(self.lnsc[:, 0:NT].bitcast(F32R), x2),
                   [("acc", dc)], [("lnsc",)], f"m{m}zc{pair}{dc}")
                self.MMr(bsum, slice(0, NT), self.onesc_r[:, 0:1],
                         self.lnsc[:, 0:NT], dc == 0, dc == NCH - 1,
                         [("c_onescr",), ("lnsc",)], f"m{m}z1{pair}{dc}", mrows=1)
                S_(lambda p, x2=x2: p.activation(self.sqsc[:, 0:NT].bitcast(F32R),
                                                 x2, AF.Square),
                   [("acc", dc)], [("sqsc", 0), ("sqsc", 1)], f"m{m}zq{pair}{dc}")
                self.MMr(bsq, slice(0, NT), self.onesc_r[:, 0:1], self.sqsc[:, 0:NT],
                         dc == 0, dc == NCH - 1, [("c_onescr",), ("sqsc", 0), ("sqsc", 1)],
                         f"m{m}z2{pair}{dc}", mrows=1)
            self._ln_rows(m, 10 + pair, bsum, bsq, NT, ffn_r=False)
            self.free_bank(bsum)
            self.free_bank(bsq)
            bs1 = self.nbank()
            bs2 = self.nbank()
            r = self.row1
            self.bc_row(bs1, slice(0, NT), r[0:1, 768:768 + NT], [("row1",)],
                        f"m{m}zb{pair}")
            self.bc_row(bs2, slice(0, NT), r[0:1, 1024:1024 + NT], [("row1",)],
                        f"m{m}zm{pair}")
            for sub in range(2):
                st = pair * 2 + sub
                for half in range(2):
                    bank = self.nbank()
                    accs = self.acc[:, half * 4 * ML:(half + 1) * 4 * ML]
                    x23 = accs.rearrange("p (d n) -> p d n", n=ML)[
                        :, :, st * 128:(st + 1) * 128]
                    racc = [("acc", half * 4 + c4) for c4 in range(4)]
                    bb1 = self.pb[bs1][:, sub * 128:(sub + 1) * 128]
                    bb2 = self.pb[bs2][:, sub * 128:(sub + 1) * 128]
                    V(lambda p, x23=x23, bb1=bb1: p.tensor_tensor(
                        self.scx[:, 0:512].rearrange("p (d n) -> p d n", d=4),
                        x23, bb1.unsqueeze(1).to_broadcast([128, 4, 128]),
                        Op.mult),
                      [("pb", bs1)] + racc, [("scx", 0), ("scx", 1)],
                      f"m{m}za{st}{half}")
                    V(lambda p, bb2=bb2: p.tensor_tensor(
                        self.scy[:, 0:512].rearrange("p (d n) -> p d n", d=4),
                        self.scx[:, 0:512].rearrange("p (d n) -> p d n", d=4),
                        bb2.unsqueeze(1).to_broadcast([128, 4, 128]),
                        Op.subtract),
                      [("pb", bs2), ("scx", 0), ("scx", 1)], [("scy",)],
                      f"m{m}zs{st}{half}")
                    for c4 in range(4):
                        self.TRf(bank, slice(c4 * 128, (c4 + 1) * 128),
                                 self.scy[:, c4 * 128:(c4 + 1) * 128],
                                 [("scy",)], f"m{m}zt{st}{half}{c4}")
                    if half == 0:
                        stg, rstg = self.scy, [("scy",)]
                    else:
                        stg, rstg = self.scx, [("scx", 0), ("scx", 1)]
                    V(lambda p, b=bank, stg=stg: p.tensor_copy(
                        stg[:, 0:512], self.pb[b][:, 0:512]),
                      [("pb", bank)], rstg, f"m{m}zo{st}{half}")
                    self.free_bank(bank)
                    self.st_out(self.y_out[sq, t0 + st * 128: t0 + (st + 1) * 128,
                                           half * 512:(half + 1) * 512],
                                stg[:, 0:512], rstg, f"m{m}so{st}{half}")
            self.free_bank(bs1)
            self.free_bank(bs2)

def build_programs(nseq, seqlen, taps=()):
    mk = MK(nseq, seqlen, taps)
    nc = mk.build()
    return mk, nc


def kernel(**inputs):
    x = np.ascontiguousarray(inputs["L0_attn_hidden_states"], dtype=np.float32)
    B, S, _ = x.shape
    n_cores = 8
    per = B // n_cores
    host = prepare_host({k: np.asarray(v) for k, v in inputs.items()})
    mk, nc = build_programs(per, S)
    in_maps = []
    for c in range(n_cores):
        xc = x[c * per:(c + 1) * per].reshape(per, S, NCH, 128)
        im = {"x": np.ascontiguousarray(xc.transpose(0, 2, 3, 1))}
        im.update({k + "_d": v for k, v in host.items()})
        in_maps.append(im)
    res = run_bass_kernel_spmd(nc, in_maps, core_ids=list(range(n_cores)))
    out = np.concatenate([res.results[c]["y"] for c in range(n_cores)], axis=0)
    return out.astype(np.float32)


# version 40
# speedup vs baseline: 1.1874x; 1.0113x over previous
import numpy as np
import ml_dtypes
from contextlib import ExitStack

import concourse.bass as bass
import concourse.mybir as mybir
from concourse.bass_utils import run_bass_kernel_spmd
from concourse.alu_op_type import AluOpType as Op

F32 = mybir.dt.float32
F32R = mybir.dt.float32r
BF16 = mybir.dt.bfloat16
AF = mybir.ActivationFunctionType
AXX = mybir.AxisListType.X

D = 1024
H = 4
NCH = 8
CONV_K = 4
FIR_S = 5
FIR_L = 64
NTAP = FIR_S + FIR_L
NEXP = 8
ML = 512
FLOOR_EPS = 0.08
ALPHA = 1.702
SWISH_CAP = float(ALPHA * 7.0 / (1.0 + np.exp(-ALPHA * 7.0)))
F32EPS = float(np.finfo(np.float32).eps)
VL = FIR_L - 1 + ML
CL = CONV_K - 1 + ML

ENGINES = ("T", "V", "S", "G", "D")
ND_SEM = 6
SEM_NAMES = ["T", "V", "S", "G"] + [f"D{i}" for i in range(ND_SEM)] + ["GD"]


class _Rec:
    __slots__ = ("idx", "eng", "seq", "fn", "deps", "clock", "inc", "is_dma",
                 "sem", "wv", "self_wait", "tag")


class Sched:

    def __init__(self):
        self.ops = []
        self.by_eng = {e: [] for e in ENGINES}
        self.eng_seq = {e: 0 for e in ENGINES}
        self.eng_clock = {e: {n: 0 for n in SEM_NAMES} for e in ENGINES}
        self.last_w = {}
        self.readers = {}
        self.final_deps = []
        self.d_rank = 0
        self.sem_counts = {n: 0 for n in SEM_NAMES}

    def op(self, eng, fn, reads=(), writes=(), tag="", is_dma=False):
        idx = len(self.ops)
        seq = self.eng_seq[eng] + 1
        self.eng_seq[eng] = seq
        deps = set()
        nopr = set()
        for r in reads:
            w = self.last_w.get(r)
            if w is not None:
                deps.add(w)
                if r[0] in ("pb", "pbB"):
                    nopr.add(w)
            self.readers.setdefault(r, []).append(idx)
        for r in writes:
            w = self.last_w.get(r)
            if w is not None:
                deps.add(w)
                if r[0] in ("pb", "pbB"):
                    nopr.add(w)
            for rd in self.readers.get(r, ()):
                if rd != idx:
                    deps.add(rd)
                    if r[0] in ("pb", "pbB"):
                        nopr.add(rd)
            self.readers[r] = []
            self.last_w[r] = idx
        ec = self.eng_clock[eng]
        real = []
        for d in sorted(deps):
            od = self.ops[d]
            if od.eng == "T" and eng == "T":
                for k, v in od.clock.items():
                    if v > ec[k]:
                        ec[k] = v
                continue
            kv = od.wv if od.is_dma else od.seq
            if d not in nopr and ec[od.sem] >= kv:
                continue
            real.append(d)
        for d in real:
            for k, v in self.ops[d].clock.items():
                if v > ec[k]:
                    ec[k] = v
        rec = _Rec()
        rec.idx, rec.eng, rec.seq, rec.fn = idx, eng, seq, fn
        rec.is_dma = is_dma or eng == "D"
        rec.inc = rec.is_dma
        rec.self_wait = None
        if rec.is_dma:
            if eng == "D":
                self.d_rank += 1
                rec.sem = f"D{(self.d_rank - 1) % ND_SEM}"
            else:
                rec.sem = "GD"
            n = self.sem_counts[rec.sem] + 1
            self.sem_counts[rec.sem] = n
            rec.wv = 16 * n
            if n > 1:
                rec.self_wait = (rec.sem, 16 * (n - 1))
                if ec[rec.sem] < 16 * (n - 1):
                    ec[rec.sem] = 16 * (n - 1)
        else:
            rec.sem = eng
            rec.wv = 0
        if eng == "T":
            ec["T"] = seq
        clock = dict(ec)
        kown = rec.wv if rec.is_dma else seq
        if clock[rec.sem] < kown:
            clock[rec.sem] = kown
        rec.deps, rec.clock = real, clock
        rec.tag = tag
        self.ops.append(rec)
        self.by_eng[eng].append(rec)
        return idx

    def dma(self, fn, reads=(), writes=(), tag="", queue="D"):
        return self.op(queue, fn, reads, writes, tag, is_dma=True)

    def barrier_on(self, idxs):
        self.final_deps.extend(idxs)

    def finalize(self, sems):
        for rec in self.ops:
            for d in rec.deps:
                self.ops[d].inc = True
        for d in self.final_deps:
            self.ops[d].inc = True
        cnt = {e: 0 for e in ENGINES}
        for rec in self.ops:
            if rec.inc and not rec.is_dma:
                cnt[rec.eng] += 1
                rec.wv = cnt[rec.eng]
        self.sems = sems

    def emit(self, eng, proxy, final_extra=None):
        for rec in self.by_eng[eng]:
            waits = {}
            for d in rec.deps:
                dr = self.ops[d]
                if waits.get(dr.sem, -1) < dr.wv:
                    waits[dr.sem] = dr.wv
            if rec.self_wait is not None:
                snm, sv = rec.self_wait
                if waits.get(snm, -1) < sv:
                    waits[snm] = sv
            for key in sorted(waits):
                proxy.wait_ge(self.sems[key], waits[key])
            inst = rec.fn(proxy)
            if rec.inc:
                inst.then_inc(self.sems[rec.sem], 16 if rec.is_dma else 1)
        if final_extra is not None:
            final_extra(proxy)


CONST_VALUES = [0.0, 1.0, -1.0, 1e-12, 1e-5, F32EPS, 1e-6, 1.0 / 256.0,
                1.0 / 1024.0, 1.0 - 2.0 * FLOOR_EPS, SWISH_CAP, ALPHA,
                1.0 / ALPHA, -6.0, 8.0, -1e30, 7.0, 0.5, 2.0, 1.0 + 1e-30]



def _stream_w(w):
    d_in, d_out = w.shape
    nk, nd = d_in // 128, d_out // 128
    v = w.reshape(nk, 128, nd, 128).transpose(2, 1, 0, 3)
    return np.ascontiguousarray(v.reshape(nd, 128, nk * 128))


def _diag_tiles(w_taps):
    T = w_taps.shape[1]
    out = np.zeros((NCH, 128, T * 128), np.float32)
    ii = np.arange(128)
    for c in range(NCH):
        for t in range(T):
            out[c, ii, t * 128 + ii] = w_taps[c * 128:(c + 1) * 128, t]
    return out


def prepare_host(inp):
    g = {}
    g["wq_s"] = _stream_w(inp["L0_attn_Wq"])
    g["wk_s"] = _stream_w(inp["L0_attn_Wk"])
    g["wv_s"] = _stream_w(inp["L0_attn_Wv"])
    g["wb_s"] = np.ascontiguousarray(
        inp["L0_attn_Wb"].reshape(NCH, 128, H).transpose(1, 0, 2).reshape(128, NCH * H))
    w1 = inp["L0_attn_mlp_w1"]
    g["w1a_s"] = _stream_w(w1[0:1024, :])
    g["w1b"] = np.ascontiguousarray(w1[1024:1040, :])
    g["w2g"] = np.ascontiguousarray(
        inp["L0_attn_mlp_w2"].reshape(NCH, 128, H).transpose(1, 0, 2).reshape(128, NCH * H))
    g["b1g"] = np.ascontiguousarray(inp["L0_attn_mlp_b1"].reshape(NCH, 128).T)
    g["b2g"] = np.ascontiguousarray(inp["L0_attn_mlp_b2"].reshape(H, 1))
    g["logit_temp"] = np.ascontiguousarray(inp["L0_attn_logit_temp"].reshape(1, 1))
    g["crl"] = np.ascontiguousarray(inp["L0_attn_conv_res_logit"].reshape(1, H))
    g["rms_row"] = np.ascontiguousarray(inp["L0_attn_rms_w"].reshape(1, 256))
    g["wo_s"] = _stream_w(inp["L0_attn_Wo"])
    cw = np.concatenate([inp["L0_attn_conv_w_q"], inp["L0_attn_conv_w_k"],
                         inp["L0_attn_conv_w_v"]], axis=1)
    g["conv_diag"] = _diag_tiles(cw)
    fir = np.concatenate([inp["L0_attn_fir_short_w"].reshape(1024, FIR_S),
                          inp["L0_attn_fir_long_w"].reshape(1024, FIR_L)], 1)
    g["fir_diag"] = _diag_tiles(fir)
    g["norm_w"] = np.ascontiguousarray(inp["L0_ffn_norm_w"].reshape(NCH, 128).T)
    g["gate_w_t"] = np.ascontiguousarray(
        inp["L0_ffn_gate_w"].T.reshape(NCH, 128, NEXP).transpose(1, 0, 2)
        .reshape(128, NCH * NEXP))
    g["gate_b"] = np.ascontiguousarray(inp["L0_ffn_gate_b"].reshape(NEXP, 1))
    m1 = inp["L0_ffn_mlp1_w"]
    w1s = np.empty((NEXP, 16, 128, 1024), np.float32)
    for e in range(NEXP):
        w1s[e, 0:8] = _stream_w(np.ascontiguousarray(m1[e, 0::2, :].T))
        w1s[e, 8:16] = _stream_w(np.ascontiguousarray(m1[e, 1::2, :].T))
    g["ffn_w1_s"] = w1s
    m2 = inp["L0_ffn_mlp2_w"]
    w2s = np.empty((NEXP, 8, 128, 1024), np.float32)
    for e in range(NEXP):
        w2s[e] = _stream_w(np.ascontiguousarray(m2[e].T))
    g["ffn_w2_s"] = w2s
    b1 = inp["L0_ffn_mlp1_b"]
    bb = np.concatenate([b1[:, 0::2].reshape(NEXP, 8, 128),
                         b1[:, 1::2].reshape(NEXP, 8, 128)], axis=1)
    g["ffn_b1"] = np.ascontiguousarray(bb.transpose(2, 0, 1).reshape(128, NEXP * 16))
    g["ffn_b2"] = np.ascontiguousarray(inp["L0_ffn_mlp2_b"])
    g["ident"] = np.eye(128, dtype=np.float32)
    g["ident_bf"] = np.eye(128, dtype=ml_dtypes.bfloat16)
    g["ones_row"] = np.ones((1, 128), np.float32)
    g["ones_col"] = np.ones((128, 1), np.float32)
    g["m_stril"] = np.tril(np.ones((128, 128), ml_dtypes.bfloat16), -1)
    g["m_striu"] = np.triu(np.ones((128, 128), np.float32), 0)
    g["floor_row"] = np.tile(np.array([FLOOR_EPS, FLOOR_EPS, 0, 0], np.float32),
                             (128, 1))
    g["consts"] = np.tile(np.array(CONST_VALUES, np.float32), (128, 1))
    sel = np.zeros((8, 8 * 128), ml_dtypes.bfloat16)
    for e in range(8):
        sel[e, e * 128:(e + 1) * 128] = 1.0
    g["sel8"] = sel
    return g


class MK:
    def __init__(self, nseq, seqlen, taps=()):
        assert seqlen % ML == 0
        self.nseq, self.seqlen = nseq, seqlen
        self.nmac = seqlen // ML
        self.taps = list(taps)
        self.tap_specs = []
        self.nc = bass.Bass("TRN2", target_bir_lowering=False)
        self.s = Sched()
        self.es = ExitStack()
        self._nb = 0
        self._pool = [0, 1, 2, 3, 4, 5]
        self._held = set()

    def sb(self, name, shape, dtype):
        return self.es.enter_context(self.nc.sbuf_tensor(name, list(shape), dtype))

    def nbank(self):
        for _ in range(len(self._pool)):
            self._nb = (self._nb + 1) % len(self._pool)
            b = self._pool[self._nb]
            if b not in self._held:
                self._held.add(b)
                return b
        raise RuntimeError("no free psum bank")

    def free_bank(self, b):
        self._held.discard(b)

    def set_pool(self, pool):
        self._pool = list(pool)
        self._nb = 0

    def ld(self, dst, src, writes, tag, reads=()):
        return self.s.op("D", lambda p, d=dst, s_=src: p.dma_start(out=d, in_=s_),
                         reads=list(reads), writes=list(writes), tag=tag)

    def st_out(self, dst, src, reads, tag):
        i = self.s.op("D", lambda p, d=dst, s_=src: p.dma_start(out=d, in_=s_),
                      reads=list(reads), writes=[], tag=tag)
        self.s.barrier_on([i])

    def V(self, fn, reads, writes, tag):
        return self.s.op("V", fn, reads, writes, tag)

    def S_(self, fn, reads, writes, tag):
        return self.s.op("S", fn, reads, writes, tag)

    def MMb(self, bank, cols, lhsT, rhs, first, last, reads, tag):
        o = self.pb[bank][:, cols]
        return self.s.op("T", lambda p, o=o, a=lhsT, b=rhs, f=first, l=last:
                         p.matmul(o, a, b, start=f, stop=l),
                         reads=list(reads), writes=[("pb", bank)], tag=tag)

    def MMr(self, bank, cols, lhsT, rhs, first, last, reads, tag, mrows=None):
        o = self.pb[bank][0:mrows, cols] if mrows else self.pb[bank][:, cols]
        a, b = lhsT.bitcast(F32R), rhs.bitcast(F32R)
        return self.s.op("T", lambda p, o=o, a=a, b=b, f=first, l=last:
                         p.matmul(o, a, b, start=f, stop=l),
                         reads=list(reads), writes=[("pb", bank)], tag=tag)

    def TRf(self, bank, cols, in_ap, reads, tag, rows=None, pk=128):
        o = self.pb[bank][0:rows, cols] if rows else self.pb[bank][:, cols]
        ident = self.ident[0:pk, 0:pk]
        return self.s.op("T", lambda p, o=o, a=in_ap, i=ident:
                         p.transpose(o, a, i),
                         reads=list(reads) + [("c_ident",)],
                         writes=[("pb", bank)], tag=tag)

    def TRb(self, slot, cols, in_ap, reads, tag, rows=None):
        o = self.pbB[slot][0:rows, cols] if rows else self.pbB[slot][:, cols]
        return self.s.op("T", lambda p, o=o, a=in_ap:
                         p.transpose(o, a, self.ident_bf[:, :]),
                         reads=list(reads) + [("c_identb",)],
                         writes=[("pbB", slot)], tag=tag)

    def bc_row(self, bank, cols, row_ap, reads, tag):
        o = self.pb[bank][:, cols]
        a = self.m_striu[0:1, :]
        return self.s.op("T", lambda p, o=o, a=a, b=row_ap:
                         p.matmul(o, a, b, start=True, stop=True),
                         reads=list(reads) + [("c_mtu",)],
                         writes=[("pb", bank)], tag=tag)

    def tap(self, name, ap, shape, reads, dtype=F32):
        if name not in self.taps:
            return
        d = self.nc.dram_tensor("tap_" + name, list(shape), dtype,
                                kind="ExternalOutput").ap()
        self.tap_specs.append(name)
        i = self.s.dma(lambda p, d=d, a=ap: p.dma_start(out=d, in_=a),
                       reads=list(reads), writes=[], tag="tap_" + name)
        self.s.barrier_on([i])

    def build(self):
        nc, s = self.nc, self.s
        NS, SL = self.nseq, self.seqlen
        self.x_in = nc.dram_tensor("x", [NS, NCH, 128, SL], F32, kind="ExternalInput").ap()
        self.y_out = nc.dram_tensor("y", [NS, SL, D], F32, kind="ExternalOutput").ap()
        din = {}
        for nm, shp, dt in [
            ("wq_s", [8, 128, 1024], F32), ("wk_s", [8, 128, 1024], F32),
            ("wv_s", [8, 128, 1024], F32), ("wb_s", [128, 32], F32),
            ("w1a_s", [8, 128, 1024], F32), ("w1b", [16, 1024], F32),
            ("w2g", [128, 32], F32), ("b1g", [128, 8], F32), ("b2g", [4, 1], F32),
            ("logit_temp", [1, 1], F32), ("crl", [1, 4], F32),
            ("rms_row", [1, 256], F32), ("wo_s", [8, 128, 1024], F32),
            ("conv_diag", [8, 128, 12 * 128], F32),
            ("fir_diag", [8, 128, NTAP * 128], F32),
            ("norm_w", [128, 8], F32), ("gate_w_t", [128, 64], F32),
            ("gate_b", [8, 1], F32),
            ("ffn_w1_s", [NEXP, 16, 128, 1024], F32),
            ("ffn_w2_s", [NEXP, 8, 128, 1024], F32),
            ("ffn_b1", [128, NEXP * 16], F32), ("ffn_b2", [8, 1024], F32),
            ("ident", [128, 128], F32), ("ident_bf", [128, 128], BF16),
            ("ones_row", [1, 128], F32), ("ones_col", [128, 1], F32),
            ("m_stril", [128, 128], BF16), ("m_striu", [128, 128], F32),
            ("floor_row", [128, 4], F32), ("consts", [128, len(CONST_VALUES)], F32),
            ("sel8", [8, 8 * 128], BF16),
        ]:
            din[nm] = nc.dram_tensor(nm + "_d", shp, dt, kind="ExternalInput").ap()
        self.din = din
        scr = {}
        for nm, shp in [("wq_b", [8, 128, 1024]), ("wk_b", [8, 128, 1024]),
                        ("wv_b", [8, 128, 1024]), ("w1a_b", [8, 128, 1024]),
                        ("wo_b", [8, 128, 1024]), ("conv_b", [8, 128, 12 * 128]),
                        ("fir_b", [8, 128, NTAP * 128]),
                        ("fw1_b", [NEXP, 16, 128, 1024]),
                        ("fw2_b", [NEXP, 8, 128, 1024]),
                        ("w1b_c", [16, 1024])]:
            scr[nm] = nc.dram_tensor(nm, shp, BF16, kind="Internal").ap()
        self.scr = scr

        sb = self.sb
        self.consts = sb("consts", [128, len(CONST_VALUES)], F32)
        self.wb = sb("wb", [128, 32], BF16)
        self.w1b = sb("w1b", [16, 1024], BF16)
        self.w2g = sb("w2g", [128, 32], F32)
        self.b1g = sb("b1g", [128, 8], F32)
        self.b2g = sb("b2g", [4, 1], F32)
        self.lt = sb("lt", [1, 8], F32)
        self.crl = sb("crl", [1, 4], F32)
        self.rmsrow = sb("rmsrow", [1, 256], F32)
        self.normw = sb("normw", [128, 8], F32)
        self.gatewt = sb("gatewt", [128, 64], F32)
        self.gateb = sb("gateb", [8, 1], F32)
        self.fb1 = sb("fb1", [128, NEXP * 16], F32)
        self.fb1s = sb("fb1s", [128, NEXP * 8], F32)
        self.fb1p = sb("fb1p", [128, NEXP * 8], F32)
        self.fb2_bf = sb("fb2_bf", [8, 1024], BF16)
        self.ident = sb("ident", [128, 128], F32)
        self.ident_r = sb("ident_r", [128, 128], F32)
        self.onesc_r = sb("onesc_r", [128, 1], F32)
        self.w2g_bf = sb("w2g_bf", [128, 32], BF16)
        self.gatewt_r = sb("gatewt_r", [128, 64], F32)
        self.lnsc = sb("lnsc", [128, 256], F32)
        self.ident_bf = sb("ident_bf", [128, 128], BF16)
        self.ones_col = sb("ones_col", [128, 1], F32)
        self.m_stril = sb("m_stril", [128, 128], BF16)
        self.m_striu = sb("m_striu", [128, 128], F32)
        self.floor_row = sb("floor_row", [128, 4], F32)
        self.sel8 = sb("sel8", [8, 8 * 128], BF16)
        self.rms_bc = sb("rms_bc", [128, 256], F32)
        self.sigc = sb("sigc", [128, 4], F32)
        self.invt = sb("invt", [128, 1], F32)

        self.x_tm = sb("x_tm", [128, 512], F32)
        self.xT_pair = sb("xT_pair", [128, NCH * 256], F32)
        self.xT_bf = sb("xT_bf", [128, NCH * ML], BF16)
        self.pre = sb("pre", [128, 2 * CL], BF16)
        self.tails = [sb(f"tail{i}", [128, 3 * NCH], BF16) for i in range(3)]
        self.qT = sb("qT", [128, NCH * ML], BF16)
        self.kT = sb("kT", [128, NCH * ML], BF16)
        self.vT = sb("vT", [128, NCH * VL], BF16)
        self.lsT = sb("lsT", [128, NCH * ML], BF16)
        self.llT = sb("llT", [128, NCH * ML], BF16)
        self.beTa = sb("beTa", [4, ML], F32)
        self.beta_tm = sb("beta_tm", [128, 16], F32)
        self.sq_tm = sb("sq_tm", [128, 16], F32)
        self.sqrow = sb("sqrow", [1, ML], F32)
        self.sqsc = sb("sqsc", [128, ML], F32)
        self.branches = [sb(f"branches{j}", [128, H * 1024], BF16)
                         for j in range(2)]
        self.ktm = sb("ktm", [128, 2 * 256], F32)
        self.S_sb = sb("S_sb", [128, H * 512], F32)
        self.S_bf = sb("S_bf", [128, 2 * 512], BF16)
        self.chain = [[sb(f"chain{sl}_{i}", [128, 384], BF16) for i in range(2)]
                      for sl in range(2)]
        self.attnT = sb("attnT", [128, 2 * 128], F32)
        self.vbk = sb("vbk", [128, 2 * 256], F32)
        self.stat_sc = sb("stat_sc", [128, 2 * 8], F32)
        self.colsc = sb("colsc", [128, 2 * 8], F32)
        self.stats_tm = sb("stats_tm", [128, 2 * 16], F32)
        self.qcsc = sb("qcsc", [128, 8], F32)
        self.statsT = sb("statsT", [16, 2 * H * 128], BF16)
        self.hg = sb("hg", [128, 2 * 512], BF16)
        self.lgrb = sb("lgrb", [4, 256], F32)
        self.fglb = sb("fglb", [8, 256], F32)
        self.probs = sb("probs", [128, H * 4], F32)
        self.o_n = sb("o_n", [128, 1024], F32)
        self.onT = sb("onT", [128, NCH * 256], BF16)
        self.hrot = sb("hrot", [128, 2 * 256], F32)
        self.h_bf = sb("h_bf", [128, NCH * ML], BF16)
        self.acc = sb("acc", [128, NCH * ML], F32)
        self.p8 = sb("p8", [128, 4 * NEXP], F32)
        self.p8T_bf = sb("p8T_bf", [8, ML], BF16)
        self.p_be = sb("p_be", [128, ML], F32)
        self.act_bf = sb("act_bf", [128, NCH * ML], BF16)
        self.gl_sb = sb("gl_sb", [128, 4 * 8], F32)
        self.row1 = sb("row1", [1, 5 * 256], F32)
        self.scx = sb("scx", [128, ML], F32)
        self.scy = sb("scy", [128, ML], F32)
        self.scz = sb("scz", [128, 128], F32)

        self.r_w = sb("r_w", [128, 2 * 1024], BF16)
        self.r_w1a = sb("r_w1a", [128, 2 * 1024], BF16)
        self.r_wo = sb("r_wo", [128, 2 * 1024], BF16)
        self.r_conv = sb("r_conv", [128, 12 * 128], BF16)
        self.r_fir = sb("r_fir", [128, 2 * 12 * 128], BF16)
        self.r_f1 = sb("r_f1", [128, 3 * 1024], BF16)
        self.r_fw2 = sb("r_fw2", [128, 2 * 1024], BF16)

        self.pb = [self.es.enter_context(nc.psum_tensor(f"pb{i}", [128, 512], F32))
                   for i in range(6)]
        self.pbB = [self.es.enter_context(nc.psum_tensor(f"pbB{i}", [128, 1024], BF16))
                    for i in range(2)]

        for i, v in enumerate(CONST_VALUES):
            nc.const_aps.register(F32, v, self.consts[:, i:i + 1])

        self._startup()
        for sq in range(NS):
            self._seq(sq)

        sems = {e: self.es.enter_context(nc.semaphore(f"sem_{e}"))
                for e in SEM_NAMES}
        s.finalize(sems)
        maxv = {}
        for d in set(s.final_deps):
            dr = s.ops[d]
            if maxv.get(dr.sem, -1) < dr.wv:
                maxv[dr.sem] = dr.wv

        def final_waits(proxy):
            for e in sorted(maxv):
                proxy.wait_ge(sems[e], maxv[e])

        with self.es, nc.Block() as block:
            @block.sync
            def _(p):
                s.emit("D", p, final_extra=final_waits)

            @block.tensor
            def _(p):
                s.emit("T", p)

            @block.vector
            def _(p):
                s.emit("V", p)

            @block.scalar
            def _(p):
                s.emit("S", p)

            @block.gpsimd
            def _(p):
                s.emit("G", p)
        return nc

    def _startup(self):
        s, din, scr = self.s, self.din, self.scr
        V = self.V
        for src, dst, rg in [("wq_s", "wq_b", "sc_wq"), ("wk_s", "wk_b", "sc_wk"),
                             ("wv_s", "wv_b", "sc_wv"), ("w1a_s", "w1a_b", "sc_w1a"),
                             ("wo_s", "wo_b", "sc_wo"), ("conv_diag", "conv_b", "sc_cv"),
                             ("fir_diag", "fir_b", "sc_fir"),
                             ("ffn_w1_s", "fw1_b", "sc_f1"),
                             ("ffn_w2_s", "fw2_b", "sc_f2"),
                             ("w1b", "w1b_c", "sc_w1b")]:
            s.op("G", lambda p, a=din[src], b=scr[dst]: p.dma_start(out=b, in_=a),
                 reads=[], writes=[(rg,)], tag=f"cast_{dst}", is_dma=True)

        L = self.ld
        L(self.consts[:, :], din["consts"], [("c_c",)], "c_c")
        s.op("G", lambda p: p.dma_start(out=self.wb[:, :], in_=din["wb_s"]),
             reads=[], writes=[("c_wb",)], tag="c_wb", is_dma=True)
        L(self.w1b[:, :], scr["w1b_c"], [("c_w1b",)], "c_w1b",
          reads=[("sc_w1b",)])
        L(self.w2g[:, :], din["w2g"], [("c_w2g",)], "c_w2g")
        L(self.b1g[:, :], din["b1g"], [("c_b1g",)], "c_b1g")
        L(self.b2g[:, :], din["b2g"], [("c_b2g",)], "c_b2g")
        L(self.lt[0:1, 0:1], din["logit_temp"], [("lt",)], "c_lt")
        L(self.crl[:, :], din["crl"], [("c_crl",)], "c_crl")
        L(self.rmsrow[:, :], din["rms_row"], [("c_rms",)], "c_rms")
        L(self.normw[:, :], din["norm_w"], [("c_nw",)], "c_nw")
        L(self.gatewt[:, :], din["gate_w_t"], [("c_gw",)], "c_gw")
        L(self.gateb[:, :], din["gate_b"], [("c_gb",)], "c_gb")
        L(self.fb1[:, :], din["ffn_b1"], [("c_fb1",)], "c_fb1")
        s.op("G", lambda p: p.dma_start(out=self.fb2_bf[:, :], in_=din["ffn_b2"]),
             reads=[], writes=[("c_fb2b",)], tag="c_fb2b", is_dma=True)
        L(self.ident[:, :], din["ident"], [("c_ident",)], "c_id")
        L(self.ident_bf[:, :], din["ident_bf"], [("c_identb",)], "c_idb")
        L(self.ones_col[:, :], din["ones_col"], [("c_onesc",)], "c_oc")
        L(self.m_stril[:, :], din["m_stril"], [("c_mtl",)], "c_mtl")
        L(self.m_striu[:, :], din["m_striu"], [("c_mtu",)], "c_mtu")
        L(self.floor_row[:, :], din["floor_row"], [("c_floor",)], "c_fl")
        L(self.sel8[:, :], din["sel8"], [("c_sel8",)], "c_s8")
        S_ = self.S_
        S_(lambda p: p.copy(self.ident_r[:, :].bitcast(F32R), self.ident[:, :]),
           [("c_ident",)], [("c_identr",)], "identr")
        S_(lambda p: p.copy(self.onesc_r[:, :].bitcast(F32R), self.ones_col[:, :]),
           [("c_onesc",)], [("c_onescr",)], "onescr")
        S_(lambda p: p.copy(self.w2g_bf[:, :], self.w2g[:, :]),
           [("c_w2g",)], [("c_w2gb",)], "w2gb")
        S_(lambda p: p.copy(self.gatewt_r[:, :].bitcast(F32R), self.gatewt[:, :]),
           [("c_gw",)], [("c_gwr",)], "gwr")

        for e in range(NEXP):
            V(lambda p, e=e: p.tensor_scalar(
                self.fb1s[:, e * 8:(e + 1) * 8], self.fb1[:, e * 16:e * 16 + 8],
                ALPHA, None, Op.mult),
              [("c_fb1",), ("c_c",)], [("c_fb1s",)], f"b1s{e}")
            V(lambda p, e=e: p.tensor_scalar(
                self.fb1p[:, e * 8:(e + 1) * 8], self.fb1[:, e * 16 + 8:e * 16 + 16],
                1.0, None, Op.add),
              [("c_fb1",), ("c_c",)], [("c_fb1p",)], f"b1p{e}")
        b_ = self.nbank()
        self.bc_row(b_, slice(0, 256), self.rmsrow[0:1, :], [("c_rms",)], "rmsbc")
        V(lambda p, b=b_: p.tensor_copy(self.rms_bc[:, :], self.pb[b][:, 0:256]),
          [("pb", b_)], [("c_rmsbc",)], "rmsbc_e")
        self.free_bank(b_)
        S_(lambda p: p.activation(self.crl[0:1, :].bitcast(F32R), self.crl[0:1, :], AF.Sigmoid),
           [("c_crl",)], [("c_crl",)], "crlsig")
        b_ = self.nbank()
        self.bc_row(b_, slice(0, 4), self.crl[0:1, :], [("c_crl",)], "crlbc")
        V(lambda p, b=b_: p.tensor_copy(self.sigc[:, :], self.pb[b][:, 0:4]),
          [("pb", b_)], [("c_sigc",)], "sigc_e")
        self.free_bank(b_)
        lt = self.lt
        S_(lambda p: p.activation(lt[0:1, 1:2], lt[0:1, 0:1], AF.Abs),
           [("lt",)], [("lt",)], "lt1")
        S_(lambda p: p.activation(lt[0:1, 2:3], lt[0:1, 1:2], AF.Exp, scale=-1.0),
           [("lt",), ("c_c",)], [("lt",)], "lt2")
        S_(lambda p: p.activation(lt[0:1, 3:4], lt[0:1, 2:3], AF.Ln, bias=1.0),
           [("lt",), ("c_c",)], [("lt",)], "lt3")
        V(lambda p: p.tensor_scalar(lt[0:1, 4:5], lt[0:1, 0:1], 0.0, None, Op.max),
          [("lt",), ("c_c",)], [("lt",)], "lt4")
        V(lambda p: p.tensor_tensor(lt[0:1, 5:6], lt[0:1, 3:4], lt[0:1, 4:5], Op.add),
          [("lt",)], [("lt",)], "lt5")
        V(lambda p: p.tensor_scalar(lt[0:1, 6:7], lt[0:1, 5:6], 1e-6, None, Op.add),
          [("lt",), ("c_c",)], [("lt",)], "lt6")
        V(lambda p: p.reciprocal(lt[0:1, 7:8], lt[0:1, 6:7]),
          [("lt",)], [("lt",)], "lt7")
        S_(lambda p: p.copy(lt[0:1, 7:8].bitcast(F32R), lt[0:1, 7:8]),
           [("lt",)], [("lt",)], "lt8")
        b_ = self.nbank()
        self.bc_row(b_, slice(0, 1), lt[0:1, 7:8], [("lt",)], "invtbc")
        V(lambda p, b=b_: p.tensor_copy(self.invt[:, :], self.pb[b][:, 0:1]),
          [("pb", b_)], [("c_invt",)], "invt_e")
        self.free_bank(b_)

    def _seq(self, sq):
        V = self.V
        rS_all = [("S_sb", h) for h in range(H)]
        V(lambda p: p.memset(self.S_sb[:, :], 0.0), [], rS_all, f"q{sq}S0")
        self.S_(lambda p: p.copy(self.S_sb[:, :].bitcast(F32R), self.S_sb[:, :]),
                rS_all, rS_all, f"q{sq}S0r")
        for i in range(3):
            V(lambda p, i=i: p.memset(self.tails[i][:, :], 0.0),
              [], [("tail", i)], f"q{sq}t{i}")
        V(lambda p: p.memset(self.vT[:, :], 0.0), [], [("vT",)], f"q{sq}v0")
        for m in range(self.nmac):
            self._ph1(sq, m)
            self._ph2(sq, m)
            self._ph4a(sq, m)
            for pair in range(2):
                self._ph3_pair(sq, m, pair)
            self._ph7(sq, m)
            self._ph8(sq, m)

    def _ph1(self, sq, m):
        V, S_ = self.V, self.S_
        t0 = m * ML
        for dc in range(NCH):
            xa = self.x_tm[:, :]
            self.ld(xa, self.x_in[sq, dc, :, t0:t0 + ML],
                    [("x_tm",)], f"m{m}x{dc}")
            S_(lambda p, dc=dc, xa=xa: p.copy(
                self.xT_bf[:, dc * ML:(dc + 1) * ML], xa),
               [("x_tm",)], [("xTb", dc)], f"m{m}xTb{dc}")
        for dc in range(NCH):
            cpiece = self.r_conv[:, :]
            self.ld(cpiece, self.scr["conv_b"][dc], [("rconv",)],
                    f"m{m}cd{dc}", reads=[("sc_cv",)])
            for pi in range(3):
                par = (dc * 3 + pi) % 2
                scrn = ["wq_b", "wk_b", "wv_b"][pi]
                piece = self.r_w[:, par * 1024:(par + 1) * 1024]
                self.ld(piece, self.scr[scrn][dc], [("rw", par)],
                        f"m{m}w{pi}l{dc}", reads=[("sc_" + ["wq", "wk", "wv"][pi],)])
                bank = self.nbank()
                for kc in range(NCH):
                    self.MMb(bank, slice(0, ML), piece[:, kc * 128:(kc + 1) * 128],
                             self.xT_bf[:, kc * ML:(kc + 1) * ML],
                             kc == 0, kc == NCH - 1,
                             [("rw", par), ("xTb", kc)], f"m{m}p{pi}_{dc}_{kc}")
                prebuf = self.pre[:, (dc % 2) * CL:((dc % 2) + 1) * CL]
                tail = self.tails[pi]
                V(lambda p, pb_=prebuf, b=bank: p.tensor_copy(
                    pb_[:, 3:], self.pb[b][:, 0:ML]),
                    [("pb", bank)], [("pre", dc % 2)], f"m{m}pe{pi}_{dc}")
                self.free_bank(bank)
                V(lambda p, pb_=prebuf, t=tail, dc=dc: p.tensor_copy(
                    pb_[:, 0:3], t[:, dc * 3:dc * 3 + 3]),
                    [("tail", pi)], [("pre", dc % 2)], f"m{m}ph{pi}_{dc}")
                V(lambda p, pb_=prebuf, t=tail, dc=dc: p.tensor_copy(
                    t[:, dc * 3:dc * 3 + 3], pb_[:, ML:ML + 3]),
                    [("pre", dc % 2)], [("tail", pi)], f"m{m}pt{pi}_{dc}")
                bank2 = self.nbank()
                for tap in range(CONV_K):
                    ti = pi * 4 + tap
                    self.MMb(bank2, slice(0, ML),
                             cpiece[:, ti * 128:(ti + 1) * 128],
                             prebuf[:, tap:tap + ML], tap == 0, tap == CONV_K - 1,
                             [("rconv",), ("pre", dc % 2)],
                             f"m{m}c{pi}_{dc}_{tap}")
                if pi == 2:
                    dst = self.vT[:, dc * VL + FIR_L - 1: dc * VL + FIR_L - 1 + ML]
                    S_(lambda p, d=dst, b=bank2: p.activation(
                        d, self.pb[b][:, 0:ML], AF.Silu),
                       [("pb", bank2)], [("vT", dc)], f"m{m}sv{dc}")
                else:
                    outT = self.qT if pi == 0 else self.kT
                    rg = "qT" if pi == 0 else "kT"
                    dst = outT[:, dc * ML:(dc + 1) * ML]
                    S_(lambda p, d=dst, b=bank2: p.activation(
                        d, self.pb[b][:, 0:ML], AF.Silu),
                       [("pb", bank2)], [(rg, dc)], f"m{m}s{pi}_{dc}")
                self.free_bank(bank2)
            fg = self._fir_gen(m, dc)
            if dc % 2 == 1:
                gens = [fg, self._norm_gen(m, dc // 2)]
                while gens:
                    g = gens.pop(0)
                    try:
                        next(g)
                        gens.append(g)
                    except StopIteration:
                        pass
            else:
                for _ in fg:
                    pass

        bb = self.nbank()
        for kc in range(NCH):
            self.s.op("T", lambda p, kc=kc, b=bb: p.matmul(
                self.pb[b][0:4, 0:ML],
                self.wb[:, kc * 4:(kc + 1) * 4],
                self.xT_bf[:, kc * ML:(kc + 1) * ML],
                start=(kc == 0), stop=(kc == NCH - 1)),
                reads=[("c_wb",), ("xTb", kc)], writes=[("pb", bb)],
                tag=f"m{m}be{kc}")
        S_(lambda p, b=bb: p.activation(self.beTa[0:4, :].bitcast(F32R), self.pb[b][0:4, 0:ML],
                                        AF.Sigmoid),
           [("pb", bb)], [("beTa",)], f"m{m}bsig")
        self.free_bank(bb)
        for st in range(4):
            bank = self.nbank()
            self.TRf(bank, slice(0, 4), self.beTa[0:4, st * 128:(st + 1) * 128],
                     [("beTa",)], f"m{m}beT{st}", pk=4)
            V(lambda p, st=st, b=bank: p.tensor_copy(
                self.beta_tm[:, st * 4:(st + 1) * 4], self.pb[b][:, 0:4]),
                [("pb", bank)], [("beta_tm",)], f"m{m}beTe{st}")
            self.free_bank(bank)

        if m == 0 and sq == 0:
            self.tap("qT", self.qT[:, :], [128, NCH * ML],
                     [("qT", c) for c in range(NCH)], BF16)
            self.tap("kT", self.kT[:, :], [128, NCH * ML],
                     [("kT", c) for c in range(NCH)], BF16)
            self.tap("vT", self.vT[:, :], [128, NCH * VL],
                     [("vT", c) for c in range(NCH)], BF16)
            self.tap("beTa", self.beTa[0:4, :], [4, ML], [("beTa",)], F32)

    def _norm_gen(self, m, h):
        V, S_ = self.V, self.S_
        c0, c1 = 2 * h, 2 * h + 1
        if True:
            bank = self.nbank()
            S_(lambda p, c=c0: p.activation(self.sqsc[:, :].bitcast(F32R),
                                            self.kT[:, c * ML:(c + 1) * ML], AF.Square),
               [("kT", c0)], [("sqsc", 0), ("sqsc", 1)], f"m{m}ka{h}")
            yield
            self.MMr(bank, slice(0, ML), self.onesc_r[:, 0:1], self.sqsc[:, :],
                     True, False, [("c_onescr",), ("sqsc", 0), ("sqsc", 1)], f"m{m}kma{h}", mrows=1)
            S_(lambda p, c=c1: p.activation(self.sqsc[:, :].bitcast(F32R),
                                            self.kT[:, c * ML:(c + 1) * ML], AF.Square),
               [("kT", c1)], [("sqsc", 0), ("sqsc", 1)], f"m{m}kb{h}")
            self.MMr(bank, slice(0, ML), self.onesc_r[:, 0:1], self.sqsc[:, :],
                     False, True, [("c_onescr",), ("sqsc", 0), ("sqsc", 1)], f"m{m}kmb{h}", mrows=1)
            yield
            S_(lambda p, b=bank: p.activation(self.sqrow[0:1, :], self.pb[b][0:1, 0:ML],
                                              AF.Sqrt, bias=1e-12),
               [("pb", bank), ("c_c",)], [("sqrow",)], f"m{m}ks{h}")
            self.free_bank(bank)
            yield
            V(lambda p: p.reciprocal(self.sqrow[0:1, :], self.sqrow[0:1, :]),
              [("sqrow",)], [("sqrow",)], f"m{m}kr{h}")
            S_(lambda p: p.copy(self.sqrow[0:1, :].bitcast(F32R),
                                self.sqrow[0:1, :]),
               [("sqrow",)], [("sqrow",)], f"m{m}kr2{h}")
            yield
            bank2 = self.nbank()
            self.bc_row(bank2, slice(0, ML), self.sqrow[0:1, :], [("sqrow",)],
                        f"m{m}kbc{h}")
            yield
            for c in (c0, c1):
                V(lambda p, c=c, b=bank2: p.tensor_tensor(
                    self.kT[:, c * ML:(c + 1) * ML], self.kT[:, c * ML:(c + 1) * ML],
                    self.pb[b][:, 0:ML], Op.mult),
                  [("pb", bank2), ("kT", c)], [("kT", c)], f"m{m}kn{h}_{c}")
            self.free_bank(bank2)
            yield
            bank3 = self.nbank()
            S_(lambda p, c=c0: p.activation(self.sqsc[:, :].bitcast(F32R),
                                            self.qT[:, c * ML:(c + 1) * ML], AF.Square),
               [("qT", c0)], [("sqsc", 0), ("sqsc", 1)], f"m{m}qa{h}")
            self.MMr(bank3, slice(0, ML), self.onesc_r[:, 0:1], self.sqsc[:, :],
                     True, False, [("c_onescr",), ("sqsc", 0), ("sqsc", 1)], f"m{m}qma{h}", mrows=1)
            S_(lambda p, c=c1: p.activation(self.sqsc[:, :].bitcast(F32R),
                                            self.qT[:, c * ML:(c + 1) * ML], AF.Square),
               [("qT", c1)], [("sqsc", 0), ("sqsc", 1)], f"m{m}qb{h}")
            self.MMr(bank3, slice(0, ML), self.onesc_r[:, 0:1], self.sqsc[:, :],
                     False, True, [("c_onescr",), ("sqsc", 0), ("sqsc", 1)], f"m{m}qmb{h}", mrows=1)
            yield
            V(lambda p, b=bank3: p.tensor_copy(self.sqrow[0:1, :].bitcast(F32R), self.pb[b][0:1, 0:ML]),
              [("pb", bank3)], [("sqrow",)], f"m{m}qe{h}")
            self.free_bank(bank3)
            yield
            bank4 = self.nbank()
            for st in range(4):
                self.TRf(bank4, slice(st, st + 1),
                         self.sqrow[0:1, st * 128:(st + 1) * 128],
                         [("sqrow",)], f"m{m}qt{h}_{st}", pk=1)
            yield
            V(lambda p, b=bank4: p.tensor_copy(self.qcsc[:, 0:4], self.pb[b][:, 0:4]),
              [("pb", bank4)], [("qcsc",)], f"m{m}qc{h}")
            self.free_bank(bank4)
            S_(lambda p: p.activation(self.qcsc[:, 4:8], self.qcsc[:, 0:4],
                                      AF.Sqrt, bias=1e-12),
               [("qcsc",), ("c_c",)], [("qcsc",)], f"m{m}qs{h}")
            V(lambda p, h=h: p.reciprocal(self.sq_tm[:, 4 * h:4 * h + 4],
                                          self.qcsc[:, 4:8]),
              [("qcsc",)], [("sq_tm",)], f"m{m}qr{h}")


    def _fir_gen(self, m, dc):
        V = self.V
        sets = [(0, 12), (12, 12), (24, 12), (36, 11), (47, 11), (58, 11)]
        if True:
            bls = self.nbank()
            bll = self.nbank()
            for si, (toff, tcnt) in enumerate(sets):
                par = (dc * 6 + si) % 2
                ring = self.r_fir[:, par * 12 * 128:(par * 12 + 12) * 128]
                self.ld(ring[:, 0:tcnt * 128],
                        self.scr["fir_b"][dc][:, toff * 128:(toff + tcnt) * 128],
                        [("rfir", par)], f"m{m}f{dc}_{si}", reads=[("sc_fir",)])
                for tj in range(tcnt):
                    t = toff + tj
                    if t < FIR_S:
                        tap = t
                        off = (FIR_L - 1) - (FIR_S - 1) + tap
                        self.MMb(bls, slice(0, ML), ring[:, tj * 128:(tj + 1) * 128],
                                 self.vT[:, dc * VL + off: dc * VL + off + ML],
                                 tap == 0, tap == FIR_S - 1,
                                 [("rfir", par), ("vT", dc)], f"m{m}fs{dc}_{t}")
                    else:
                        j = t - FIR_S
                        off = j
                        self.MMb(bll, slice(0, ML), ring[:, tj * 128:(tj + 1) * 128],
                                 self.vT[:, dc * VL + off: dc * VL + off + ML],
                                 j == 0, j == FIR_L - 1,
                                 [("rfir", par), ("vT", dc)], f"m{m}fl{dc}_{j}")
                yield
            V(lambda p, d=dc, b=bls: p.tensor_copy(
                self.lsT[:, d * ML:(d + 1) * ML], self.pb[b][:, 0:ML]),
              [("pb", bls)], [("lsT", dc)], f"m{m}fse{dc}")
            self.free_bank(bls)
            V(lambda p, d=dc, b=bll: p.tensor_copy(
                self.llT[:, d * ML:(d + 1) * ML], self.pb[b][:, 0:ML]),
              [("pb", bll)], [("llT", dc)], f"m{m}fle{dc}")
            self.free_bank(bll)

    def _ph2(self, sq, m):
        V = self.V
        if m == 0 and sq == 0:
            self.tap("lsT", self.lsT[:, :], [128, NCH * ML],
                     [("lsT", c) for c in range(NCH)], BF16)
            self.tap("llT", self.llT[:, :], [128, NCH * ML],
                     [("llT", c) for c in range(NCH)], BF16)
        for dc in range(NCH):
            V(lambda p, d=dc: p.tensor_copy(
                self.vT[:, d * VL: d * VL + FIR_L - 1],
                self.vT[:, d * VL + ML: d * VL + ML + FIR_L - 1]),
              [("vT", dc)], [("vT", dc)], f"m{m}vr{dc}")

    def _ph4a(self, sq, m):
        V = self.V
        for fc in range(NCH):
            par = fc % 2
            piece = self.r_w1a[:, par * 1024:(par + 1) * 1024]
            self.ld(piece, self.scr["w1a_b"][fc], [("rw1a", par)],
                    f"m{m}gw{fc}", reads=[("sc_w1a",)])
            ba = self.nbank()
            for kc in range(NCH):
                self.MMb(ba, slice(0, ML), piece[:, kc * 128:(kc + 1) * 128],
                         self.xT_bf[:, kc * ML:(kc + 1) * ML],
                         kc == 0, kc == NCH - 1,
                         [("rw1a", par), ("xTb", kc)], f"m{m}gA{fc}{kc}")
            V(lambda p, fc=fc, b=ba: p.tensor_copy(
                self.act_bf[:, fc * ML:(fc + 1) * ML], self.pb[b][:, 0:ML]),
              [("pb", ba)], [("act", fc)], f"m{m}gE{fc}")
            self.free_bank(ba)

    def _ph3_pair(self, sq, m, pair):
        pending = [(sub, self._head_gen(m, pair * 2 + sub, h))
                   for sub in range(2) for h in range(H)]
        ph6 = self._ph6_gen(sq, m, pair)
        ph6_in = False
        active = []
        holders = 0
        done_cnt = [0, 0]
        epi_added = [False, False]
        while pending or active:
            while pending and holders < 2 and len(active) < 4:
                sub, g = pending.pop(0)
                active.append([g, True, sub])
                holders += 1
            if not pending and not ph6_in:
                active.append([ph6, False, None])
                ph6_in = True
            rec = active.pop(0)
            try:
                tok = next(rec[0])
                if tok == "free" and rec[1]:
                    rec[1] = False
                    holders -= 1
                if tok == "hold":
                    continue
                active.append(rec)
            except StopIteration:
                if rec[1]:
                    holders -= 1
                sub = rec[2]
                if sub is not None:
                    done_cnt[sub] += 1
                    if done_cnt[sub] == H and not epi_added[sub]:
                        active.append(
                            [self._sub_epi(m, pair * 2 + sub), False, None])
                        epi_added[sub] = True
        for _ in ph6:
            pass
        if m == 0 and sq == 0 and pair == 0:
            self.tap("branches0", self.branches[0][:, :], [128, H * 1024],
                     [("br", h) for h in range(H)], BF16)
            self.tap("statsT", self.statsT[0:16, :], [16, H * 128],
                     [("statsT", h) for h in range(H)], BF16)
            self.tap("probs0", self.probs[:, :], [128, H * 4],
                     [("probs", h2) for h2 in range(H)], F32)
            self.tap("on0", self.o_n[:, :], [128, 1024],
                     [("o_n", h2) for h2 in range(H)], F32)

    def _head_gen(self, m, st, h):
        V, S_ = self.V, self.S_
        sub = st % 2
        branches = self.branches[sub]
        rbr = ("br", sub, h)
        sl = h % 2
        c0, c1 = 2 * h, 2 * h + 1
        base = h * 1024
        qTs = [self.qT[:, c * ML + st * 128: c * ML + (st + 1) * 128] for c in (c0, c1)]
        kTs = [self.kT[:, c * ML + st * 128: c * ML + (st + 1) * 128] for c in (c0, c1)]
        bcol = self.beta_tm[:, st * 4 + h: st * 4 + h + 1]
        sqcol = self.sq_tm[:, 4 * h + st: 4 * h + st + 1]
        Ssl = self.S_sb[:, h * 512:(h + 1) * 512]
        rS = ("S_sb", h)
        ktm = self.ktm[:, sl * 256:(sl + 1) * 256]
        vbk = self.vbk[:, sl * 256:(sl + 1) * 256]
        ui = vbk
        attnT = self.attnT[:, sl * 128:(sl + 1) * 128]
        S_bf = self.S_bf[:, sl * 512:(sl + 1) * 512]
        pbB = self.pbB[sl]
        rk, rvb = ("ktm", sl), ("vbk", sl)
        ru = rvb
        rat, rsb, rpB = ("attnT", sl), ("S_bf", sl), ("pbB", sl)
        ch = self.chain[sl]
        rch = lambda i, sl=sl: ("ch", sl, i)

        for i, c in enumerate((c0, c1)):
            self.TRb(sl, slice(i * 128, (i + 1) * 128),
                     self.vT[:, c * VL + FIR_L - 1 + st * 128:
                             c * VL + FIR_L - 1 + (st + 1) * 128],
                     [("vT", c)], f"m{m}vt{st}{h}{i}")
        yield
        dstv = branches[:, base + 768: base + 1024]
        V(lambda p, d=dstv: p.tensor_copy(d, pbB[:, 0:256]),
          [rpB], [rbr], f"m{m}vb{st}{h}")
        yield
        for i in range(2):
            self.TRb(sl, slice(i * 128, (i + 1) * 128), kTs[i],
                     [("kT", 2 * h + i)], f"m{m}kt{st}{h}{i}")
        yield
        V(lambda p, d=ktm: p.tensor_copy(d.bitcast(F32R), pbB[:, 0:256]),
          [rpB], [rk], f"m{m}ke{st}{h}")
        yield
        for bi, (srcT, rgn) in enumerate([(self.lsT, "lsT"), (self.llT, "llT")]):
            for i, c in enumerate((c0, c1)):
                self.TRb(sl, slice((bi * 2 + i) * 128, (bi * 2 + i + 1) * 128),
                         srcT[:, c * ML + st * 128: c * ML + (st + 1) * 128],
                         [(rgn, c)], f"m{m}bt{st}{h}{bi}{i}")
        yield
        dst = branches[:, base: base + 512]
        V(lambda p, d=dst: p.tensor_copy(d, pbB[:, 0:512]),
          [rpB], [rbr], f"m{m}be{st}{h}")
        bg = self.nbank()
        for i in range(2):
            self.MMb(bg, slice(0, 128), kTs[i], kTs[i], i == 0, i == 1,
                     [("kT", 2 * h + i)], f"m{m}G{st}{h}{i}")
        for i in range(2):
            self.MMb(bg, slice(128, 256), kTs[i], qTs[i], i == 0, i == 1,
                     [("kT", 2 * h + i), ("qT", 2 * h + i)], f"m{m}A{st}{h}{i}")
        yield
        V(lambda p, b=bg: p.scalar_tensor_tensor(
            ch[0][:, 0:128], self.pb[b][:, 0:128], bcol,
            self.m_stril[:, :], Op.mult, Op.mult),
          [("pb", bg), ("beta_tm",), ("c_mtl",)], [rch(0)], f"m{m}Am{st}{h}")
        V(lambda p, b=bg, d=attnT: p.tensor_tensor(
            d.bitcast(F32R), self.pb[b][:, 128:256],
            self.m_striu[:, :], Op.mult),
          [("pb", bg), ("c_mtu",)], [rat], f"m{m}At{st}{h}")
        self.free_bank(bg)
        V(lambda p, d=S_bf, S=Ssl: p.tensor_copy(d, S),
          [rS], [rsb], f"m{m}Sb{st}{h}")
        yield
        self.TRb(sl, slice(512, 640), ch[0][:, 0:128], [rch(0)], f"m{m}Y0{st}{h}")
        yield
        V(lambda p: p.tensor_copy(ch[0][:, 128:256], pbB[:, 512:640]),
          [rpB], [rch(0)], f"m{m}Ye{st}{h}")
        V(lambda p: p.scalar_tensor_tensor(
            ch[0][:, 256:384], ch[0][:, 128:256], -1.0,
            self.ident[:, :], Op.mult, Op.add),
          [rch(0), ("c_ident",), ("c_c",)], [rch(0)], f"m{m}M0{st}{h}")
        yield
        par = 0
        for lvl in range(1, 7):
            nxt = par ^ 1
            bc = self.nbank()
            self.MMb(bc, slice(0, 128), ch[par][:, 128:256], ch[par][:, 0:128],
                     True, True, [rch(par)], f"m{m}c1_{st}{h}{lvl}")
            self.MMb(bc, slice(128, 256), ch[par][:, 0:128], ch[par][:, 128:256],
                     True, True, [rch(par)], f"m{m}c2_{st}{h}{lvl}")
            yield
            V(lambda p, b=bc, n=nxt: p.tensor_copy(ch[n][:, 0:256],
                                                   self.pb[b][:, 0:256]),
              [("pb", bc)], [rch(nxt)], f"m{m}ce{st}{h}{lvl}")
            self.MMb(bc, slice(256, 384), ch[nxt][:, 0:128], ch[par][:, 256:384],
                     True, True, [rch(nxt), rch(par)], f"m{m}c3_{st}{h}{lvl}")
            yield
            V(lambda p, b=bc, n=nxt, o=par: p.tensor_tensor(
                ch[n][:, 256:384], self.pb[b][:, 256:384],
                ch[o][:, 256:384], Op.add),
              [("pb", bc), rch(par)], [rch(nxt)], f"m{m}cm{st}{h}{lvl}")
            self.free_bank(bc)
            par = nxt
            yield
        Tt = ch[par][:, 256:384]
        for i in range(2):
            V(lambda p, i=i: p.tensor_copy(
                vbk[:, i * 128:(i + 1) * 128].bitcast(F32R), kTs[i]),
              [("kT", 2 * h + i)], [rvb], f"m{m}ku{st}{h}{i}")
        yield
        bu = self.nbank()
        for i in range(2):
            self.MMr(bu, slice(0, 256), vbk[:, i * 128:(i + 1) * 128],
                     Ssl[:, i * 256:(i + 1) * 256], i == 0, i == 1,
                     [rvb, rS], f"m{m}kS{st}{h}{i}")
        bo = self.nbank()
        for i in range(2):
            self.MMb(bo, slice(0, 256), qTs[i], S_bf[:, i * 256:(i + 1) * 256],
                     i == 0, False, [("qT", 2 * h + i), rsb], f"m{m}oS{st}{h}{i}")
        yield
        dstvb = branches[:, base + 512: base + 768]
        V(lambda p, b=bu, d=vbk, s=dstv: p.tensor_tensor(
            d.bitcast(F32R), s, self.pb[b][:, 0:256], Op.subtract),
          [("pb", bu), rbr], [rvb], f"m{m}vk0{st}{h}")
        V(lambda p, d=dstvb, s=vbk: p.tensor_scalar(
            d, s, bcol, None, Op.mult),
          [rvb, ("beta_tm",)], [rbr], f"m{m}vk{st}{h}")
        yield
        self.MMb(bu, slice(256, 512), Tt, dstvb, True, True,
                 [rch(par), rbr], f"m{m}u{st}{h}")
        yield
        V(lambda p, b=bu, d=ui: p.tensor_copy(d.bitcast(F32R),
                                              self.pb[b][:, 256:512]),
          [("pb", bu)], [ru], f"m{m}ue{st}{h}")
        self.free_bank(bu)
        yield
        self.MMr(bo, slice(0, 256), attnT, ui, False, True,
                 [rat, ru], f"m{m}oA{st}{h}")
        yield
        dsto = branches[:, base + 512: base + 768]
        V(lambda p, d=dsto, b=bo: p.tensor_scalar(
            d, self.pb[b][:, 0:256], sqcol, None, Op.mult),
          [("pb", bo), ("sq_tm",)], [rbr], f"m{m}oe{st}{h}")
        self.free_bank(bo)
        yield
        bs = self.nbank()
        for i in range(2):
            self.MMr(bs, slice(i * 256, (i + 1) * 256), self.ident_r[:, :],
                     Ssl[:, i * 256:(i + 1) * 256], True, False,
                     [("c_identr",), rS], f"m{m}Sr{st}{h}{i}")
            self.MMr(bs, slice(i * 256, (i + 1) * 256),
                     ktm[:, i * 128:(i + 1) * 128], ui,
                     False, True, [rk, ru], f"m{m}Su{st}{h}{i}")
        yield
        V(lambda p, b=bs, S=Ssl: p.tensor_copy(S.bitcast(F32R),
                                               self.pb[b][:, 0:512]),
          [("pb", bs)], [rS], f"m{m}Sc{st}{h}")
        self.free_bank(bs)
        yield
        br = branches[:, base: base + 1024]
        s8, s16 = sl * 8, sl * 16
        stv = self.stats_tm
        rst, rcs, rtm = ("stat", sl), ("colsc", sl), ("stats_tm", sl)
        V(lambda p, br=br: p.tensor_reduce(
            self.stat_sc[:, s8:s8 + 4], br.rearrange("p (b n) -> p b n", b=4),
            AXX, Op.add),
          [rbr], [rst], f"m{m}ts{st}{h}")
        V(lambda p, br=br: p.tensor_reduce(
            self.stat_sc[:, s8 + 4:s8 + 8], br.rearrange("p (b n) -> p b n", b=4),
            AXX, Op.add, apply_absolute_value=True),
          [rbr], [rst], f"m{m}ta{st}{h}")
        for b4 in range(4):
            S_(lambda p, br=br, b4=b4: p.activation(
                self.sqsc[:, 0:256].bitcast(F32R), br[:, b4 * 256:(b4 + 1) * 256],
                AF.Square, accum_out=self.colsc[:, s8 + b4:s8 + b4 + 1]),
               [rbr], [("sqsc", 0), rcs], f"m{m}tq{st}{h}{b4}")
        yield
        stv16 = stv[:, s16:s16 + 16].rearrange("p (b j) -> p b j", j=4)
        jm, jv = stv16[:, :, 0:1], stv16[:, :, 1:2]
        ja, jl = stv16[:, :, 2:3], stv16[:, :, 3:4]
        sum4 = self.stat_sc[:, s8:s8 + 4].unsqueeze(2)
        abs4 = self.stat_sc[:, s8 + 4:s8 + 8].unsqueeze(2)
        sq4 = self.colsc[:, s8:s8 + 4].unsqueeze(2)
        m2t = self.colsc[:, s8 + 4:s8 + 8].unsqueeze(2)
        V(lambda p: p.tensor_scalar(jm, sum4, 1.0 / 256.0, None, Op.mult),
          [rst, ("c_c",)], [rtm], f"m{m}tm{st}{h}")
        V(lambda p: p.tensor_tensor(m2t, jm, jm, Op.mult),
          [rtm], [rcs], f"m{m}t2{st}{h}")
        V(lambda p: p.scalar_tensor_tensor(
            jv, sq4, 1.0 / 256.0, m2t, Op.mult, Op.subtract),
          [rcs, ("c_c",)], [rtm], f"m{m}tv{st}{h}")
        V(lambda p: p.tensor_scalar(ja, abs4, 1.0 / 256.0, None, Op.mult),
          [rst, ("c_c",)], [rtm], f"m{m}tb{st}{h}")
        S_(lambda p: p.activation(jl, sq4, AF.Sqrt),
           [rcs], [rtm], f"m{m}tl{st}{h}")
        yield
        bt = self.nbank()
        self.TRf(bt, slice(0, 128), stv[:, s16:s16 + 16], [rtm],
                 f"m{m}tt{st}{h}", rows=16)
        yield
        V(lambda p, b=bt, h=h, sub=sub: p.tensor_copy(
            self.statsT[0:16, sub * 512 + h * 128: sub * 512 + (h + 1) * 128],
            self.pb[b][0:16, 0:128]),
          [("pb", bt)], [("statsT", sub, h)], f"m{m}te{st}{h}")
        self.free_bank(bt)
        yield "free"

    def _sub_epi(self, m, st):
        V, S_ = self.V, self.S_
        P0 = st * 128
        sub = st % 2
        branches = self.branches[sub]
        rstT = [("statsT", sub, h) for h in range(H)]
        blg = self.nbank()
        for fc in range(NCH):
            hb = self.nbank()
            self.s.op("T", lambda p, fc=fc, b=hb: p.matmul(
                self.pb[b][:, 0:512],
                self.w1b[0:16, fc * 128:(fc + 1) * 128],
                self.statsT[0:16, sub * 512: sub * 512 + 512],
                start=True, stop=True),
                reads=[("c_w1b",)] + rstT, writes=[("pb", hb)],
                tag=f"m{m}gB{st}{fc}")
            yield
            hgs = self.hg[:, (fc % 2) * 512:(fc % 2 + 1) * 512]
            t1b = self.act_bf[:, fc * ML + P0: fc * ML + P0 + 128]
            t13 = t1b.unsqueeze(1).to_broadcast([128, 4, 128])
            V(lambda p, o=hgs, b=hb, t=t13: p.tensor_tensor(
                o.rearrange("p (h n) -> p h n", h=4),
                self.pb[b][:, 0:512].rearrange("p (h n) -> p h n", h=4),
                t, Op.add),
              [("pb", hb), ("act", fc)], [("hg", fc % 2)], f"m{m}gd{st}{fc}")
            self.free_bank(hb)
            S_(lambda p, o=hgs, fc=fc: p.activation(
                o, o, AF.Gelu, bias=self.b1g[:, fc:fc + 1]),
               [("hg", fc % 2), ("c_b1g",)], [("hg", fc % 2)], f"m{m}gg{st}{fc}")
            yield
            self.s.op("T", lambda p, fc=fc, o=hgs, b=blg: p.matmul(
                self.pb[b][0:4, 0:512],
                self.w2g_bf[:, fc * 4:(fc + 1) * 4],
                o, start=(fc == 0), stop=(fc == NCH - 1)),
                reads=[("c_w2gb",), ("hg", fc % 2)],
                writes=[("pb", blg)], tag=f"m{m}gl{st}{fc}")
            yield
        lgs = self.beTa[0:4, 0:512]
        S_(lambda p, b=blg, d=lgs: p.activation(
            d, self.pb[b][0:4, 0:512], AF.Identity, bias=self.b2g[0:4, 0:1]),
           [("pb", blg), ("c_b2g",)], [("beTa",)], f"m{m}lb{st}")
        self.free_bank(blg)
        yield
        bt2 = self.nbank()
        for h in range(H):
            self.TRf(bt2, slice(h * 4, (h + 1) * 4),
                     lgs[0:4, h * 128:(h + 1) * 128],
                     [("beTa",)], f"m{m}lt{st}{h}", pk=4)
        yield
        c = self.scz
        rcz = ("sczS",)
        V(lambda p, b=bt2: p.tensor_copy(c[:, 0:16], self.pb[b][:, 0:16]),
          [("pb", bt2)], [rcz], f"m{m}le{st}")
        self.free_bank(bt2)
        z3 = c[:, 0:16].rearrange("p (h b) -> p h b", h=4)
        V(lambda p: p.tensor_reduce(c[:, 16:20], z3, AXX, Op.max, negate=True),
          [rcz], [rcz], f"m{m}l1{st}")
        nm3 = c[:, 16:20].unsqueeze(2).to_broadcast([128, 4, 4])
        V(lambda p: p.tensor_tensor(
            c[:, 20:36].rearrange("p (h b) -> p h b", h=4), z3, nm3, Op.add),
          [rcz], [rcz], f"m{m}l2{st}")
        V(lambda p: p.tensor_scalar(c[:, 20:36], c[:, 20:36],
                                    self.invt[:, 0:1], None, Op.mult),
          [rcz, ("c_invt",)], [rcz], f"m{m}l2b{st}")
        yield
        S_(lambda p: p.activation(c[:, 20:36], c[:, 20:36], AF.Exp),
           [rcz], [rcz], f"m{m}l3{st}")
        e3 = c[:, 20:36].rearrange("p (h b) -> p h b", h=4)
        V(lambda p: p.tensor_reduce(c[:, 36:40], e3, AXX, Op.add),
          [rcz], [rcz], f"m{m}l4{st}")
        V(lambda p: p.reciprocal(c[:, 36:40], c[:, 36:40]),
          [rcz], [rcz], f"m{m}l5{st}")
        rc3 = c[:, 36:40].unsqueeze(2).to_broadcast([128, 4, 4])
        V(lambda p: p.scalar_tensor_tensor(
            c[:, 20:36].rearrange("p (h b) -> p h b", h=4), e3,
            1.0 - 2.0 * FLOOR_EPS, rc3, Op.mult, Op.mult),
          [rcz, ("c_c",)], [rcz], f"m{m}l6{st}")
        fl3 = self.floor_row[:, 0:4].unsqueeze(1).to_broadcast([128, 4, 4])
        V(lambda p: p.tensor_tensor(
            self.probs[:, 0:16].rearrange("p (h b) -> p h b", h=4), e3, fl3,
            Op.add),
          [rcz, ("c_floor",)], [("probs", h2) for h2 in range(H)],
          f"m{m}l7{st}")
        yield
        c3 = c[:, 96:128].rearrange("p (h x) -> p h x", h=4)
        V(lambda p: p.tensor_tensor(
            c3[:, :, 0:1], self.probs[:, 0:16].rearrange(
                "p (h b) -> p h b", h=4)[:, :, 0:1],
            self.sigc[:, 0:4].unsqueeze(2), Op.add),
          [("probs", h2) for h2 in range(H)] + [("c_sigc",)], [rcz],
          f"m{m}mc{st}")
        for h in range(H):
            base = h * 1024
            br = branches[:, base: base + 1024]
            pr = self.probs[:, h * 4:h * 4 + 4]
            cb = 96 + h * 8
            don = self.o_n[:, h * 256:(h + 1) * 256]
            V(lambda p, br=br, cb=cb, don=don: p.tensor_scalar(
                don, br[:, 0:256], c[:, cb:cb + 1], None, Op.mult),
              [("br", sub, h), rcz], [("o_n", h)], f"m{m}m0{st}{h}")
            for bi in range(1, 4):
                V(lambda p, br=br, pr=pr, bi=bi, don=don: p.scalar_tensor_tensor(
                    don, br[:, bi * 256:(bi + 1) * 256],
                    pr[:, bi:bi + 1], don, Op.mult, Op.add),
                  [("br", sub, h), ("probs", h), ("o_n", h)], [("o_n", h)],
                  f"m{m}m{bi}{st}{h}")
            yield
        for h in range(H):
            cb = 96 + h * 8
            don = self.o_n[:, h * 256:(h + 1) * 256]
            S_(lambda p, cb=cb, don=don: p.activation(
                self.sqsc[:, (h % 2) * 256:(h % 2) * 256 + 256].bitcast(F32R),
                don, AF.Square, accum_out=c[:, cb + 1:cb + 2]),
               [("o_n", h)], [("sqsc", h % 2), rcz], f"m{m}mq{st}{h}")
        yield
        S_(lambda p: p.activation(c3[:, :, 2:3], c3[:, :, 1:2],
                                  AF.Sqrt, scale=1.0 / 256.0, bias=1e-5),
           [rcz, ("c_c",)], [rcz], f"m{m}ms{st}")
        V(lambda p: p.reciprocal(c3[:, :, 3:4], c3[:, :, 2:3]),
          [rcz], [rcz], f"m{m}mr{st}")
        yield
        for h in range(H):
            cb = 96 + h * 8
            don = self.o_n[:, h * 256:(h + 1) * 256]
            V(lambda p, cb=cb, don=don: p.scalar_tensor_tensor(
                don, don, c[:, cb + 3:cb + 4],
                self.rms_bc[:, :], Op.mult, Op.mult),
              [rcz, ("c_rmsbc",), ("o_n", h)], [("o_n", h)], f"m{m}mo{st}{h}")
            yield
            bo2 = self.nbank()
            for i, cchunk in enumerate((2 * h, 2 * h + 1)):
                self.TRf(bo2, slice(i * 128, (i + 1) * 128),
                         self.o_n[:, cchunk * 128:(cchunk + 1) * 128],
                         [("o_n", h)], f"m{m}ot{st}{cchunk}")
            yield
            for i, cchunk in enumerate((2 * h, 2 * h + 1)):
                dstT = self.onT[:, cchunk * 256 + (st % 2) * 128:
                                cchunk * 256 + (st % 2 + 1) * 128]
                S_(lambda p, d=dstT, b=bo2, i=i: p.copy(
                    d, self.pb[b][:, i * 128:(i + 1) * 128]),
                   [("pb", bo2)], [("onT", cchunk)], f"m{m}oe{st}{cchunk}")
            self.free_bank(bo2)
            yield

    def _ph6_gen(self, sq, m, pair):
        V, S_ = self.V, self.S_
        t0 = m * ML
        P0 = pair * 256
        NT = 256
        for c in range(NCH):
            self.ld(self.xT_pair[:, c * 256:(c + 1) * 256],
                    self.x_in[sq, c, :, t0 + P0: t0 + P0 + 256],
                    [("xTp", c)], f"m{m}rx{pair}{c}")
        yield
        for dc in range(2):
            piece = self.r_wo[:, (dc % 2) * 1024:((dc % 2) + 1) * 1024]
            self.ld(piece, self.scr["wo_b"][dc], [("rwo", dc % 2)],
                    f"m{m}o{pair}w{dc}", reads=[("sc_wo",)])
        yield "hold"
        bsum = self.nbank()
        bsq = self.nbank()
        for dc in range(NCH):
            par = dc % 2
            piece = self.r_wo[:, par * 1024:(par + 1) * 1024]
            if dc >= 2:
                self.ld(piece, self.scr["wo_b"][dc], [("rwo", par)],
                        f"m{m}o{pair}w{dc}", reads=[("sc_wo",)])
            bo = self.nbank()
            for kc in range(NCH):
                self.MMb(bo, slice(0, NT), piece[:, kc * 128:(kc + 1) * 128],
                         self.onT[:, kc * 256: kc * 256 + NT],
                         kc == 0, kc == NCH - 1,
                         [("rwo", par), ("onT", kc)], f"m{m}wo{pair}{dc}{kc}")
            x1 = self.acc[:, dc * ML + P0: dc * ML + P0 + NT]
            V(lambda p, x1=x1, b=bo, dc=dc: p.tensor_tensor(
                x1, self.pb[b][:, 0:NT], self.xT_pair[:, dc * 256:(dc + 1) * 256],
                Op.add),
              [("pb", bo), ("xTp", dc)], [("acc", dc)], f"m{m}x1{pair}{dc}")
            self.free_bank(bo)
            S_(lambda p, x1=x1: p.copy(self.lnsc[:, 0:NT].bitcast(F32R), x1),
               [("acc", dc)], [("lnsc",)], f"m{m}sc{pair}{dc}")
            self.MMr(bsum, slice(0, NT), self.onesc_r[:, 0:1], self.lnsc[:, 0:NT],
                     dc == 0, dc == NCH - 1, [("c_onescr",), ("lnsc",)],
                     f"m{m}s1{pair}{dc}", mrows=1)
            S_(lambda p, x1=x1: p.activation(self.sqsc[:, 0:NT].bitcast(F32R),
                                             x1, AF.Square),
               [("acc", dc)], [("sqsc", 0), ("sqsc", 1)], f"m{m}sq{pair}{dc}")
            self.MMr(bsq, slice(0, NT), self.onesc_r[:, 0:1], self.sqsc[:, 0:NT],
                     dc == 0, dc == NCH - 1, [("c_onescr",), ("sqsc", 0), ("sqsc", 1)],
                     f"m{m}s2{pair}{dc}", mrows=1)
        self._ln_rows(m, pair, bsum, bsq, NT, ffn_r=True)
        self.free_bank(bsum)
        self.free_bank(bsq)
        bs1 = self.nbank()
        bs2 = self.nbank()
        bs3 = self.nbank()
        r = self.row1
        self.bc_row(bs1, slice(0, NT), r[0:1, 768:768 + NT], [("row1",)],
                    f"m{m}bs{pair}")
        self.bc_row(bs2, slice(0, NT), r[0:1, 1024:1024 + NT], [("row1",)],
                    f"m{m}bm{pair}")
        self.bc_row(bs3, slice(0, NT), r[0:1, 256:256 + NT], [("row1",)],
                    f"m{m}br{pair}")
        bg = self.nbank()
        for dc in range(NCH):
            hpar = dc % 2
            x1 = self.acc[:, dc * ML + P0: dc * ML + P0 + NT]
            V(lambda p, x1=x1: p.tensor_tensor(
                self.scx[:, 0:NT], x1, self.pb[bs1][:, 0:NT], Op.mult),
              [("pb", bs1), ("acc", dc)], [("scx", 0)], f"m{m}n1{pair}{dc}")
            V(lambda p, x1=x1: p.tensor_tensor(
                x1, self.scx[:, 0:NT], self.pb[bs2][:, 0:NT], Op.subtract),
              [("pb", bs2), ("scx", 0)], [("acc", dc)], f"m{m}n2{pair}{dc}")
            hsl = self.hrot[:, hpar * 256:(hpar + 1) * 256]
            V(lambda p, x1=x1, o=hsl, dc=dc: p.scalar_tensor_tensor(
                o.bitcast(F32R), x1, self.normw[:, dc:dc + 1],
                self.pb[bs3][:, 0:NT], Op.mult, Op.mult),
              [("pb", bs3), ("acc", dc), ("c_nw",)], [("hrot", hpar)],
              f"m{m}h{pair}{dc}")
            hb = self.h_bf[:, dc * ML + P0: dc * ML + P0 + NT]
            S_(lambda p, o=hb, i=hsl: p.copy(o, i),
               [("hrot", hpar)], [("h_bf", dc)], f"m{m}hb{pair}{dc}")
            self.MMr(bg, slice(0, NT), self.gatewt_r[:, dc * 8:(dc + 1) * 8], hsl,
                     dc == 0, dc == NCH - 1, [("c_gwr",), ("hrot", hpar)],
                     f"m{m}fg{pair}{dc}", mrows=8)
        self.free_bank(bs1)
        self.free_bank(bs2)
        self.free_bank(bs3)
        S_(lambda p, b=bg: p.activation(self.fglb[0:8, 0:NT], self.pb[b][0:8, 0:NT],
                                        AF.Identity, bias=self.gateb[0:8, 0:1]),
           [("pb", bg), ("c_gb",)], [("fglb",)], f"m{m}fb{pair}")
        self.free_bank(bg)
        st0 = pair * 2
        bt = self.nbank()
        for sub in range(2):
            self.TRf(bt, slice(sub * 8, sub * 8 + 8),
                     self.fglb[0:8, sub * 128:(sub + 1) * 128],
                     [("fglb",)], f"m{m}ft{pair}{sub}", pk=8)
        gl2 = self.gl_sb[:, st0 * 8:(st0 + 2) * 8]
        rgl = [("gl", st0), ("gl", st0 + 1)]
        V(lambda p, b=bt: p.tensor_copy(gl2, self.pb[b][:, 0:16]),
          [("pb", bt)], rgl, f"m{m}fe{pair}")
        self.free_bank(bt)
        c = self.scz
        gl3 = gl2.rearrange("p (s e) -> p s e", s=2)
        mk13 = c[:, 40:56].rearrange("p (s e) -> p s e", s=2)
        l23 = c[:, 56:72].rearrange("p (s e) -> p s e", s=2)
        m1b = c[:, 72:74].unsqueeze(2).to_broadcast([128, 2, 8])
        m2b = c[:, 74:76].unsqueeze(2).to_broadcast([128, 2, 8])
        rcb = c[:, 82:84].unsqueeze(2).to_broadcast([128, 2, 8])
        wb_ = c[:, 84:86].unsqueeze(2).to_broadcast([128, 2, 8])
        V(lambda p: p.tensor_reduce(c[:, 72:74], gl3, AXX, Op.max),
          rgl, [("scz6",)], f"m{m}u1{pair}")
        V(lambda p: p.tensor_tensor(mk13, gl3, m1b, Op.is_equal),
          rgl + [("scz6",)], [("scz6",)], f"m{m}u2{pair}")
        V(lambda p: p.scalar_tensor_tensor(l23, mk13, -1e30, gl3,
                                           Op.mult, Op.add),
          rgl + [("scz6",), ("c_c",)], [("scz6",)], f"m{m}u3{pair}")
        V(lambda p: p.tensor_reduce(c[:, 74:76], l23, AXX, Op.max),
          [("scz6",)], [("scz6",)], f"m{m}u4{pair}")
        V(lambda p: p.tensor_tensor(l23, l23, m2b, Op.is_equal),
          [("scz6",)], [("scz6",)], f"m{m}u5{pair}")
        V(lambda p: p.tensor_tensor(c[:, 76:78], c[:, 74:76], c[:, 72:74],
                                    Op.subtract),
          [("scz6",)], [("scz6",)], f"m{m}u6{pair}")
        S_(lambda p: p.activation(c[:, 78:80], c[:, 76:78], AF.Exp),
           [("scz6",)], [("scz6",)], f"m{m}u7{pair}")
        V(lambda p: p.tensor_scalar(c[:, 80:82], c[:, 78:80], 1.0, None, Op.add),
          [("scz6",), ("c_c",)], [("scz6",)], f"m{m}u8{pair}")
        V(lambda p: p.reciprocal(c[:, 82:84], c[:, 80:82]),
          [("scz6",)], [("scz6",)], f"m{m}u9{pair}")
        V(lambda p: p.tensor_tensor(c[:, 84:86], c[:, 78:80], c[:, 82:84],
                                    Op.mult),
          [("scz6",)], [("scz6",)], f"m{m}ua{pair}")
        ps3 = self.p8[:, st0 * 8:(st0 + 2) * 8].rearrange("p (s e) -> p s e",
                                                          s=2)
        rp8 = [("p8", st0), ("p8", st0 + 1)]
        V(lambda p: p.tensor_tensor(ps3, mk13, rcb, Op.mult),
          [("scz6",)], rp8, f"m{m}ub{pair}")
        V(lambda p: p.tensor_tensor(mk13, l23, wb_, Op.mult),
          [("scz6",)], [("scz6",)], f"m{m}ut{pair}")
        V(lambda p: p.tensor_tensor(ps3, ps3, mk13, Op.add),
          [("scz6",)] + rp8, rp8, f"m{m}uc{pair}")
        bt2 = self.nbank()
        for sub in range(2):
            st = st0 + sub
            self.TRf(bt2, slice(sub * 128, (sub + 1) * 128),
                     self.p8[:, st * 8:(st + 1) * 8],
                     [("p8", st)], f"m{m}ud{st}", rows=8)
        for sub in range(2):
            st = st0 + sub
            V(lambda p, b=bt2, st=st, sub=sub: p.tensor_copy(
                self.p8T_bf[0:8, st * 128:(st + 1) * 128],
                self.pb[b][0:8, sub * 128:(sub + 1) * 128]),
              [("pb", bt2)], [("p8Tb",)], f"m{m}ue2{st}")
        self.free_bank(bt2)
        if m == 0 and sq == 0 and pair == 1:
            self.tap("acc1", self.acc[:, :], [128, NCH * ML],
                     [("acc", dc2) for dc2 in range(NCH)], F32)
            self.tap("hbf", self.h_bf[:, :], [128, NCH * ML],
                     [("h_bf", dc2) for dc2 in range(NCH)], BF16)
            self.tap("p8t", self.p8T_bf[0:8, :], [8, ML], [("p8Tb",)], BF16)

    def _ln_rows(self, m, pair, bsum, bsq, NT, ffn_r):
        V, S_ = self.V, self.S_
        r = self.row1
        V(lambda p: p.tensor_scalar(r[0:1, 0:NT], self.pb[bsum][0:1, 0:NT],
                                    1.0 / 1024.0, None, Op.mult),
          [("pb", bsum), ("c_c",)], [("row1",)], f"m{m}ln1{pair}")
        V(lambda p: p.tensor_tensor(r[0:1, 256:256 + NT], r[0:1, 0:NT],
                                    r[0:1, 0:NT], Op.mult),
          [("row1",)], [("row1",)], f"m{m}ln2{pair}")
        V(lambda p: p.scalar_tensor_tensor(
            r[0:1, 512:512 + NT], self.pb[bsq][0:1, 0:NT], 1.0 / 1024.0,
            r[0:1, 256:256 + NT], Op.mult, Op.subtract),
          [("pb", bsq), ("row1",), ("c_c",)], [("row1",)], f"m{m}ln3{pair}")
        S_(lambda p: p.activation(r[0:1, 768:768 + NT], r[0:1, 512:512 + NT],
                                  AF.Sqrt, bias=1e-5),
           [("row1",), ("c_c",)], [("row1",)], f"m{m}ln4{pair}")
        V(lambda p: p.reciprocal(r[0:1, 768:768 + NT], r[0:1, 768:768 + NT]),
          [("row1",)], [("row1",)], f"m{m}ln5{pair}")
        S_(lambda p: p.copy(r[0:1, 768:768 + NT].bitcast(F32R),
                            r[0:1, 768:768 + NT]),
           [("row1",)], [("row1",)], f"m{m}ln5r{pair}")
        V(lambda p: p.tensor_tensor(r[0:1, 1024:1024 + NT].bitcast(F32R),
                                    r[0:1, 0:NT], r[0:1, 768:768 + NT], Op.mult),
          [("row1",)], [("row1",)], f"m{m}ln6{pair}")
        if ffn_r:
            V(lambda p: p.tensor_tensor(r[0:1, 256:256 + NT],
                                        r[0:1, 768:768 + NT],
                                        r[0:1, 768:768 + NT], Op.mult),
              [("row1",)], [("row1",)], f"m{m}ln7{pair}")
            V(lambda p: p.tensor_tensor(r[0:1, 256:256 + NT],
                                        r[0:1, 256:256 + NT],
                                        r[0:1, 512:512 + NT], Op.mult),
              [("row1",)], [("row1",)], f"m{m}ln8{pair}")
            S_(lambda p: p.activation(r[0:1, 256:256 + NT],
                                      r[0:1, 256:256 + NT], AF.Sqrt, bias=F32EPS),
               [("row1",), ("c_c",)], [("row1",)], f"m{m}ln9{pair}")
            V(lambda p: p.reciprocal(r[0:1, 256:256 + NT], r[0:1, 256:256 + NT]),
              [("row1",)], [("row1",)], f"m{m}lnA{pair}")
            V(lambda p: p.tensor_scalar(r[0:1, 768:768 + NT].bitcast(F32R),
                                        r[0:1, 768:768 + NT], 2.0, None, Op.mult),
              [("row1",), ("c_c",)], [("row1",)], f"m{m}lnB{pair}")
            V(lambda p: p.tensor_scalar(r[0:1, 1024:1024 + NT].bitcast(F32R),
                                        r[0:1, 1024:1024 + NT], 2.0, None,
                                        Op.mult),
              [("row1",), ("c_c",)], [("row1",)], f"m{m}lnC{pair}")
            V(lambda p: p.tensor_scalar(r[0:1, 256:256 + NT].bitcast(F32R),
                                        r[0:1, 256:256 + NT], 0.5, None, Op.mult),
              [("row1",), ("c_c",)], [("row1",)], f"m{m}lnD{pair}")

    def _ph7(self, sq, m):
        V, S_ = self.V, self.S_
        for e in range(NEXP):
            if True:
                bpe = self.nbank()
                self.MMb(bpe, slice(0, ML), self.sel8[0:8, e * 128:(e + 1) * 128],
                         self.p8T_bf[0:8, :], True, True,
                         [("c_sel8",), ("p8Tb",)], f"m{m}e{e}b")
                V(lambda p, b=bpe: p.tensor_scalar(
                    self.p_be[:, :], self.pb[b][:, 0:ML], 1.0 / ALPHA, None,
                    Op.mult),
                  [("pb", bpe), ("c_c",)], [("p_be",)], f"m{m}e{e}p")
                self.free_bank(bpe)
                for fc in range(NCH):
                    s1 = (e * NCH * 2 + 2 * fc) % 3
                    pg = self.r_f1[:, s1 * 1024:(s1 + 1) * 1024]
                    self.ld(pg, self.scr["fw1_b"][e, fc], [("rf1", s1)],
                            f"m{m}e{e}wg{fc}", reads=[("sc_f1",)])
                    bG = self.nbank()
                    for kc in range(NCH):
                        self.MMb(bG, slice(0, ML), pg[:, kc * 128:(kc + 1) * 128],
                                 self.h_bf[:, kc * ML:(kc + 1) * ML],
                                 kc == 0, kc == NCH - 1,
                                 [("rf1", s1), ("h_bf", kc)], f"m{m}e{e}G{fc}{kc}")
                    s1b = (e * NCH * 2 + 2 * fc + 1) % 3
                    pl = self.r_f1[:, s1b * 1024:(s1b + 1) * 1024]
                    self.ld(pl, self.scr["fw1_b"][e, NCH + fc], [("rf1", s1b)],
                            f"m{m}e{e}wl{fc}", reads=[("sc_f1",)])
                    bL = self.nbank()
                    for kc in range(NCH):
                        self.MMb(bL, slice(0, ML), pl[:, kc * 128:(kc + 1) * 128],
                                 self.h_bf[:, kc * ML:(kc + 1) * ML],
                                 kc == 0, kc == NCH - 1,
                                 [("rf1", s1b), ("h_bf", kc)], f"m{m}e{e}L{fc}{kc}")
                    S_(lambda p, b=bG, e=e, fc=fc: p.activation(
                        self.scx[:, :], self.pb[b][:, 0:ML], AF.Silu, scale=ALPHA,
                        bias=self.fb1s[:, e * 8 + fc:e * 8 + fc + 1]),
                       [("pb", bG), ("c_fb1s",)], [("scx", 0), ("scx", 1)],
                       f"m{m}e{e}s1{fc}")
                    self.free_bank(bG)
                    S_(lambda p, b=bL, e=e, fc=fc: p.activation(
                        self.scy[:, :], self.pb[b][:, 0:ML], AF.Identity,
                        bias=self.fb1p[:, e * 8 + fc:e * 8 + fc + 1]),
                       [("pb", bL), ("c_fb1p",)], [("scy",)], f"m{m}e{e}s2{fc}")
                    self.free_bank(bL)
                    V(lambda p: p.tensor_scalar(self.scy[:, :], self.scy[:, :],
                                                8.0, -6.0, Op.min, Op.max),
                      [("scy",), ("c_c",)], [("scy",)], f"m{m}e{e}s3{fc}")
                    V(lambda p: p.tensor_tensor(
                        self.scy[:, :], self.scy[:, :], self.p_be[:, :], Op.mult),
                      [("scy",), ("p_be",)], [("scy",)], f"m{m}e{e}s4{fc}")
                    act = self.act_bf[:, fc * ML:(fc + 1) * ML]
                    V(lambda p, a=act: p.scalar_tensor_tensor(
                        a, self.scx[:, :], SWISH_CAP, self.scy[:, :],
                        Op.min, Op.mult),
                      [("scx", 0), ("scx", 1), ("scy",), ("c_c",)], [("act", fc)],
                      f"m{m}e{e}s5{fc}")
                for g in range(2):
                    m2banks = [self.nbank() for _ in range(4)]
                    for dj in range(4):
                        dd = g * 4 + dj
                        s2 = (e * 4 + dj) % 2
                        pw = self.r_fw2[:, s2 * 1024:(s2 + 1) * 1024]
                        self.ld(pw, self.scr["fw2_b"][e, dd], [("rfw2", s2)],
                                f"m{m}e{e}w2{dd}", reads=[("sc_f2",)])
                        bank = m2banks[dj]
                        cols = slice(0, 512)
                        if e == 0:
                            self.MMb(bank, cols,
                                     self.fb2_bf[0:8, dd * 128:(dd + 1) * 128],
                                     self.p8T_bf[0:8, 0:512],
                                     True, False,
                                     [("c_fb2b",), ("p8Tb",)],
                                     f"m{m}b2_{dd}")
                        for ic in range(NCH):
                            first = (e != 0 and ic == 0)
                            last = (ic == NCH - 1)
                            self.MMb(bank, cols,
                                     pw[:, ic * 128:(ic + 1) * 128],
                                     self.act_bf[:, ic * ML: ic * ML + 512],
                                     first, last,
                                     [("rfw2", s2), ("act", ic)],
                                     f"m{m}e{e}m2_{dd}_{ic}")
                    for dj in range(4):
                        dd = g * 4 + dj
                        bank = m2banks[dj]
                        x2 = self.acc[:, dd * ML: dd * ML + 512]
                        V(lambda p, x2=x2, b=bank: p.tensor_tensor(
                            x2, x2, self.pb[b][:, 0:512], Op.add),
                          [("pb", bank), ("acc", dd)], [("acc", dd)],
                          f"m{m}fa{e}_{dd}")
                        self.free_bank(bank)

    def _ph8(self, sq, m):
        V, S_ = self.V, self.S_
        t0 = m * ML
        if m == 0 and sq == 0:
            self.tap("acc2", self.acc[:, :], [128, NCH * ML],
                     [("acc", dc2) for dc2 in range(NCH)], F32)
        pair_banks = []
        for pair in range(2):
            P0 = pair * 256
            NT = 256
            bsum = self.nbank()
            bsq = self.nbank()
            for dc in range(NCH):
                x2 = self.acc[:, dc * ML + P0: dc * ML + P0 + NT]
                S_(lambda p, x2=x2: p.copy(self.lnsc[:, 0:NT].bitcast(F32R), x2),
                   [("acc", dc)], [("lnsc",)], f"m{m}zc{pair}{dc}")
                self.MMr(bsum, slice(0, NT), self.onesc_r[:, 0:1],
                         self.lnsc[:, 0:NT], dc == 0, dc == NCH - 1,
                         [("c_onescr",), ("lnsc",)], f"m{m}z1{pair}{dc}", mrows=1)
                S_(lambda p, x2=x2: p.activation(self.sqsc[:, 0:NT].bitcast(F32R),
                                                 x2, AF.Square),
                   [("acc", dc)], [("sqsc", 0), ("sqsc", 1)], f"m{m}zq{pair}{dc}")
                self.MMr(bsq, slice(0, NT), self.onesc_r[:, 0:1], self.sqsc[:, 0:NT],
                         dc == 0, dc == NCH - 1, [("c_onescr",), ("sqsc", 0), ("sqsc", 1)],
                         f"m{m}z2{pair}{dc}", mrows=1)
            pair_banks.append((bsum, bsq))
        for pair in range(2):
            P0 = pair * 256
            NT = 256
            bsum, bsq = pair_banks[pair]
            self._ln_rows(m, 10 + pair, bsum, bsq, NT, ffn_r=False)
            self.free_bank(bsum)
            self.free_bank(bsq)
            bs1 = self.nbank()
            bs2 = self.nbank()
            r = self.row1
            self.bc_row(bs1, slice(0, NT), r[0:1, 768:768 + NT], [("row1",)],
                        f"m{m}zb{pair}")
            self.bc_row(bs2, slice(0, NT), r[0:1, 1024:1024 + NT], [("row1",)],
                        f"m{m}zm{pair}")
            for sub in range(2):
                st = pair * 2 + sub
                for half in range(2):
                    bank = self.nbank()
                    accs = self.acc[:, half * 4 * ML:(half + 1) * 4 * ML]
                    x23 = accs.rearrange("p (d n) -> p d n", n=ML)[
                        :, :, st * 128:(st + 1) * 128]
                    racc = [("acc", half * 4 + c4) for c4 in range(4)]
                    bb1 = self.pb[bs1][:, sub * 128:(sub + 1) * 128]
                    bb2 = self.pb[bs2][:, sub * 128:(sub + 1) * 128]
                    V(lambda p, x23=x23, bb1=bb1: p.tensor_tensor(
                        self.scx[:, 0:512].rearrange("p (d n) -> p d n", d=4),
                        x23, bb1.unsqueeze(1).to_broadcast([128, 4, 128]),
                        Op.mult),
                      [("pb", bs1)] + racc, [("scx", 0), ("scx", 1)],
                      f"m{m}za{st}{half}")
                    V(lambda p, bb2=bb2: p.tensor_tensor(
                        self.scy[:, 0:512].rearrange("p (d n) -> p d n", d=4),
                        self.scx[:, 0:512].rearrange("p (d n) -> p d n", d=4),
                        bb2.unsqueeze(1).to_broadcast([128, 4, 128]),
                        Op.subtract),
                      [("pb", bs2), ("scx", 0), ("scx", 1)], [("scy",)],
                      f"m{m}zs{st}{half}")
                    for c4 in range(4):
                        self.TRf(bank, slice(c4 * 128, (c4 + 1) * 128),
                                 self.scy[:, c4 * 128:(c4 + 1) * 128],
                                 [("scy",)], f"m{m}zt{st}{half}{c4}")
                    if half == 0:
                        stg, rstg = self.scy, [("scy",)]
                    else:
                        stg, rstg = self.scx, [("scx", 0), ("scx", 1)]
                    V(lambda p, b=bank, stg=stg: p.tensor_copy(
                        stg[:, 0:512], self.pb[b][:, 0:512]),
                      [("pb", bank)], rstg, f"m{m}zo{st}{half}")
                    self.free_bank(bank)
                    self.st_out(self.y_out[sq, t0 + st * 128: t0 + (st + 1) * 128,
                                           half * 512:(half + 1) * 512],
                                stg[:, 0:512], rstg, f"m{m}so{st}{half}")
            self.free_bank(bs1)
            self.free_bank(bs2)

def build_programs(nseq, seqlen, taps=()):
    mk = MK(nseq, seqlen, taps)
    nc = mk.build()
    return mk, nc


def kernel(**inputs):
    x = np.ascontiguousarray(inputs["L0_attn_hidden_states"], dtype=np.float32)
    B, S, _ = x.shape
    n_cores = 8
    per = B // n_cores
    host = prepare_host({k: np.asarray(v) for k, v in inputs.items()})
    mk, nc = build_programs(per, S)
    in_maps = []
    for c in range(n_cores):
        xc = x[c * per:(c + 1) * per].reshape(per, S, NCH, 128)
        im = {"x": np.ascontiguousarray(xc.transpose(0, 2, 3, 1))}
        im.update({k + "_d": v for k, v in host.items()})
        in_maps.append(im)
    res = run_bass_kernel_spmd(nc, in_maps, core_ids=list(range(n_cores)))
    out = np.concatenate([res.results[c]["y"] for c in range(n_cores)], axis=0)
    return out.astype(np.float32)
